# version 01
# baseline (speedup 1.0000x reference)
import numpy as np
import ml_dtypes
import concourse.bass as bass
import concourse.mybir as mybir
from concourse.bass_utils import run_bass_kernel_spmd

F32 = mybir.dt.float32
BF16 = mybir.dt.bfloat16
AF = mybir.ActivationFunctionType
ALU = mybir.AluOpType
AX = mybir.AxisListType
BF = ml_dtypes.bfloat16

T, E, NH, D, L = 4096, 1024, 16, 64, 512
NE, DFF = 8, 4096
NT = T // 128
NG = T // 512
EPS = 1e-5
NCORE = 8

DBG_STAGE = 9
DBG = 0


def _host_consts():
    inv = 1.0 / (10000.0 ** (np.arange(0, D, 2, dtype=np.float32) / D))
    ang = np.outer(np.arange(T, dtype=np.float32), inv)
    cos = np.cos(ang).astype(np.float32)
    sin = np.sin(ang).astype(np.float32)
    C = np.zeros((128, T), np.float32)
    S = np.zeros((128, T), np.float32)
    for r in range(128):
        i = r % 64
        pp = i // 2
        C[r] = cos[:, pp]
        S[r] = sin[:, pp] * (1.0 if i % 2 == 1 else -1.0)
    swp = np.zeros((128, 128), BF)
    for r in range(128):
        swp[r ^ 1, r] = 1.0
    ident = np.eye(128, dtype=BF)
    return C, S, swp, ident


class _FakeInstr:
    def then_inc(self, sem, k):
        return self


class TokDict(dict):

    def __init__(self):
        super().__init__()
        self.missed = set()

    def __missing__(self, k):
        self.missed.add(k)
        return 0


class _CountEng:
    def wait_ge(self, sem, val):
        pass

    def __getattr__(self, name):
        def f(*a, **kw):
            return _FakeInstr()
        return f


class Prog:

    ORDER = ("sync", "gpsimd", "tensor", "scalar", "vector")

    def __init__(self, nc):
        self.nc = nc
        self.c = {}
        self.snap = {}

    def inc(self, instr, sem, k):
        instr.then_inc(sem, k)
        self.c[sem] = self.c.get(sem, 0) + k
        return (sem, self.c[sem])

    def dinc(self, eng, sem, k):
        d = eng.drain()
        d.then_inc(sem, k)
        self.c[sem] = self.c.get(sem, 0) + k
        return (sem, self.c[sem])

    def tot(self, sem):
        return self.c.get(sem, 0)

    def wt(self, eng, tok):
        if tok:
            sem, val = tok
            if val:
                eng.wait_ge(sem, val)

    def wait_all(self, eng):
        for sem, t in self.snap.items():
            if t:
                eng.wait_ge(sem, t)

    def run_phase(self, streams, tk=None):
        self.snap = dict(self.c)
        start = dict(self.c)
        for name in self.ORDER:
            if name in streams:
                streams[name](_CountEng())
        if tk is not None:
            unresolved = tk.missed - set(tk.keys())
            assert not unresolved, f"unresolved tokens: {unresolved}"
        end = dict(self.c)
        self.c = dict(start)
        with self.nc.Block() as blk:
            for name in self.ORDER:
                if name in streams:
                    getattr(blk, name)(streams[name])
        assert self.c == end, "two-pass mismatch"


def build_nc(dbg_stage=9, dbg=0):
    nc = bass.Bass("TRN2", target_bir_lowering=False)
    _uc = [0]

    def un(s):
        _uc[0] += 1
        return f"{s}_{_uc[0]}"

    x_in = nc.dram_tensor("x", [T, E], F32, kind="ExternalInput").ap()
    wq = nc.dram_tensor("wq", [E, E], F32, kind="ExternalInput").ap()
    wd = nc.dram_tensor("wd", [E, L], F32, kind="ExternalInput").ap()
    wk = nc.dram_tensor("wk", [L, E], F32, kind="ExternalInput").ap()
    wv = nc.dram_tensor("wv", [L, E], F32, kind="ExternalInput").ap()
    wo = nc.dram_tensor("wo", [E, E], F32, kind="ExternalInput").ap()
    w1 = nc.dram_tensor("w1", [NE * DFF, E], F32, kind="ExternalInput").ap()
    w2 = nc.dram_tensor("w2", [NE * E, DFF], F32, kind="ExternalInput").ap()
    wr = nc.dram_tensor("wr", [NE, E], F32, kind="ExternalInput").ap()
    ropec = nc.dram_tensor("ropec", [128, T], F32, kind="ExternalInput").ap()
    ropes = nc.dram_tensor("ropes", [128, T], F32, kind="ExternalInput").ap()
    swp_in = nc.dram_tensor("swp", [128, 128], BF16, kind="ExternalInput").ap()
    id_in = nc.dram_tensor("ident", [128, 128], BF16, kind="ExternalInput").ap()

    out = nc.dram_tensor("out", [T, E], F32, kind="ExternalOutput").ap()
    if dbg:
        q_dbg = nc.dram_tensor("q_dbg", [128, 8 * T], BF16, kind="ExternalOutput").ap()
        k_dbg = nc.dram_tensor("k_dbg", [128, 8 * T], BF16, kind="ExternalOutput").ap()
        x1_dbg = nc.dram_tensor("x1_dbg", [T, E], F32, kind="ExternalOutput").ap()
        g_dbg = nc.dram_tensor("g_dbg", [128, 256], F32, kind="ExternalOutput").ap()
        v_dbg = nc.dram_tensor("v_dbg", [T, 1040], BF16, kind="ExternalOutput").ap()

    w1bf = nc.dram_tensor("w1bf", [NE * DFF, E], BF16).ap()
    w2bf = nc.dram_tensor("w2bf", [NE * E, DFF], BF16).ap()
    x1h = nc.dram_tensor("x1h", [T, E], F32).ap()
    vh = nc.dram_tensor("vh", [T, 1040], BF16).ap()

    from contextlib import ExitStack
    with ExitStack() as _st:
        en = _st.enter_context
        A = en(nc.sbuf_tensor("A", [128, 8 * T], BF16))
        Bq = en(nc.sbuf_tensor("Bq", [128, 8 * T], BF16))
        ones128 = en(nc.sbuf_tensor("ones128", [128, 128], BF16))
        swp_sb = en(nc.sbuf_tensor("swp_sb", [128, 128], BF16))
        id_sb = en(nc.sbuf_tensor("id_sb", [128, 128], BF16))
        stats = en(nc.sbuf_tensor("stats", [128, 192], F32))
        gates = en(nc.sbuf_tensor("gates", [128, 256], F32))
        zero8 = en(nc.sbuf_tensor("zero8", [128, 8], F32))
        epsc = en(nc.sbuf_tensor("epsc", [128, 1], F32))
        eqs = en(nc.sbuf_tensor("eqs", [128, 512], F32))
        wr_t = en(nc.sbuf_tensor("wr_t", [128, 64], BF16))
        ps_a = en(nc.psum_tensor("ps_a", [128, 1024], F32))
        ps_b = en(nc.psum_tensor("ps_b", [128, 1024], F32))
        ps_c = en(nc.psum_tensor("ps_c", [128, 1024], F32))
        ps_sp = en(nc.psum_tensor("ps_sp", [128, 512], F32))
        ps_tr = en(nc.psum_tensor("ps_tr", [128, 1024], BF16))
        s_pe = nc.alloc_semaphore("s_pe")
        s_act = nc.alloc_semaphore("s_act")
        s_dve = nc.alloc_semaphore("s_dve")
        s_gp = nc.alloc_semaphore("s_gp")
        s_once = nc.alloc_semaphore("s_once")
        s_sc = nc.alloc_semaphore("s_sc")
        s_mv = nc.alloc_semaphore("s_mv")
        s_ax = nc.alloc_semaphore("s_ax")
        s_lx2 = [nc.alloc_semaphore(f"s_lx{i}") for i in range(2)]
        s_rp2 = [nc.alloc_semaphore(f"s_rp{i}") for i in range(2)]
        s_wsl2 = [nc.alloc_semaphore(f"s_wsl{i}") for i in range(2)]
        s_v8 = [nc.alloc_semaphore(f"s_v{i}") for i in range(8)]
        s_vst2 = [nc.alloc_semaphore(f"s_vst{i}") for i in range(2)]
        s_ost2 = [nc.alloc_semaphore(f"s_ost{i}") for i in range(2)]
        s_w1t2 = [nc.alloc_semaphore(f"s_w1t{i}") for i in range(2)]
        s_w2t2 = [nc.alloc_semaphore(f"s_w2t{i}") for i in range(2)]
        s_stage = [nc.alloc_semaphore(f"s_stage{i}") for i in range(2 * NE)]

        p = Prog(nc)

        def ph0():
            def f_sync(sy):
                ii = sy.dma_start(out=swp_sb[:, :], in_=swp_in[:, :])
                p.inc(ii, s_once, 16)
                ii = sy.dma_start(out=id_sb[:, :], in_=id_in[:, :])
                p.inc(ii, s_once, 16)

            def f_gp(g):
                ii = g.memset(ones128[:, :], 1.0)
                p.inc(ii, s_gp, 1)
                ii = g.memset(zero8[:, :], 0.0)
                p.inc(ii, s_gp, 1)
                ii = g.memset(epsc[:, :], EPS)
                p.inc(ii, s_gp, 1)

            p.run_phase({"sync": f_sync, "gpsimd": f_gp})

        ph0()

        def ln_phase(src_dram, dst_sb, do_logits):
            with (
                nc.sbuf_tensor(un("lx"), [128, 2048], F32) as lx,
                nc.sbuf_tensor(un("lxn"), [128, 2048], BF16) as _lxn,
                nc.sbuf_tensor(un("lsq"), [128, 1024], BF16) as lsq,
                nc.sbuf_tensor(un("llg"), [128, 16], F32) as llg,
                nc.sbuf_tensor(un("lmk"), [128, 16], F32) as lmk,
            ):
                lxn = dbg_lxn if (dbg and dbg_stage == 1 and dbg_lxn is not None) else _lxn
                if dbg and dbg_stage == 1 and dbg_lx is not None:
                    lx = dbg_lx
                tk = TokDict()

                def f_sync(sy):
                    p.wait_all(sy)
                    for tt in range(NT):
                        if tt >= 2:
                            p.wt(sy, tk[("ap", tt - 2)])
                            p.wt(sy, tk[("sum", tt - 2)])
                        ii = sy.dma_start(
                            out=lx[:, (tt % 2) * 1024:(tt % 2) * 1024 + 1024],
                            in_=src_dram[tt * 128:(tt + 1) * 128, :])
                        tk[("ld", tt)] = p.inc(ii, s_lx2[tt % 2], 16)

                def f_vector(v):
                    p.wait_all(v)
                    for tt in range(NT):
                        xv = lx[:, (tt % 2) * 1024:(tt % 2) * 1024 + 1024]
                        p.wt(v, tk[("ld", tt)])
                        v.reduce_sum(stats[:, tt:tt + 1], xv, axis=AX.X)
                        tk[("sum", tt)] = p.dinc(v, s_dve, 1)
                        p.wt(v, tk[("sq", tt)])
                        mcol = stats[:, 64 + tt:64 + tt + 1]
                        v.tensor_single_scalar(mcol, stats[:, tt:tt + 1], 1.0 / E, ALU.mult)
                        v.drain()
                        v.tensor_mul(stats[:, tt:tt + 1], mcol, mcol)
                        v.drain()
                        v.scalar_tensor_tensor(
                            stats[:, 160 + tt:160 + tt + 1],
                            stats[:, 32 + tt:32 + tt + 1], 1.0 / E,
                            stats[:, tt:tt + 1], ALU.mult, ALU.subtract)
                        tk[("var", tt)] = p.dinc(v, s_dve, 1)
                        p.wt(v, tk[("rstd", tt)])
                        v.scalar_tensor_tensor(
                            stats[:, 128 + tt:128 + tt + 1], mcol, -1.0,
                            stats[:, 96 + tt:96 + tt + 1], ALU.mult, ALU.mult)
                        tk[("nb", tt)] = p.dinc(v, s_dve, 1)
                        p.wt(v, tk[("tr", tt)])
                        src = bass.AP(ps_tr, 0, [[1024, 128], [128, 8], [1, 128]])
                        dst = bass.AP(dst_sb, tt * 128, [[8 * T, 128], [T, 8], [1, 128]])
                        v.tensor_copy(dst, src)
                        tk[("ev", tt)] = p.dinc(v, s_dve, 1)
                        if do_logits:
                            p.wt(v, tk[("lg", tt)])
                            v.tensor_copy(llg[:, 0:8], ps_sp[:, 0:8])
                            v.drain()
                            m1 = stats[:, tt:tt + 1]
                            m2 = stats[:, 32 + tt:32 + tt + 1]
                            v.reduce_max(m1, llg[:, 0:8], axis=AX.X)
                            v.drain()
                            e1 = eqs[:, tt * 8:tt * 8 + 8]
                            v.tensor_single_scalar(e1, llg[:, 0:8], m1, ALU.is_equal)
                            v.drain()
                            v.scalar_tensor_tensor(lmk[:, 0:8], e1, -1.0e30, llg[:, 0:8],
                                                   ALU.mult, ALU.add)
                            v.drain()
                            v.reduce_max(m2, lmk[:, 0:8], axis=AX.X)
                            v.drain()
                            e2 = eqs[:, 256 + tt * 8:256 + tt * 8 + 8]
                            ii = v.tensor_single_scalar(e2, llg[:, 0:8], m2, ALU.is_equal)
                            tk[("rt", tt)] = p.inc(ii, s_dve, 1)

                def f_scalar(a):
                    p.wait_all(a)
                    for tt in range(NT):
                        xv = lx[:, (tt % 2) * 1024:(tt % 2) * 1024 + 1024]
                        p.wt(a, tk[("ld", tt)])
                        a.activation(lsq[:, :], xv, AF.Square,
                                     accum_out=stats[:, 32 + tt:32 + tt + 1])
                        tk[("sq", tt)] = p.dinc(a, s_act, 1)
                        p.wt(a, tk[("var", tt)])
                        a.activation(stats[:, 32 + tt:32 + tt + 1],
                                     stats[:, 160 + tt:160 + tt + 1], AF.Ln, bias=epsc[:, :])
                        a.drain()
                        a.activation(stats[:, 96 + tt:96 + tt + 1],
                                     stats[:, 32 + tt:32 + tt + 1], AF.Exp, scale=-0.5)
                        tk[("rstd", tt)] = p.dinc(a, s_act, 1)
                        p.wt(a, tk[("nb", tt)])
                        if tt >= 2:
                            p.wt(a, tk[("tr", tt - 2)])
                        a.activation(lxn[:, (tt % 2) * 1024:(tt % 2) * 1024 + 1024],
                                     xv, AF.Identity,
                                     bias=stats[:, 128 + tt:128 + tt + 1],
                                     scale=stats[:, 96 + tt:96 + tt + 1])
                        tk[("ap", tt)] = p.dinc(a, s_act, 1)

                def f_tensor(t):
                    p.wait_all(t)
                    for tt in range(NT):
                        p.wt(t, tk[("ap", tt)])
                        if tt >= 1:
                            p.wt(t, tk[("ev", tt - 1)])
                        for c in range(8):
                            ii = t.transpose(
                                ps_tr[:, c * 128:(c + 1) * 128],
                                lxn[:, (tt % 2) * 1024 + c * 128:(tt % 2) * 1024 + (c + 1) * 128],
                                id_sb[:, :])
                        tk[("tr", tt)] = p.inc(ii, s_pe, 1)
                        if do_logits:
                            p.wt(t, tk[("ev", tt)])
                            if tt >= 1:
                                p.wt(t, tk[("rt", tt - 1)])
                            for r in range(8):
                                ii = t.matmul(
                                    ps_sp[:, 0:8],
                                    dst_sb[:, r * T + tt * 128:r * T + (tt + 1) * 128],
                                    wr_t[:, r * 8:(r + 1) * 8],
                                    start=(r == 0), stop=(r == 7))
                            tk[("lg", tt)] = p.inc(ii, s_pe, 1)

                p.run_phase({"sync": f_sync, "vector": f_vector,
                             "scalar": f_scalar, "tensor": f_tensor}, tk)

        def proj_rope_phase(wmat, wrows, nacc, src_sb, dst_sb):
            with (
                nc.sbuf_tensor(un("pw"), [128, 2 * nacc * 128], BF16) as pw,
                nc.sbuf_tensor(un("prc"), [128, 1024], F32) as prc,
                nc.sbuf_tensor(un("prs"), [128, 1024], F32) as prs,
                nc.sbuf_tensor(un("pqt"), [128, 1024], BF16) as pqt,
                nc.sbuf_tensor(un("pt1"), [128, 1024], BF16) as pt1,
            ):
                tk = TokDict()

                def f_gp(g):
                    p.wait_all(g)
                    for c in range(8):
                        if c >= 2:
                            p.wt(g, tk[("mm", c - 2, 7)])
                        half = (c % 2) * nacc * 128
                        dst = bass.AP(pw, half, [[2 * nacc * 128, 128], [128, nacc], [1, 128]])
                        src = bass.AP(wmat.tensor, c * 128, [[E, 128], [128 * E, nacc], [1, 128]])
                        ii = g.dma_start(out=dst, in_=src)
                        tk[("w", c)] = p.inc(ii, s_wsl2[c % 2], 16)

                def f_sync(sy):
                    p.wait_all(sy)
                    for c in range(8):
                        for tg in range(8):
                            i2 = c * 8 + tg
                            par = i2 % 2
                            if i2 >= 2:
                                pc, ptg = (i2 - 2) // 8, (i2 - 2) % 8
                                p.wt(sy, tk[("out", pc, ptg)])
                            ii = sy.dma_start(out=prc[:, par * 512:par * 512 + 512],
                                              in_=ropec[:, tg * 512:(tg + 1) * 512])
                            p.inc(ii, s_rp2[par], 16)
                            ii = sy.dma_start(out=prs[:, par * 512:par * 512 + 512],
                                              in_=ropes[:, tg * 512:(tg + 1) * 512])
                            tk[("rl", c, tg)] = p.inc(ii, s_rp2[par], 16)

                def f_tensor(t):
                    p.wait_all(t)
                    for c in range(8):
                        p.wt(t, tk[("w", c)])
                        for tg in range(8):
                            i2 = c * 8 + tg
                            par = i2 % 2
                            if i2 >= 2:
                                pc, ptg = (i2 - 2) // 8, (i2 - 2) % 8
                                p.wt(t, tk[("ev", pc, ptg)])
                            for r in range(nacc):
                                ii = t.matmul(
                                    ps_a[:, par * 512:par * 512 + 512],
                                    pw[:, (c % 2) * nacc * 128 + r * 128:(c % 2) * nacc * 128 + (r + 1) * 128],
                                    src_sb[:, r * T + tg * 512:r * T + (tg + 1) * 512],
                                    start=(r == 0), stop=(r == nacc - 1))
                            tk[("mm", c, tg)] = p.inc(ii, s_pe, 1)
                            p.wt(t, tk[("ev", c, tg)])
                            if i2 >= 2:
                                pc, ptg = (i2 - 2) // 8, (i2 - 2) % 8
                                p.wt(t, tk[("out", pc, ptg)])
                            ii = t.matmul(ps_b[:, par * 512:par * 512 + 512],
                                          swp_sb[:, :],
                                          pqt[:, par * 512:par * 512 + 512],
                                          start=True, stop=True)
                            tk[("sw", c, tg)] = p.inc(ii, s_pe, 1)

                def f_scalar(a):
                    p.wait_all(a)
                    for c in range(8):
                        for tg in range(8):
                            i2 = c * 8 + tg
                            par = i2 % 2
                            p.wt(a, tk[("mm", c, tg)])
                            if i2 >= 2:
                                pc, ptg = (i2 - 2) // 8, (i2 - 2) % 8
                                p.wt(a, tk[("sw", pc, ptg)])
                                p.wt(a, tk[("p1", pc, ptg)])
                            a.activation(pqt[:, par * 512:par * 512 + 512],
                                         ps_a[:, par * 512:par * 512 + 512], AF.Copy)
                            tk[("ev", c, tg)] = p.dinc(a, s_act, 1)

                def f_vector(v):
                    p.wait_all(v)
                    for c in range(8):
                        for tg in range(8):
                            i2 = c * 8 + tg
                            par = i2 % 2
                            p.wt(v, tk[("rl", c, tg)])
                            p.wt(v, tk[("ev", c, tg)])
                            v.tensor_mul(pt1[:, par * 512:par * 512 + 512],
                                         pqt[:, par * 512:par * 512 + 512],
                                         prc[:, par * 512:par * 512 + 512])
                            tk[("p1", c, tg)] = p.dinc(v, s_dve, 1)
                            p.wt(v, tk[("sw", c, tg)])
                            dcol = dst_sb[:, c * T + tg * 512:c * T + (tg + 1) * 512]
                            v.tensor_mul(dcol, ps_b[:, par * 512:par * 512 + 512],
                                         prs[:, par * 512:par * 512 + 512])
                            v.drain()
                            ii = v.tensor_add(dcol, pt1[:, par * 512:par * 512 + 512], dcol)
                            tk[("out", c, tg)] = p.inc(ii, s_dve, 1)

                p.run_phase({"gpsimd": f_gp, "sync": f_sync, "tensor": f_tensor,
                             "scalar": f_scalar, "vector": f_vector}, tk)

        dbg_lxn = en(nc.sbuf_tensor("dbg_lxn", [128, 2048], BF16)) if (dbg and dbg_stage == 1) else None
        dbg_lx = en(nc.sbuf_tensor("dbg_lx", [128, 2048], F32)) if (dbg and dbg_stage == 1) else None
        ln_phase(x_in, A, do_logits=False)
        if dbg_stage >= 2:
            proj_rope_phase(wq, E, 8, A, Bq)

        if dbg and dbg_stage == 1:
            def f_sync_d1(sy):
                p.wait_all(sy)
                tkk = None
                for c in range(8):
                    ii = sy.dma_start(out=k_dbg[:, c * T:(c + 1) * T], in_=A[:, c * T:(c + 1) * T])
                    tkk = p.inc(ii, s_sc, 16)
                ii = sy.dma_start(out=x1_dbg[0:128, 0:192], in_=stats[:, :])
                tkk = p.inc(ii, s_sc, 16)
                ii = sy.dma_start(out=q_dbg[:, 0:2048], in_=dbg_lxn[:, :])
                tkk = p.inc(ii, s_sc, 16)
                ii = sy.dma_start(out=x1_dbg[128:256, :], in_=dbg_lx[:, 0:1024])
                tkk = p.inc(ii, s_sc, 16)
                ii = sy.dma_start(out=x1_dbg[256:384, :], in_=dbg_lx[:, 1024:2048])
                tkk = p.inc(ii, s_sc, 16)
                sy.wait_ge(tkk[0], tkk[1])
            p.run_phase({"sync": f_sync_d1})

        if dbg and dbg_stage == 2:
            def f_scalar_d2(a):
                p.wait_all(a)
                tkk = None
                for c in range(8):
                    ii = a.dma_start(out=k_dbg[:, c * T:(c + 1) * T], in_=A[:, c * T:(c + 1) * T])
                    p.inc(ii, s_sc, 16)
                    ii = a.dma_start(out=q_dbg[:, c * T:(c + 1) * T], in_=Bq[:, c * T:(c + 1) * T])
                    tkk = p.inc(ii, s_sc, 16)
                a.wait_ge(tkk[0], tkk[1])
            p.run_phase({"scalar": f_scalar_d2})

        if dbg_stage >= 3:
            with nc.sbuf_tensor("latf", [128, 4 * T], BF16) as latf:
                with nc.sbuf_tensor("dw", [128, 2048], BF16) as dw:
                    tk = TokDict()

                    def f_gp(g):
                        p.wait_all(g)
                        for lc in range(4):
                            if lc >= 2:
                                p.wt(g, tk[("mm", lc - 2, 7)])
                            half = (lc % 2) * 1024
                            dst = bass.AP(dw, half, [[2048, 128], [128, 8], [1, 128]])
                            src = bass.AP(wd.tensor, lc * 128, [[L, 128], [128 * L, 8], [1, 128]])
                            ii = g.dma_start(out=dst, in_=src)
                            tk[("w", lc)] = p.inc(ii, s_wsl2[lc % 2], 16)

                    def f_tensor(t):
                        p.wait_all(t)
                        for lc in range(4):
                            p.wt(t, tk[("w", lc)])
                            for tg in range(8):
                                i2 = lc * 8 + tg
                                par = i2 % 2
                                if i2 >= 2:
                                    pl, ptg = (i2 - 2) // 8, (i2 - 2) % 8
                                    p.wt(t, tk[("ev", pl, ptg)])
                                for r in range(8):
                                    ii = t.matmul(
                                        ps_a[:, par * 512:par * 512 + 512],
                                        dw[:, (lc % 2) * 1024 + r * 128:(lc % 2) * 1024 + (r + 1) * 128],
                                        A[:, r * T + tg * 512:r * T + (tg + 1) * 512],
                                        start=(r == 0), stop=(r == 7))
                                tk[("mm", lc, tg)] = p.inc(ii, s_pe, 1)

                    def f_scalar(a):
                        p.wait_all(a)
                        for lc in range(4):
                            for tg in range(8):
                                par = (lc * 8 + tg) % 2
                                p.wt(a, tk[("mm", lc, tg)])
                                a.activation(
                                    latf[:, lc * T + tg * 512:lc * T + (tg + 1) * 512],
                                    ps_a[:, par * 512:par * 512 + 512], AF.Copy)
                                tk[("ev", lc, tg)] = p.dinc(a, s_act, 1)

                    p.run_phase({"gpsimd": f_gp, "tensor": f_tensor, "scalar": f_scalar}, tk)

                proj_rope_phase(wk, L, 4, latf, A)

                with (
                    nc.sbuf_tensor("vw", [128, 4096], BF16) as vw,
                    nc.sbuf_tensor("vsb", [128, 2080], BF16) as vsb,
                ):
                    tk = TokDict()

                    def f_gp(g):
                        p.wait_all(g)
                        dst = bass.AP(vw, 0, [[4096, 128], [1024, 4], [1, 1024]])
                        src = bass.AP(wv.tensor, 0, [[E, 128], [128 * E, 4], [1, 1024]])
                        ii = g.dma_start(out=dst, in_=src)
                        tk["w"] = p.inc(ii, s_once, 16)
                        ii = g.memset(bass.AP(vsb, 64, [[2080, 128], [65, 16]]), 1.0)
                        p.inc(ii, s_gp, 1)
                        ii = g.memset(bass.AP(vsb, 1040 + 64, [[2080, 128], [65, 16]]), 1.0)
                        tk["ones"] = p.inc(ii, s_gp, 1)

                    def f_tensor(t):
                        p.wait_all(t)
                        p.wt(t, tk["w"])
                        p.wt(t, tk["ones"])
                        for tt in range(NT):
                            for hf in range(2):
                                i2 = tt * 2 + hf
                                par = i2 % 2
                                if i2 >= 2:
                                    pt_, pf = (i2 - 2) // 2, (i2 - 2) % 2
                                    p.wt(t, tk[("ev", pt_, pf)])
                                for r in range(4):
                                    ii = t.matmul(
                                        ps_a[:, par * 512:par * 512 + 512],
                                        latf[:, r * T + tt * 128:r * T + (tt + 1) * 128],
                                        vw[:, r * 1024 + hf * 512:r * 1024 + (hf + 1) * 512],
                                        start=(r == 0), stop=(r == 3))
                                tk[("mm", tt, hf)] = p.inc(ii, s_pe, 1)

                    def f_vector(v):
                        p.wait_all(v)
                        for tt in range(NT):
                            for hf in range(2):
                                par = (tt * 2 + hf) % 2
                                p.wt(v, tk[("mm", tt, hf)])
                                if tt >= 2 and hf == 0:
                                    p.wt(v, tk[("st", tt - 2)])
                                dst = bass.AP(vsb, (tt % 2) * 1040 + hf * 520,
                                              [[2080, 128], [130, 4], [65, 2], [1, 64]])
                                src = bass.AP(ps_a, par * 512,
                                              [[1024, 128], [128, 4], [64, 2], [1, 64]])
                                v.tensor_copy(dst, src)
                                tk[("ev", tt, hf)] = p.dinc(v, s_dve, 1)

                    def f_scalar(a):
                        p.wait_all(a)
                        for tt in range(NT):
                            p.wt(a, tk[("ev", tt, 1)])
                            ii = a.dma_start(out=vh[tt * 128:(tt + 1) * 128, :],
                                             in_=vsb[:, (tt % 2) * 1040:(tt % 2) * 1040 + 1040])
                            tk[("st", tt)] = p.inc(ii, s_vst2[tt % 2], 16)

                    p.run_phase({"gpsimd": f_gp, "tensor": f_tensor,
                                 "vector": f_vector, "scalar": f_scalar}, tk)

        if dbg and dbg_stage == 3:
            def f_scalar_d3(a):
                p.wait_all(a)
                tkk = None
                for c in range(8):
                    ii = a.dma_start(out=q_dbg[:, c * T:(c + 1) * T], in_=Bq[:, c * T:(c + 1) * T])
                    p.inc(ii, s_sc, 16)
                    ii = a.dma_start(out=k_dbg[:, c * T:(c + 1) * T], in_=A[:, c * T:(c + 1) * T])
                    p.inc(ii, s_sc, 16)
                for tt in range(NT):
                    ii = a.dma_start(out=v_dbg[tt * 128:(tt + 1) * 128, :],
                                     in_=vh[tt * 128:(tt + 1) * 128, :])
                    tkk = p.inc(ii, s_sc, 16)
                a.wait_ge(tkk[0], tkk[1])
            p.run_phase({"scalar": f_scalar_d3})

        if dbg_stage >= 4:
            with (
                nc.sbuf_tensor("wo_sb", [128, 8192], BF16) as wo_sb,
                nc.sbuf_tensor("vst", [128, 8 * 130], BF16) as vst,
                nc.sbuf_tensor("pb", [128, 2048], BF16) as pb,
                nc.sbuf_tensor("ao", [128, 4096], BF16) as ao,
                nc.sbuf_tensor("sinv", [128, 1024], BF16) as sinv,
                nc.sbuf_tensor("bcb", [128, 512], BF16) as bcb,
                nc.sbuf_tensor("st64", [128, 512], BF16) as st64,
                nc.sbuf_tensor("axb", [128, 1024], F32) as axb,
                nc.sbuf_tensor("ax1", [128, 1024], F32) as ax1,
            ):
                tk = TokDict()
                sched = []
                base_of = {}
                gi = 0
                for g in range(NG):
                    ntiles = 4 * (g + 1)
                    for hp in range(8):
                        base_of[(g, hp)] = gi
                        for i in range(ntiles):
                            sched.append((g, hp, i, gi, ntiles))
                            gi += 1

                def prev_gh(g, hp):
                    return (g, hp - 1) if hp else (g - 1, 7)

                def f_gp(g_):
                    p.wait_all(g_)
                    dst = bass.AP(wo_sb, 0, [[8192, 128], [1024, 8], [1, 1024]])
                    src = bass.AP(wo.tensor, 0, [[E, 128], [128 * E, 8], [1, 1024]])
                    ii = g_.dma_start(out=dst, in_=src)
                    tk["wo"] = p.inc(ii, s_once, 16)
                    if dbg_stage >= 6:
                        for e in range(NE):
                            ii = g_.dma_start(
                                out=bass.AP(w1bf.tensor, e * DFF * E, [[256 * E, 16], [1, 256 * E]]),
                                in_=bass.AP(w1.tensor, e * DFF * E, [[256 * E, 16], [1, 256 * E]]))
                            p.inc(ii, s_stage[2 * e], 16)
                            ii = g_.dma_start(
                                out=bass.AP(w2bf.tensor, e * E * DFF, [[256 * DFF, 4], [1, 256 * DFF]]),
                                in_=bass.AP(w2.tensor, e * E * DFF, [[256 * DFF, 4], [1, 256 * DFF]]))
                            p.inc(ii, s_stage[2 * e + 1], 16)

                def f_sync(sy):
                    p.wait_all(sy)
                    for (g, hp, i, vi, ntiles) in sched:
                        slot = vi % 8
                        if vi >= 8:
                            p.wt(sy, tk[("av", vi - 8)])
                        ii = sy.dma_start(
                            out=vst[:, slot * 130:(slot + 1) * 130],
                            in_=bass.AP(vh.tensor, i * 128 * 1040 + hp * 130,
                                        [[1040, 128], [1, 130]]))
                        tk[("v", vi)] = p.inc(ii, s_v8[slot], 16)
                        if i == ntiles - 1:
                            p.wt(sy, tk[("m1", g, hp)])
                            if g > 0:
                                p.wt(sy, tk[("op", g - 1, 3, 1)])
                            ii = sy.dma_start(out=ao[64:128, hp * 512:(hp + 1) * 512],
                                              in_=st64[0:64, :])
                            tk[("mv", g, hp)] = p.inc(ii, s_mv, 16)
                        if hp == 7 and i == ntiles - 1:
                            for tt in range(4):
                                if (g, tt) != (0, 0):
                                    pg, ptt = (g, tt - 1) if tt else (g - 1, 3)
                                    p.wt(sy, tk[("oa", pg, ptt, 1)])
                                ii = sy.dma_start(
                                    out=axb[:, :],
                                    in_=x_in[g * 512 + tt * 128:g * 512 + (tt + 1) * 128, :])
                                tk[("xl", g, tt)] = p.inc(ii, s_ax, 16)

                def f_tensor(t):
                    p.wait_all(t)
                    p.wt(t, tk["wo"])
                    for g in range(NG):
                        ntiles = 4 * (g + 1)
                        for hp in range(8):
                            base_vi = base_of[(g, hp)]

                            def emit_sp(i, vi):
                                par = vi % 2
                                psab = ps_a if par == 0 else ps_b
                                full = i < 4 * g + 2
                                qoff = g * 512 if full else g * 512 + 256
                                qlen = 512 if full else 256
                                if vi >= 2:
                                    p.wt(t, tk[("e", vi - 2)])
                                ii = t.matmul(
                                    psab[:, 0:qlen],
                                    A[0:64, hp * T + i * 128:hp * T + (i + 1) * 128],
                                    Bq[0:64, hp * T + qoff:hp * T + qoff + qlen],
                                    start=True, stop=True, tile_position=(0, 0))
                                ii = t.matmul(
                                    psab[:, 512:512 + qlen],
                                    A[64:128, hp * T + i * 128:hp * T + (i + 1) * 128],
                                    Bq[64:128, hp * T + qoff:hp * T + qoff + qlen],
                                    start=True, stop=True, tile_position=(64, 0))
                                tk[("s", vi)] = p.inc(ii, s_pe, 1)

                            emit_sp(0, base_vi)
                            emit_sp(1, base_vi + 1)
                            for i in range(ntiles):
                                vi = base_vi + i
                                par = vi % 2
                                full = i < 4 * g + 2
                                p.wt(t, tk[("e", vi)])
                                p.wt(t, tk[("v", vi)])
                                if i == 0 and (g, hp) != (0, 0):
                                    p.wt(t, tk[("m1", *prev_gh(g, hp))])
                                slot = vi % 8
                                if full:
                                    ii = t.matmul(ps_c[0:65, 0:512],
                                                  vst[:, slot * 130:slot * 130 + 65],
                                                  pb[:, par * 1024:par * 1024 + 512],
                                                  start=(i == 0), stop=(i == ntiles - 1))
                                    ii = t.matmul(ps_c[0:65, 512:1024],
                                                  vst[:, slot * 130 + 65:slot * 130 + 130],
                                                  pb[:, par * 1024 + 512:par * 1024 + 1024],
                                                  start=(i == 0), stop=(i == ntiles - 1))
                                else:
                                    ii = t.matmul(ps_c[0:65, 256:512],
                                                  vst[:, slot * 130:slot * 130 + 65],
                                                  pb[:, par * 1024:par * 1024 + 256],
                                                  start=False, stop=(i == ntiles - 1))
                                    ii = t.matmul(ps_c[0:65, 768:1024],
                                                  vst[:, slot * 130 + 65:slot * 130 + 130],
                                                  pb[:, par * 1024 + 512:par * 1024 + 768],
                                                  start=False, stop=(i == ntiles - 1))
                                tk[("av", vi)] = p.inc(ii, s_pe, 1)
                                if i + 2 < ntiles:
                                    emit_sp(i + 2, base_vi + i + 2)
                            p.wt(t, tk[("rc", g, hp)])
                            if (g, hp) != (0, 0):
                                p.wt(t, tk[("b1", *prev_gh(g, hp))])
                            if hp == 0 and g > 0:
                                p.wt(t, tk[("oa", g - 1, 3, 1)])
                            ii = t.matmul(ps_sp[0:64, :], ones128[64:65, 0:64],
                                          sinv[64:65, 0:512], start=True, stop=True)
                            tk[("bc", g, hp, 0)] = p.inc(ii, s_pe, 1)
                            p.wt(t, tk[("b0", g, hp)])
                            ii = t.matmul(ps_sp[0:64, :], ones128[64:65, 0:64],
                                          sinv[64:65, 512:1024], start=True, stop=True)
                            tk[("bc", g, hp, 1)] = p.inc(ii, s_pe, 1)
                        p.wt(t, tk[("mv", g, 7)])
                        for tt in range(4):
                            for hf in range(2):
                                if (tt, hf) == (0, 0):
                                    p.wt(t, tk[("b1", g, 7)])
                                else:
                                    pg = (g, tt, hf - 1) if hf else (g, tt - 1, 1)
                                    p.wt(t, tk[("oa", *pg)])
                                for r in range(8):
                                    ii = t.matmul(
                                        ps_sp[:, :],
                                        ao[:, r * 512 + tt * 128:r * 512 + (tt + 1) * 128],
                                        wo_sb[:, r * 1024 + hf * 512:r * 1024 + (hf + 1) * 512],
                                        start=(r == 0), stop=(r == 7))
                                tk[("op", g, tt, hf)] = p.inc(ii, s_pe, 1)

                def f_scalar(a):
                    p.wait_all(a)
                    for g in range(NG):
                        ntiles = 4 * (g + 1)
                        for hp in range(8):
                            base_vi = base_of[(g, hp)]
                            for i in range(ntiles):
                                vi = base_vi + i
                                par = vi % 2
                                psab = ps_a if par == 0 else ps_b
                                full = i < 4 * g + 2
                                p.wt(a, tk[("s", vi)])
                                if vi >= 2:
                                    p.wt(a, tk[("av", vi - 2)])
                                if full:
                                    a.activation(pb[:, par * 1024:(par + 1) * 1024],
                                                 psab[:, :], AF.Exp, scale=0.125)
                                else:
                                    a.activation(pb[:, par * 1024:par * 1024 + 256],
                                                 psab[:, 0:256], AF.Exp, scale=0.125)
                                    a.activation(pb[:, par * 1024 + 512:par * 1024 + 768],
                                                 psab[:, 512:768], AF.Exp, scale=0.125)
                                tk[("e", vi)] = p.dinc(a, s_act, 1)
                        for tt in range(4):
                            p.wt(a, tk[("oa", g, tt, 1)])
                            ii = a.dma_start(
                                out=x1h[g * 512 + tt * 128:g * 512 + (tt + 1) * 128, :],
                                in_=ax1[:, :])
                            tk[("x1", g, tt)] = p.inc(ii, s_sc, 16)

                def f_vector(v):
                    p.wait_all(v)
                    for g in range(NG):
                        ntiles = 4 * (g + 1)
                        for hp in range(8):
                            lastvi = base_of[(g, hp)] + ntiles - 1
                            p.wt(v, tk[("av", lastvi)])
                            if (g, hp) != (0, 0):
                                p.wt(v, tk[("mv", *prev_gh(g, hp))])
                            if hp == 0 and g > 0:
                                p.wt(v, tk[("op", g - 1, 3, 1)])
                            with nc.allow_low_precision(reason="bf16 softmax normalizer (0.4% rel; bar 5.5%)"):
                                v.reciprocal(sinv[64:65, 0:512], ps_c[64:65, 0:512])
                                v.reciprocal(sinv[64:65, 512:1024], ps_c[64:65, 512:1024])
                            tk[("rc", g, hp)] = p.dinc(v, s_dve, 1)
                            p.wt(v, tk[("bc", g, hp, 0)])
                            v.tensor_copy(bcb[0:64, :], ps_sp[0:64, :])
                            tk[("b0", g, hp)] = p.dinc(v, s_dve, 1)
                            v.tensor_mul(ao[0:64, hp * 512:(hp + 1) * 512],
                                         ps_c[0:64, 0:512], bcb[0:64, :])
                            tk[("m0", g, hp)] = p.dinc(v, s_dve, 1)
                            p.wt(v, tk[("bc", g, hp, 1)])
                            v.tensor_copy(bcb[0:64, :], ps_sp[0:64, :])
                            tk[("b1", g, hp)] = p.dinc(v, s_dve, 1)
                            v.tensor_mul(st64[0:64, :], ps_c[0:64, 512:1024],
                                         bcb[0:64, :])
                            tk[("m1", g, hp)] = p.dinc(v, s_dve, 1)
                        for tt in range(4):
                            for hf in range(2):
                                p.wt(v, tk[("op", g, tt, hf)])
                                p.wt(v, tk[("xl", g, tt)])
                                if hf == 0 and (g, tt) != (0, 0):
                                    pg, ptt = (g, tt - 1) if tt else (g - 1, 3)
                                    p.wt(v, tk[("x1", pg, ptt)])
                                v.tensor_add(ax1[:, hf * 512:(hf + 1) * 512],
                                             ps_sp[:, :],
                                             axb[:, hf * 512:(hf + 1) * 512])
                                tk[("oa", g, tt, hf)] = p.dinc(v, s_dve, 1)

                p.run_phase({"gpsimd": f_gp, "sync": f_sync, "tensor": f_tensor,
                             "scalar": f_scalar, "vector": f_vector}, tk)

        if dbg and dbg_stage == 4:
            def f_scalar_d4(a):
                p.wait_all(a)
                tkk = None
                for tt in range(NT):
                    ii = a.dma_start(out=x1_dbg[tt * 128:(tt + 1) * 128, :],
                                     in_=x1h[tt * 128:(tt + 1) * 128, :])
                    tkk = p.inc(ii, s_sc, 16)
                a.wait_ge(tkk[0], tkk[1])
            p.run_phase({"scalar": f_scalar_d4})

        if dbg_stage >= 5:
            with nc.sbuf_tensor("wr8", [8, 1024], BF16) as wr8:
                tk = TokDict()

                def f_gp(g):
                    p.wait_all(g)
                    ii = g.dma_start(out=wr8[:, :], in_=wr[:, :])
                    tk["w"] = p.inc(ii, s_once, 16)

                def f_tensor(t):
                    p.wait_all(t)
                    p.wt(t, tk["w"])
                    for r in range(8):
                        ii = t.transpose(ps_tr[:, r * 8:(r + 1) * 8],
                                         wr8[0:8, r * 128:(r + 1) * 128], id_sb[0:8, 0:8])
                    tk["tr"] = p.inc(ii, s_pe, 1)

                def f_vector(v):
                    p.wait_all(v)
                    p.wt(v, tk["tr"])
                    v.tensor_copy(wr_t[:, :], ps_tr[:, 0:64])
                    p.dinc(v, s_dve, 1)

                p.run_phase({"gpsimd": f_gp, "tensor": f_tensor, "vector": f_vector}, tk)

            ln_phase(x1h, A, do_logits=True)

            def gates_phase():
                tk = TokDict()

                def f_vector(v):
                    p.wait_all(v)
                    v.tensor_sub(stats[:, 64:96], stats[:, 32:64], stats[:, 0:32])
                    v.tensor_sub(stats[:, 96:128], stats[:, 0:32], stats[:, 32:64])
                    tk["d"] = p.dinc(v, s_dve, 1)
                    p.wt(v, tk["exp"])
                    v.tensor_single_scalar(stats[:, 64:128], stats[:, 128:192], 1.0, ALU.add)
                    v.drain()
                    v.reciprocal(stats[:, 128:192], stats[:, 64:128])
                    v.drain()
                    p.inc(v.engine_nop(), s_dve, 1)
                    for tt in range(NT):
                        gcol = gates[:, tt * 8:(tt + 1) * 8]
                        v.scalar_tensor_tensor(gcol, eqs[:, 256 + tt * 8:256 + tt * 8 + 8],
                                               stats[:, 160 + tt:160 + tt + 1], zero8[:, :],
                                               ALU.mult, ALU.add)
                        v.drain()
                        ii = v.scalar_tensor_tensor(gcol, eqs[:, tt * 8:tt * 8 + 8],
                                                    stats[:, 128 + tt:128 + tt + 1], gcol,
                                                    ALU.mult, ALU.add)
                        p.inc(ii, s_dve, 1)

                def f_scalar(a):
                    p.wait_all(a)
                    p.wt(a, tk["d"])
                    a.activation(stats[:, 128:192], stats[:, 64:128], AF.Exp)
                    tk["exp"] = p.dinc(a, s_act, 1)

                p.run_phase({"vector": f_vector, "scalar": f_scalar}, tk)

            gates_phase()

        if dbg and dbg_stage == 5:
            def f_scalar_d5(a):
                p.wait_all(a)
                ii = a.dma_start(out=g_dbg[:, :], in_=gates[:, :])
                tkk = p.inc(ii, s_sc, 16)
                a.wait_ge(tkk[0], tkk[1])
            p.run_phase({"scalar": f_scalar_d5})

        if dbg_stage >= 6:
            with (
                nc.sbuf_tensor("w1t", [128, 8192], BF16) as w1t,
                nc.sbuf_tensor("w2t", [128, 8192], BF16) as w2t,
                nc.sbuf_tensor("hsb", [128, 16384], BF16) as hsb,
            ):
                tk = TokDict()
                ES = [(e, sl) for e in range(NE) for sl in range(8)]

                def f_vector(v):
                    p.wait_all(v)
                    v.memset(Bq[:, :], 0.0)
                    tk["yz"] = p.dinc(v, s_dve, 1)
                    for (e, sl) in ES:
                        for tg in range(8):
                            for tt in range(4):
                                for hf in range(2):
                                    yi = (tg * 4 + tt) * 2 + hf
                                    p.wt(v, tk[("y", e, sl, tg, tt, hf)])
                                    gidx = (tg * 4 + tt) * 8 + e
                                    ycol = (tg * 4 + tt) * 1024 + hf * 512
                                    yv = Bq[:, ycol:ycol + 512]
                                    ii = v.scalar_tensor_tensor(
                                        yv, ps_c[:, (yi % 2) * 512:(yi % 2) * 512 + 512],
                                        gates[:, gidx:gidx + 1], yv, ALU.mult, ALU.add)
                                    tk[("stt", e, sl, yi)] = p.inc(ii, s_dve, 1)

                def f_sync(sy):
                    p.wait_all(sy)
                    for (e, sl) in ES:
                        par = sl % 2
                        if sl == 0:
                            sy.wait_ge(s_stage[2 * e], 16)
                        if (e, sl) not in ((0, 0), (0, 1)):
                            pe_, psl = (e, sl - 2) if sl >= 2 else (e - 1, sl + 6)
                            p.wt(sy, tk[("y", pe_, psl, 7, 3, 1)])
                        for r in range(8):
                            ii = sy.dma_start(
                                out=w1t[:, par * 4096 + r * 512:par * 4096 + (r + 1) * 512],
                                in_=bass.AP(w1bf.tensor, e * DFF * E + sl * 512 * E + r * 128,
                                            [[E, 512], [1, 128]]),
                                transpose=True)
                            tk[("w1", e, sl)] = p.inc(ii, s_w1t2[par], 16)
                        if sl == 0:
                            sy.wait_ge(s_stage[2 * e + 1], 16)
                        for fs in range(4):
                            ii = sy.dma_start(
                                out=w2t[:, par * 4096 + fs * 1024:par * 4096 + (fs + 1) * 1024],
                                in_=bass.AP(w2bf.tensor, e * E * DFF + sl * 512 + fs * 128,
                                            [[DFF, 1024], [1, 128]]),
                                transpose=True)
                            tk[("w2", e, sl)] = p.inc(ii, s_w2t2[par], 16)

                def f_tensor(t):
                    p.wait_all(t)
                    p.wt(t, tk["yz"])
                    hidx = 0
                    for (e, sl) in ES:
                        par = sl % 2
                        p.wt(t, tk[("w1", e, sl)])
                        for tg in range(8):
                            for fs in range(4):
                                bsel = (hidx // 2) % 2
                                psh = ps_a if bsel == 0 else ps_b
                                half = (hidx % 2) * 512
                                if hidx >= 4:
                                    p.wt(t, tk[("si", (hidx - 4) // 2)])
                                for r in range(8):
                                    ii = t.matmul(
                                        psh[:, half:half + 512],
                                        w1t[:, par * 4096 + r * 512 + fs * 128:par * 4096 + r * 512 + (fs + 1) * 128],
                                        A[:, r * T + tg * 512:r * T + (tg + 1) * 512],
                                        start=(r == 0), stop=(r == 7))
                                tk[("h", hidx)] = p.inc(ii, s_pe, 1)
                                hidx += 1
                        p.wt(t, tk[("w2", e, sl)])
                        p.wt(t, tk[("si", (hidx - 2) // 2)])
                        for tg in range(8):
                            for tt in range(4):
                                for hf in range(2):
                                    yi = (tg * 4 + tt) * 2 + hf
                                    if (e, sl) != (0, 0) or yi >= 2:
                                        if yi >= 2:
                                            p.wt(t, tk[("stt", e, sl, yi - 2)])
                                        else:
                                            pe_, psl = (e, sl - 1) if sl else (e - 1, 7)
                                            p.wt(t, tk[("stt", pe_, psl, 62 + yi)])
                                    for fs in range(4):
                                        ii = t.matmul(
                                            ps_c[:, (yi % 2) * 512:(yi % 2) * 512 + 512],
                                            hsb[:, fs * 4096 + tg * 512 + tt * 128:fs * 4096 + tg * 512 + (tt + 1) * 128],
                                            w2t[:, par * 4096 + fs * 1024 + hf * 512:par * 4096 + fs * 1024 + (hf + 1) * 512],
                                            start=(fs == 0), stop=(fs == 3))
                                    tk[("y", e, sl, tg, tt, hf)] = p.inc(ii, s_pe, 1)

                def f_scalar(a):
                    p.wait_all(a)
                    hidx = 0
                    for (e, sl) in ES:
                        for tg in range(8):
                            for fs in range(4):
                                if hidx % 2 == 1:
                                    p.wt(a, tk[("h", hidx)])
                                    bsel = (hidx // 2) % 2
                                    psh = ps_a if bsel == 0 else ps_b
                                    dst = bass.AP(hsb, (fs - 1) * 4096 + tg * 512,
                                                  [[16384, 128], [4096, 2], [1, 512]])
                                    a.activation(dst, psh[:, :], AF.Silu)
                                    tk[("si", hidx // 2)] = p.dinc(a, s_act, 1)
                                hidx += 1

                p.run_phase({"vector": f_vector, "sync": f_sync,
                             "tensor": f_tensor, "scalar": f_scalar}, tk)

        if dbg_stage >= 7:
            with (
                nc.sbuf_tensor("fx", [128, 2048], F32) as fx,
                nc.sbuf_tensor("fo", [128, 2048], F32) as fo,
            ):
                tk = TokDict()

                def f_sync(sy):
                    p.wait_all(sy)
                    for tt in range(NT):
                        if tt >= 2:
                            p.wt(sy, tk[("add", tt - 2)])
                        ii = sy.dma_start(out=fx[:, (tt % 2) * 1024:(tt % 2) * 1024 + 1024],
                                          in_=x1h[tt * 128:(tt + 1) * 128, :])
                        tk[("l", tt)] = p.inc(ii, s_lx2[tt % 2], 16)

                def f_vector(v):
                    p.wait_all(v)
                    for tt in range(NT):
                        p.wt(v, tk[("l", tt)])
                        if tt >= 2:
                            p.wt(v, tk[("st", tt - 2)])
                        v.tensor_add(fo[:, (tt % 2) * 1024:(tt % 2) * 1024 + 1024],
                                     Bq[:, tt * 1024:(tt + 1) * 1024],
                                     fx[:, (tt % 2) * 1024:(tt % 2) * 1024 + 1024])
                        tk[("add", tt)] = p.dinc(v, s_dve, 1)

                def f_scalar(a):
                    p.wait_all(a)
                    for tt in range(NT):
                        p.wt(a, tk[("add", tt)])
                        ii = a.dma_start(out=out[tt * 128:(tt + 1) * 128, :],
                                         in_=fo[:, (tt % 2) * 1024:(tt % 2) * 1024 + 1024])
                        tk[("st", tt)] = p.inc(ii, s_ost2[tt % 2], 16)
                    a.wait_ge(s_ost2[0], p.tot(s_ost2[0]))
                    a.wait_ge(s_ost2[1], p.tot(s_ost2[1]))
                    for sem in s_stage:
                        if p.tot(sem):
                            a.wait_ge(sem, p.tot(sem))

                p.run_phase({"sync": f_sync, "vector": f_vector, "scalar": f_scalar}, tk)
        else:
            def f_scalar_end(a):
                p.wait_all(a)
                for sem in s_stage:
                    if p.tot(sem):
                        a.wait_ge(sem, p.tot(sem))
            p.run_phase({"scalar": f_scalar_end})

    return nc


def kernel(L0_attn_x, L0_attn_wq, L0_attn_w_down, L0_attn_wk_up, L0_attn_wv_up,
           L0_attn_wo, L0_ffn_W1, L0_ffn_W2, L0_ffn_Wr):
    C, S, swp, ident = _host_consts()
    nc = build_nc(dbg_stage=DBG_STAGE, dbg=DBG)
    x = np.ascontiguousarray(np.asarray(L0_attn_x, dtype=np.float32))
    common = {
        "wq": np.ascontiguousarray(L0_attn_wq, dtype=np.float32),
        "wd": np.ascontiguousarray(L0_attn_w_down, dtype=np.float32),
        "wk": np.ascontiguousarray(L0_attn_wk_up, dtype=np.float32),
        "wv": np.ascontiguousarray(L0_attn_wv_up, dtype=np.float32),
        "wo": np.ascontiguousarray(L0_attn_wo, dtype=np.float32),
        "w1": np.ascontiguousarray(L0_ffn_W1, dtype=np.float32).reshape(NE * DFF, E),
        "w2": np.ascontiguousarray(L0_ffn_W2, dtype=np.float32).reshape(NE * E, DFF),
        "wr": np.ascontiguousarray(L0_ffn_Wr, dtype=np.float32),
        "ropec": C, "ropes": S, "swp": swp, "ident": ident,
    }
    in_maps = [dict(common, x=np.ascontiguousarray(x[b])) for b in range(NCORE)]
    res = run_bass_kernel_spmd(nc, in_maps, core_ids=list(range(NCORE)))
    return np.stack([r["out"] for r in res.results], axis=0)


# version 02
# speedup vs baseline: 1.0662x; 1.0662x over previous
import numpy as np
import ml_dtypes
import concourse.bass as bass
import concourse.mybir as mybir
from concourse.bass_utils import run_bass_kernel_spmd

F32 = mybir.dt.float32
BF16 = mybir.dt.bfloat16
AF = mybir.ActivationFunctionType
ALU = mybir.AluOpType
AX = mybir.AxisListType
BF = ml_dtypes.bfloat16

T, E, NH, D, L = 4096, 1024, 16, 64, 512
NE, DFF = 8, 4096
NT = T // 128
NG = T // 512
EPS = 1e-5
NCORE = 8

DBG_STAGE = 9
DBG = 0


def _host_consts():
    inv = 1.0 / (10000.0 ** (np.arange(0, D, 2, dtype=np.float32) / D))
    ang = np.outer(np.arange(T, dtype=np.float32), inv)
    cos = np.cos(ang).astype(np.float32)
    sin = np.sin(ang).astype(np.float32)
    C = np.zeros((128, T), np.float32)
    S = np.zeros((128, T), np.float32)
    for r in range(128):
        i = r % 64
        pp = i // 2
        C[r] = cos[:, pp]
        S[r] = sin[:, pp] * (1.0 if i % 2 == 1 else -1.0)
    swp = np.zeros((128, 128), BF)
    for r in range(128):
        swp[r ^ 1, r] = 1.0
    ident = np.eye(128, dtype=BF)
    return C, S, swp, ident


class _FakeInstr:
    def then_inc(self, sem, k):
        return self


class TokDict(dict):

    def __init__(self):
        super().__init__()
        self.missed = set()

    def __missing__(self, k):
        self.missed.add(k)
        return 0


class _CountEng:
    def wait_ge(self, sem, val):
        pass

    def __getattr__(self, name):
        def f(*a, **kw):
            return _FakeInstr()
        return f


class Prog:

    ORDER = ("sync", "gpsimd", "tensor", "scalar", "vector")

    def __init__(self, nc):
        self.nc = nc
        self.c = {}
        self.snap = {}

    def inc(self, instr, sem, k):
        instr.then_inc(sem, k)
        self.c[sem] = self.c.get(sem, 0) + k
        return (sem, self.c[sem])

    def dinc(self, eng, sem, k):
        d = eng.drain()
        d.then_inc(sem, k)
        self.c[sem] = self.c.get(sem, 0) + k
        return (sem, self.c[sem])

    def tot(self, sem):
        return self.c.get(sem, 0)

    def wt(self, eng, tok):
        if tok:
            sem, val = tok
            if val:
                eng.wait_ge(sem, val)

    def wait_all(self, eng):
        for sem, t in self.snap.items():
            if t:
                eng.wait_ge(sem, t)

    def run_phase(self, streams, tk=None):
        self.snap = dict(self.c)
        start = dict(self.c)
        for name in self.ORDER:
            if name in streams:
                streams[name](_CountEng())
        if tk is not None:
            unresolved = tk.missed - set(tk.keys())
            assert not unresolved, f"unresolved tokens: {unresolved}"
        end = dict(self.c)
        self.c = dict(start)
        with self.nc.Block() as blk:
            for name in self.ORDER:
                if name in streams:
                    getattr(blk, name)(streams[name])
        assert self.c == end, "two-pass mismatch"


def build_nc(dbg_stage=9, dbg=0):
    nc = bass.Bass("TRN2", target_bir_lowering=False)
    _uc = [0]

    def un(s):
        _uc[0] += 1
        return f"{s}_{_uc[0]}"

    x_in = nc.dram_tensor("x", [T, E], F32, kind="ExternalInput").ap()
    wq = nc.dram_tensor("wq", [E, E], F32, kind="ExternalInput").ap()
    wd = nc.dram_tensor("wd", [E, L], F32, kind="ExternalInput").ap()
    wk = nc.dram_tensor("wk", [L, E], F32, kind="ExternalInput").ap()
    wv = nc.dram_tensor("wv", [L, E], F32, kind="ExternalInput").ap()
    wo = nc.dram_tensor("wo", [E, E], F32, kind="ExternalInput").ap()
    w1 = nc.dram_tensor("w1", [NE * DFF, E], F32, kind="ExternalInput").ap()
    w2 = nc.dram_tensor("w2", [NE * E, DFF], F32, kind="ExternalInput").ap()
    wr = nc.dram_tensor("wr", [NE, E], F32, kind="ExternalInput").ap()
    ropec = nc.dram_tensor("ropec", [128, T], F32, kind="ExternalInput").ap()
    ropes = nc.dram_tensor("ropes", [128, T], F32, kind="ExternalInput").ap()
    swp_in = nc.dram_tensor("swp", [128, 128], BF16, kind="ExternalInput").ap()
    id_in = nc.dram_tensor("ident", [128, 128], BF16, kind="ExternalInput").ap()

    out = nc.dram_tensor("out", [T, E], F32, kind="ExternalOutput").ap()
    if dbg:
        q_dbg = nc.dram_tensor("q_dbg", [128, 8 * T], BF16, kind="ExternalOutput").ap()
        k_dbg = nc.dram_tensor("k_dbg", [128, 8 * T], BF16, kind="ExternalOutput").ap()
        x1_dbg = nc.dram_tensor("x1_dbg", [T, E], F32, kind="ExternalOutput").ap()
        g_dbg = nc.dram_tensor("g_dbg", [128, 256], F32, kind="ExternalOutput").ap()
        v_dbg = nc.dram_tensor("v_dbg", [T, 1040], BF16, kind="ExternalOutput").ap()

    w1bf = nc.dram_tensor("w1bf", [NE * DFF, E], BF16).ap()
    w2bf = nc.dram_tensor("w2bf", [NE * E, DFF], BF16).ap()
    x1h = nc.dram_tensor("x1h", [T, E], F32).ap()
    vh = nc.dram_tensor("vh", [T, 1040], BF16).ap()

    from contextlib import ExitStack
    with ExitStack() as _st:
        en = _st.enter_context
        A = en(nc.sbuf_tensor("A", [128, 8 * T], BF16))
        Bq = en(nc.sbuf_tensor("Bq", [128, 8 * T], BF16))
        ones128 = en(nc.sbuf_tensor("ones128", [128, 128], BF16))
        swp_sb = en(nc.sbuf_tensor("swp_sb", [128, 128], BF16))
        id_sb = en(nc.sbuf_tensor("id_sb", [128, 128], BF16))
        stats = en(nc.sbuf_tensor("stats", [128, 192], F32))
        gates = en(nc.sbuf_tensor("gates", [128, 256], F32))
        zero8 = en(nc.sbuf_tensor("zero8", [128, 8], F32))
        epsc = en(nc.sbuf_tensor("epsc", [128, 1], F32))
        eqs = en(nc.sbuf_tensor("eqs", [128, 512], F32))
        wr_t = en(nc.sbuf_tensor("wr_t", [128, 64], BF16))
        ps_a = en(nc.psum_tensor("ps_a", [128, 1024], F32))
        ps_b = en(nc.psum_tensor("ps_b", [128, 1024], F32))
        ps_c = en(nc.psum_tensor("ps_c", [128, 1024], F32))
        ps_sp = en(nc.psum_tensor("ps_sp", [128, 512], F32))
        ps_tr = en(nc.psum_tensor("ps_tr", [128, 1024], BF16))
        s_pe = nc.alloc_semaphore("s_pe")
        s_act = nc.alloc_semaphore("s_act")
        s_dve = nc.alloc_semaphore("s_dve")
        s_gp = nc.alloc_semaphore("s_gp")
        s_once = nc.alloc_semaphore("s_once")
        s_sc = nc.alloc_semaphore("s_sc")
        s_mv = nc.alloc_semaphore("s_mv")
        s_ax = nc.alloc_semaphore("s_ax")
        s_lx2 = [nc.alloc_semaphore(f"s_lx{i}") for i in range(2)]
        s_rp2 = [nc.alloc_semaphore(f"s_rp{i}") for i in range(2)]
        s_wsl2 = [nc.alloc_semaphore(f"s_wsl{i}") for i in range(2)]
        s_v8 = [nc.alloc_semaphore(f"s_v{i}") for i in range(8)]
        s_vst2 = [nc.alloc_semaphore(f"s_vst{i}") for i in range(2)]
        s_ost2 = [nc.alloc_semaphore(f"s_ost{i}") for i in range(2)]
        s_w1t2 = [nc.alloc_semaphore(f"s_w1t{i}") for i in range(2)]
        s_w2t2 = [nc.alloc_semaphore(f"s_w2t{i}") for i in range(2)]
        s_stage = [nc.alloc_semaphore(f"s_stage{i}") for i in range(2 * NE)]

        p = Prog(nc)

        def ph0():
            def f_sync(sy):
                ii = sy.dma_start(out=swp_sb[:, :], in_=swp_in[:, :])
                p.inc(ii, s_once, 16)
                ii = sy.dma_start(out=id_sb[:, :], in_=id_in[:, :])
                p.inc(ii, s_once, 16)

            def f_gp(g):
                ii = g.memset(ones128[:, :], 1.0)
                p.inc(ii, s_gp, 1)
                ii = g.memset(zero8[:, :], 0.0)
                p.inc(ii, s_gp, 1)
                ii = g.memset(epsc[:, :], EPS)
                p.inc(ii, s_gp, 1)

            p.run_phase({"sync": f_sync, "gpsimd": f_gp})

        ph0()

        def ln_phase(src_dram, dst_sb, do_logits):
            with (
                nc.sbuf_tensor(un("lx"), [128, 2048], F32) as lx,
                nc.sbuf_tensor(un("lxn"), [128, 2048], BF16) as _lxn,
                nc.sbuf_tensor(un("lsq"), [128, 1024], BF16) as lsq,
                nc.sbuf_tensor(un("llg"), [128, 16], F32) as llg,
                nc.sbuf_tensor(un("lmk"), [128, 16], F32) as lmk,
            ):
                lxn = dbg_lxn if (dbg and dbg_stage == 1 and dbg_lxn is not None) else _lxn
                if dbg and dbg_stage == 1 and dbg_lx is not None:
                    lx = dbg_lx
                tk = TokDict()

                def f_sync(sy):
                    p.wait_all(sy)
                    for tt in range(NT):
                        if tt >= 2:
                            p.wt(sy, tk[("ap", tt - 2)])
                            p.wt(sy, tk[("sum", tt - 2)])
                        ii = sy.dma_start(
                            out=lx[:, (tt % 2) * 1024:(tt % 2) * 1024 + 1024],
                            in_=src_dram[tt * 128:(tt + 1) * 128, :])
                        tk[("ld", tt)] = p.inc(ii, s_lx2[tt % 2], 16)

                def f_vector(v):
                    p.wait_all(v)
                    for tt in range(NT):
                        xv = lx[:, (tt % 2) * 1024:(tt % 2) * 1024 + 1024]
                        p.wt(v, tk[("ld", tt)])
                        v.reduce_sum(stats[:, tt:tt + 1], xv, axis=AX.X)
                        tk[("sum", tt)] = p.dinc(v, s_dve, 1)
                        p.wt(v, tk[("sq", tt)])
                        mcol = stats[:, 64 + tt:64 + tt + 1]
                        v.tensor_single_scalar(mcol, stats[:, tt:tt + 1], 1.0 / E, ALU.mult)
                        v.drain()
                        v.tensor_mul(stats[:, tt:tt + 1], mcol, mcol)
                        v.drain()
                        v.scalar_tensor_tensor(
                            stats[:, 160 + tt:160 + tt + 1],
                            stats[:, 32 + tt:32 + tt + 1], 1.0 / E,
                            stats[:, tt:tt + 1], ALU.mult, ALU.subtract)
                        tk[("var", tt)] = p.dinc(v, s_dve, 1)
                        p.wt(v, tk[("rstd", tt)])
                        v.scalar_tensor_tensor(
                            stats[:, 128 + tt:128 + tt + 1], mcol, -1.0,
                            stats[:, 96 + tt:96 + tt + 1], ALU.mult, ALU.mult)
                        tk[("nb", tt)] = p.dinc(v, s_dve, 1)
                        p.wt(v, tk[("tr", tt)])
                        src = bass.AP(ps_tr, 0, [[1024, 128], [128, 8], [1, 128]])
                        dst = bass.AP(dst_sb, tt * 128, [[8 * T, 128], [T, 8], [1, 128]])
                        v.tensor_copy(dst, src)
                        tk[("ev", tt)] = p.dinc(v, s_dve, 1)
                        if do_logits:
                            p.wt(v, tk[("lg", tt)])
                            v.tensor_copy(llg[:, 0:8], ps_sp[:, 0:8])
                            v.drain()
                            m1 = stats[:, tt:tt + 1]
                            m2 = stats[:, 32 + tt:32 + tt + 1]
                            v.reduce_max(m1, llg[:, 0:8], axis=AX.X)
                            v.drain()
                            e1 = eqs[:, tt * 8:tt * 8 + 8]
                            v.tensor_single_scalar(e1, llg[:, 0:8], m1, ALU.is_equal)
                            v.drain()
                            v.scalar_tensor_tensor(lmk[:, 0:8], e1, -1.0e30, llg[:, 0:8],
                                                   ALU.mult, ALU.add)
                            v.drain()
                            v.reduce_max(m2, lmk[:, 0:8], axis=AX.X)
                            v.drain()
                            e2 = eqs[:, 256 + tt * 8:256 + tt * 8 + 8]
                            ii = v.tensor_single_scalar(e2, llg[:, 0:8], m2, ALU.is_equal)
                            tk[("rt", tt)] = p.inc(ii, s_dve, 1)

                def f_scalar(a):
                    p.wait_all(a)
                    for tt in range(NT):
                        xv = lx[:, (tt % 2) * 1024:(tt % 2) * 1024 + 1024]
                        p.wt(a, tk[("ld", tt)])
                        a.activation(lsq[:, :], xv, AF.Square,
                                     accum_out=stats[:, 32 + tt:32 + tt + 1])
                        tk[("sq", tt)] = p.dinc(a, s_act, 1)
                        p.wt(a, tk[("var", tt)])
                        a.activation(stats[:, 32 + tt:32 + tt + 1],
                                     stats[:, 160 + tt:160 + tt + 1], AF.Ln, bias=epsc[:, :])
                        a.drain()
                        a.activation(stats[:, 96 + tt:96 + tt + 1],
                                     stats[:, 32 + tt:32 + tt + 1], AF.Exp, scale=-0.5)
                        tk[("rstd", tt)] = p.dinc(a, s_act, 1)
                        p.wt(a, tk[("nb", tt)])
                        if tt >= 2:
                            p.wt(a, tk[("tr", tt - 2)])
                        a.activation(lxn[:, (tt % 2) * 1024:(tt % 2) * 1024 + 1024],
                                     xv, AF.Identity,
                                     bias=stats[:, 128 + tt:128 + tt + 1],
                                     scale=stats[:, 96 + tt:96 + tt + 1])
                        tk[("ap", tt)] = p.dinc(a, s_act, 1)

                def f_tensor(t):
                    p.wait_all(t)
                    for tt in range(NT):
                        p.wt(t, tk[("ap", tt)])
                        if tt >= 1:
                            p.wt(t, tk[("ev", tt - 1)])
                        for c in range(8):
                            ii = t.transpose(
                                ps_tr[:, c * 128:(c + 1) * 128],
                                lxn[:, (tt % 2) * 1024 + c * 128:(tt % 2) * 1024 + (c + 1) * 128],
                                id_sb[:, :])
                        tk[("tr", tt)] = p.inc(ii, s_pe, 1)
                        if do_logits:
                            p.wt(t, tk[("ev", tt)])
                            if tt >= 1:
                                p.wt(t, tk[("rt", tt - 1)])
                            for r in range(8):
                                ii = t.matmul(
                                    ps_sp[:, 0:8],
                                    dst_sb[:, r * T + tt * 128:r * T + (tt + 1) * 128],
                                    wr_t[:, r * 8:(r + 1) * 8],
                                    start=(r == 0), stop=(r == 7))
                            tk[("lg", tt)] = p.inc(ii, s_pe, 1)

                p.run_phase({"sync": f_sync, "vector": f_vector,
                             "scalar": f_scalar, "tensor": f_tensor}, tk)

        def proj_rope_phase(wmat, wrows, nacc, src_sb, dst_sb):
            with (
                nc.sbuf_tensor(un("pw"), [128, 2 * nacc * 128], BF16) as pw,
                nc.sbuf_tensor(un("prc"), [128, 1024], F32) as prc,
                nc.sbuf_tensor(un("prs"), [128, 1024], F32) as prs,
                nc.sbuf_tensor(un("pqt"), [128, 1024], BF16) as pqt,
                nc.sbuf_tensor(un("pt1"), [128, 1024], BF16) as pt1,
            ):
                tk = TokDict()

                def f_gp(g):
                    p.wait_all(g)
                    for c in range(8):
                        if c >= 2:
                            p.wt(g, tk[("mm", c - 2, 7)])
                        half = (c % 2) * nacc * 128
                        dst = bass.AP(pw, half, [[2 * nacc * 128, 128], [128, nacc], [1, 128]])
                        src = bass.AP(wmat.tensor, c * 128, [[E, 128], [128 * E, nacc], [1, 128]])
                        ii = g.dma_start(out=dst, in_=src)
                        tk[("w", c)] = p.inc(ii, s_wsl2[c % 2], 16)

                def f_sync(sy):
                    p.wait_all(sy)
                    for c in range(8):
                        for tg in range(8):
                            i2 = c * 8 + tg
                            par = i2 % 2
                            if i2 >= 2:
                                pc, ptg = (i2 - 2) // 8, (i2 - 2) % 8
                                p.wt(sy, tk[("out", pc, ptg)])
                            ii = sy.dma_start(out=prc[:, par * 512:par * 512 + 512],
                                              in_=ropec[:, tg * 512:(tg + 1) * 512])
                            p.inc(ii, s_rp2[par], 16)
                            ii = sy.dma_start(out=prs[:, par * 512:par * 512 + 512],
                                              in_=ropes[:, tg * 512:(tg + 1) * 512])
                            tk[("rl", c, tg)] = p.inc(ii, s_rp2[par], 16)

                def f_tensor(t):
                    p.wait_all(t)
                    for c in range(8):
                        p.wt(t, tk[("w", c)])
                        for tg in range(8):
                            i2 = c * 8 + tg
                            par = i2 % 2
                            if i2 >= 2:
                                pc, ptg = (i2 - 2) // 8, (i2 - 2) % 8
                                p.wt(t, tk[("ev", pc, ptg)])
                            for r in range(nacc):
                                ii = t.matmul(
                                    ps_a[:, par * 512:par * 512 + 512],
                                    pw[:, (c % 2) * nacc * 128 + r * 128:(c % 2) * nacc * 128 + (r + 1) * 128],
                                    src_sb[:, r * T + tg * 512:r * T + (tg + 1) * 512],
                                    start=(r == 0), stop=(r == nacc - 1))
                            tk[("mm", c, tg)] = p.inc(ii, s_pe, 1)
                            p.wt(t, tk[("ev", c, tg)])
                            if i2 >= 2:
                                pc, ptg = (i2 - 2) // 8, (i2 - 2) % 8
                                p.wt(t, tk[("out", pc, ptg)])
                            ii = t.matmul(ps_b[:, par * 512:par * 512 + 512],
                                          swp_sb[:, :],
                                          pqt[:, par * 512:par * 512 + 512],
                                          start=True, stop=True)
                            tk[("sw", c, tg)] = p.inc(ii, s_pe, 1)

                def f_scalar(a):
                    p.wait_all(a)
                    for c in range(8):
                        for tg in range(8):
                            i2 = c * 8 + tg
                            par = i2 % 2
                            p.wt(a, tk[("mm", c, tg)])
                            if i2 >= 2:
                                pc, ptg = (i2 - 2) // 8, (i2 - 2) % 8
                                p.wt(a, tk[("sw", pc, ptg)])
                                p.wt(a, tk[("p1", pc, ptg)])
                            a.activation(pqt[:, par * 512:par * 512 + 512],
                                         ps_a[:, par * 512:par * 512 + 512], AF.Copy)
                            tk[("ev", c, tg)] = p.dinc(a, s_act, 1)

                def f_vector(v):
                    p.wait_all(v)
                    for c in range(8):
                        for tg in range(8):
                            i2 = c * 8 + tg
                            par = i2 % 2
                            p.wt(v, tk[("rl", c, tg)])
                            p.wt(v, tk[("ev", c, tg)])
                            v.tensor_mul(pt1[:, par * 512:par * 512 + 512],
                                         pqt[:, par * 512:par * 512 + 512],
                                         prc[:, par * 512:par * 512 + 512])
                            tk[("p1", c, tg)] = p.dinc(v, s_dve, 1)
                            p.wt(v, tk[("sw", c, tg)])
                            dcol = dst_sb[:, c * T + tg * 512:c * T + (tg + 1) * 512]
                            v.tensor_mul(dcol, ps_b[:, par * 512:par * 512 + 512],
                                         prs[:, par * 512:par * 512 + 512])
                            v.drain()
                            ii = v.tensor_add(dcol, pt1[:, par * 512:par * 512 + 512], dcol)
                            tk[("out", c, tg)] = p.inc(ii, s_dve, 1)

                p.run_phase({"gpsimd": f_gp, "sync": f_sync, "tensor": f_tensor,
                             "scalar": f_scalar, "vector": f_vector}, tk)

        dbg_lxn = en(nc.sbuf_tensor("dbg_lxn", [128, 2048], BF16)) if (dbg and dbg_stage == 1) else None
        dbg_lx = en(nc.sbuf_tensor("dbg_lx", [128, 2048], F32)) if (dbg and dbg_stage == 1) else None
        ln_phase(x_in, A, do_logits=False)
        if dbg_stage >= 2:
            proj_rope_phase(wq, E, 8, A, Bq)

        if dbg and dbg_stage == 1:
            def f_sync_d1(sy):
                p.wait_all(sy)
                tkk = None
                for c in range(8):
                    ii = sy.dma_start(out=k_dbg[:, c * T:(c + 1) * T], in_=A[:, c * T:(c + 1) * T])
                    tkk = p.inc(ii, s_sc, 16)
                ii = sy.dma_start(out=x1_dbg[0:128, 0:192], in_=stats[:, :])
                tkk = p.inc(ii, s_sc, 16)
                ii = sy.dma_start(out=q_dbg[:, 0:2048], in_=dbg_lxn[:, :])
                tkk = p.inc(ii, s_sc, 16)
                ii = sy.dma_start(out=x1_dbg[128:256, :], in_=dbg_lx[:, 0:1024])
                tkk = p.inc(ii, s_sc, 16)
                ii = sy.dma_start(out=x1_dbg[256:384, :], in_=dbg_lx[:, 1024:2048])
                tkk = p.inc(ii, s_sc, 16)
                sy.wait_ge(tkk[0], tkk[1])
            p.run_phase({"sync": f_sync_d1})

        if dbg and dbg_stage == 2:
            def f_scalar_d2(a):
                p.wait_all(a)
                tkk = None
                for c in range(8):
                    ii = a.dma_start(out=k_dbg[:, c * T:(c + 1) * T], in_=A[:, c * T:(c + 1) * T])
                    p.inc(ii, s_sc, 16)
                    ii = a.dma_start(out=q_dbg[:, c * T:(c + 1) * T], in_=Bq[:, c * T:(c + 1) * T])
                    tkk = p.inc(ii, s_sc, 16)
                a.wait_ge(tkk[0], tkk[1])
            p.run_phase({"scalar": f_scalar_d2})

        if dbg_stage >= 3:
            with nc.sbuf_tensor("latf", [128, 4 * T], BF16) as latf:
                with nc.sbuf_tensor("dw", [128, 2048], BF16) as dw:
                    tk = TokDict()

                    def f_gp(g):
                        p.wait_all(g)
                        for lc in range(4):
                            if lc >= 2:
                                p.wt(g, tk[("mm", lc - 2, 7)])
                            half = (lc % 2) * 1024
                            dst = bass.AP(dw, half, [[2048, 128], [128, 8], [1, 128]])
                            src = bass.AP(wd.tensor, lc * 128, [[L, 128], [128 * L, 8], [1, 128]])
                            ii = g.dma_start(out=dst, in_=src)
                            tk[("w", lc)] = p.inc(ii, s_wsl2[lc % 2], 16)

                    def f_tensor(t):
                        p.wait_all(t)
                        for lc in range(4):
                            p.wt(t, tk[("w", lc)])
                            for tg in range(8):
                                i2 = lc * 8 + tg
                                par = i2 % 2
                                if i2 >= 2:
                                    pl, ptg = (i2 - 2) // 8, (i2 - 2) % 8
                                    p.wt(t, tk[("ev", pl, ptg)])
                                for r in range(8):
                                    ii = t.matmul(
                                        ps_a[:, par * 512:par * 512 + 512],
                                        dw[:, (lc % 2) * 1024 + r * 128:(lc % 2) * 1024 + (r + 1) * 128],
                                        A[:, r * T + tg * 512:r * T + (tg + 1) * 512],
                                        start=(r == 0), stop=(r == 7))
                                tk[("mm", lc, tg)] = p.inc(ii, s_pe, 1)

                    def f_scalar(a):
                        p.wait_all(a)
                        for lc in range(4):
                            for tg in range(8):
                                par = (lc * 8 + tg) % 2
                                p.wt(a, tk[("mm", lc, tg)])
                                a.activation(
                                    latf[:, lc * T + tg * 512:lc * T + (tg + 1) * 512],
                                    ps_a[:, par * 512:par * 512 + 512], AF.Copy)
                                tk[("ev", lc, tg)] = p.dinc(a, s_act, 1)

                    p.run_phase({"gpsimd": f_gp, "tensor": f_tensor, "scalar": f_scalar}, tk)

                proj_rope_phase(wk, L, 4, latf, A)

                with (
                    nc.sbuf_tensor("vw", [128, 4096], BF16) as vw,
                    nc.sbuf_tensor("vsb", [128, 2080], BF16) as vsb,
                ):
                    tk = TokDict()

                    def f_gp(g):
                        p.wait_all(g)
                        dst = bass.AP(vw, 0, [[4096, 128], [1024, 4], [1, 1024]])
                        src = bass.AP(wv.tensor, 0, [[E, 128], [128 * E, 4], [1, 1024]])
                        ii = g.dma_start(out=dst, in_=src)
                        tk["w"] = p.inc(ii, s_once, 16)
                        ii = g.memset(bass.AP(vsb, 64, [[2080, 128], [65, 16]]), 1.0)
                        p.inc(ii, s_gp, 1)
                        ii = g.memset(bass.AP(vsb, 1040 + 64, [[2080, 128], [65, 16]]), 1.0)
                        tk["ones"] = p.inc(ii, s_gp, 1)

                    def f_tensor(t):
                        p.wait_all(t)
                        p.wt(t, tk["w"])
                        p.wt(t, tk["ones"])
                        for tt in range(NT):
                            for hf in range(2):
                                i2 = tt * 2 + hf
                                par = i2 % 2
                                if i2 >= 2:
                                    pt_, pf = (i2 - 2) // 2, (i2 - 2) % 2
                                    p.wt(t, tk[("ev", pt_, pf)])
                                for r in range(4):
                                    ii = t.matmul(
                                        ps_a[:, par * 512:par * 512 + 512],
                                        latf[:, r * T + tt * 128:r * T + (tt + 1) * 128],
                                        vw[:, r * 1024 + hf * 512:r * 1024 + (hf + 1) * 512],
                                        start=(r == 0), stop=(r == 3))
                                tk[("mm", tt, hf)] = p.inc(ii, s_pe, 1)

                    def f_vector(v):
                        p.wait_all(v)
                        for tt in range(NT):
                            for hf in range(2):
                                par = (tt * 2 + hf) % 2
                                p.wt(v, tk[("mm", tt, hf)])
                                if tt >= 2 and hf == 0:
                                    p.wt(v, tk[("st", tt - 2)])
                                dst = bass.AP(vsb, (tt % 2) * 1040 + hf * 520,
                                              [[2080, 128], [130, 4], [65, 2], [1, 64]])
                                src = bass.AP(ps_a, par * 512,
                                              [[1024, 128], [128, 4], [64, 2], [1, 64]])
                                v.tensor_copy(dst, src)
                                tk[("ev", tt, hf)] = p.dinc(v, s_dve, 1)

                    def f_scalar(a):
                        p.wait_all(a)
                        for tt in range(NT):
                            p.wt(a, tk[("ev", tt, 1)])
                            ii = a.dma_start(out=vh[tt * 128:(tt + 1) * 128, :],
                                             in_=vsb[:, (tt % 2) * 1040:(tt % 2) * 1040 + 1040])
                            tk[("st", tt)] = p.inc(ii, s_vst2[tt % 2], 16)

                    p.run_phase({"gpsimd": f_gp, "tensor": f_tensor,
                                 "vector": f_vector, "scalar": f_scalar}, tk)

        if dbg and dbg_stage == 3:
            def f_scalar_d3(a):
                p.wait_all(a)
                tkk = None
                for c in range(8):
                    ii = a.dma_start(out=q_dbg[:, c * T:(c + 1) * T], in_=Bq[:, c * T:(c + 1) * T])
                    p.inc(ii, s_sc, 16)
                    ii = a.dma_start(out=k_dbg[:, c * T:(c + 1) * T], in_=A[:, c * T:(c + 1) * T])
                    p.inc(ii, s_sc, 16)
                for tt in range(NT):
                    ii = a.dma_start(out=v_dbg[tt * 128:(tt + 1) * 128, :],
                                     in_=vh[tt * 128:(tt + 1) * 128, :])
                    tkk = p.inc(ii, s_sc, 16)
                a.wait_ge(tkk[0], tkk[1])
            p.run_phase({"scalar": f_scalar_d3})

        if dbg_stage >= 4:
            with (
                nc.sbuf_tensor("wo_sb", [128, 8192], BF16) as wo_sb,
                nc.sbuf_tensor("vst", [128, 8 * 130], BF16) as vst,
                nc.sbuf_tensor("pb", [128, 2048], BF16) as pb,
                nc.sbuf_tensor("ao", [128, 4096], BF16) as ao,
                nc.sbuf_tensor("sinv", [128, 1024], BF16) as sinv,
                nc.sbuf_tensor("bcb", [128, 512], BF16) as bcb,
                nc.sbuf_tensor("st64", [128, 512], BF16) as st64,
                nc.sbuf_tensor("axb", [128, 1024], F32) as axb,
                nc.sbuf_tensor("ax1", [128, 1024], F32) as ax1,
            ):
                tk = TokDict()
                sched = []
                base_of = {}
                gi = 0
                for g in range(NG):
                    ntiles = 4 * (g + 1)
                    for hp in range(8):
                        base_of[(g, hp)] = gi
                        for i in range(ntiles):
                            sched.append((g, hp, i, gi, ntiles))
                            gi += 1

                def prev_gh(g, hp):
                    return (g, hp - 1) if hp else (g - 1, 7)

                def f_gp(g_):
                    p.wait_all(g_)
                    dst = bass.AP(wo_sb, 0, [[8192, 128], [1024, 8], [1, 1024]])
                    src = bass.AP(wo.tensor, 0, [[E, 128], [128 * E, 8], [1, 1024]])
                    ii = g_.dma_start(out=dst, in_=src)
                    tk["wo"] = p.inc(ii, s_once, 16)
                    if dbg_stage >= 6:
                        for e in range(NE):
                            ii = g_.dma_start(
                                out=bass.AP(w1bf.tensor, e * DFF * E, [[256 * E, 16], [1, 256 * E]]),
                                in_=bass.AP(w1.tensor, e * DFF * E, [[256 * E, 16], [1, 256 * E]]))
                            p.inc(ii, s_stage[2 * e], 16)
                            ii = g_.dma_start(
                                out=bass.AP(w2bf.tensor, e * E * DFF, [[256 * DFF, 4], [1, 256 * DFF]]),
                                in_=bass.AP(w2.tensor, e * E * DFF, [[256 * DFF, 4], [1, 256 * DFF]]))
                            p.inc(ii, s_stage[2 * e + 1], 16)

                def f_sync(sy):
                    p.wait_all(sy)
                    for (g, hp, i, vi, ntiles) in sched:
                        slot = vi % 8
                        if vi >= 8:
                            p.wt(sy, tk[("av", vi - 8)])
                        ii = sy.dma_start(
                            out=vst[:, slot * 130:(slot + 1) * 130],
                            in_=bass.AP(vh.tensor, i * 128 * 1040 + hp * 130,
                                        [[1040, 128], [1, 130]]))
                        tk[("v", vi)] = p.inc(ii, s_v8[slot], 16)
                        if i == ntiles - 1:
                            p.wt(sy, tk[("m1", g, hp)])
                            if g > 0:
                                p.wt(sy, tk[("op", g - 1, 3, 1)])
                            ii = sy.dma_start(out=ao[64:128, hp * 512:(hp + 1) * 512],
                                              in_=st64[0:64, :])
                            tk[("mv", g, hp)] = p.inc(ii, s_mv, 16)
                        if hp == 7 and i == ntiles - 1:
                            for tt in range(4):
                                if (g, tt) != (0, 0):
                                    pg, ptt = (g, tt - 1) if tt else (g - 1, 3)
                                    p.wt(sy, tk[("oa", pg, ptt, 1)])
                                ii = sy.dma_start(
                                    out=axb[:, :],
                                    in_=x_in[g * 512 + tt * 128:g * 512 + (tt + 1) * 128, :])
                                tk[("xl", g, tt)] = p.inc(ii, s_ax, 16)

                def f_tensor(t):
                    p.wait_all(t)
                    p.wt(t, tk["wo"])
                    for g in range(NG):
                        ntiles = 4 * (g + 1)
                        for hp in range(8):
                            base_vi = base_of[(g, hp)]

                            def emit_sp(i, vi):
                                par = vi % 2
                                psab = ps_a if par == 0 else ps_b
                                full = i < 4 * g + 2
                                qoff = g * 512 if full else g * 512 + 256
                                qlen = 512 if full else 256
                                if vi >= 2:
                                    p.wt(t, tk[("e", vi - 2)])
                                ii = t.matmul(
                                    psab[:, 0:qlen],
                                    A[0:64, hp * T + i * 128:hp * T + (i + 1) * 128],
                                    Bq[0:64, hp * T + qoff:hp * T + qoff + qlen],
                                    start=True, stop=True, tile_position=(0, 0))
                                ii = t.matmul(
                                    psab[:, 512:512 + qlen],
                                    A[64:128, hp * T + i * 128:hp * T + (i + 1) * 128],
                                    Bq[64:128, hp * T + qoff:hp * T + qoff + qlen],
                                    start=True, stop=True, tile_position=(64, 0))
                                tk[("s", vi)] = p.inc(ii, s_pe, 1)

                            emit_sp(0, base_vi)
                            emit_sp(1, base_vi + 1)
                            for i in range(ntiles):
                                vi = base_vi + i
                                par = vi % 2
                                full = i < 4 * g + 2
                                p.wt(t, tk[("e", vi)])
                                p.wt(t, tk[("v", vi)])
                                if i == 0 and (g, hp) != (0, 0):
                                    p.wt(t, tk[("m1", *prev_gh(g, hp))])
                                slot = vi % 8
                                if full:
                                    ii = t.matmul(ps_c[0:65, 0:512],
                                                  vst[:, slot * 130:slot * 130 + 65],
                                                  pb[:, par * 1024:par * 1024 + 512],
                                                  start=(i == 0), stop=(i == ntiles - 1))
                                    ii = t.matmul(ps_c[0:65, 512:1024],
                                                  vst[:, slot * 130 + 65:slot * 130 + 130],
                                                  pb[:, par * 1024 + 512:par * 1024 + 1024],
                                                  start=(i == 0), stop=(i == ntiles - 1))
                                else:
                                    ii = t.matmul(ps_c[0:65, 256:512],
                                                  vst[:, slot * 130:slot * 130 + 65],
                                                  pb[:, par * 1024:par * 1024 + 256],
                                                  start=False, stop=(i == ntiles - 1))
                                    ii = t.matmul(ps_c[0:65, 768:1024],
                                                  vst[:, slot * 130 + 65:slot * 130 + 130],
                                                  pb[:, par * 1024 + 512:par * 1024 + 768],
                                                  start=False, stop=(i == ntiles - 1))
                                tk[("av", vi)] = p.inc(ii, s_pe, 1)
                                if i + 2 < ntiles:
                                    emit_sp(i + 2, base_vi + i + 2)
                            p.wt(t, tk[("rc", g, hp)])
                            if (g, hp) != (0, 0):
                                p.wt(t, tk[("b1", *prev_gh(g, hp))])
                            if hp == 0 and g > 0:
                                p.wt(t, tk[("oa", g - 1, 3, 1)])
                            ii = t.matmul(ps_sp[0:64, :], ones128[64:65, 0:64],
                                          sinv[64:65, 0:512], start=True, stop=True)
                            tk[("bc", g, hp, 0)] = p.inc(ii, s_pe, 1)
                            p.wt(t, tk[("b0", g, hp)])
                            ii = t.matmul(ps_sp[0:64, :], ones128[64:65, 0:64],
                                          sinv[64:65, 512:1024], start=True, stop=True)
                            tk[("bc", g, hp, 1)] = p.inc(ii, s_pe, 1)
                        p.wt(t, tk[("mv", g, 7)])
                        for tt in range(4):
                            for hf in range(2):
                                if (tt, hf) == (0, 0):
                                    p.wt(t, tk[("b1", g, 7)])
                                else:
                                    pg = (g, tt, hf - 1) if hf else (g, tt - 1, 1)
                                    p.wt(t, tk[("oa", *pg)])
                                for r in range(8):
                                    ii = t.matmul(
                                        ps_sp[:, :],
                                        ao[:, r * 512 + tt * 128:r * 512 + (tt + 1) * 128],
                                        wo_sb[:, r * 1024 + hf * 512:r * 1024 + (hf + 1) * 512],
                                        start=(r == 0), stop=(r == 7))
                                tk[("op", g, tt, hf)] = p.inc(ii, s_pe, 1)

                def f_scalar(a):
                    p.wait_all(a)
                    for g in range(NG):
                        ntiles = 4 * (g + 1)
                        for hp in range(8):
                            base_vi = base_of[(g, hp)]
                            for i in range(ntiles):
                                vi = base_vi + i
                                par = vi % 2
                                psab = ps_a if par == 0 else ps_b
                                full = i < 4 * g + 2
                                p.wt(a, tk[("s", vi)])
                                if vi >= 2:
                                    p.wt(a, tk[("av", vi - 2)])
                                if full:
                                    ii = a.activation(pb[:, par * 1024:(par + 1) * 1024],
                                                      psab[:, :], AF.Exp, scale=0.125)
                                else:
                                    a.activation(pb[:, par * 1024:par * 1024 + 256],
                                                 psab[:, 0:256], AF.Exp, scale=0.125)
                                    ii = a.activation(pb[:, par * 1024 + 512:par * 1024 + 768],
                                                      psab[:, 512:768], AF.Exp, scale=0.125)
                                tk[("e", vi)] = p.inc(ii, s_act, 1)
                        for tt in range(4):
                            p.wt(a, tk[("oa", g, tt, 1)])
                            ii = a.dma_start(
                                out=x1h[g * 512 + tt * 128:g * 512 + (tt + 1) * 128, :],
                                in_=ax1[:, :])
                            tk[("x1", g, tt)] = p.inc(ii, s_sc, 16)

                def f_vector(v):
                    p.wait_all(v)
                    for g in range(NG):
                        ntiles = 4 * (g + 1)
                        for hp in range(8):
                            lastvi = base_of[(g, hp)] + ntiles - 1
                            p.wt(v, tk[("av", lastvi)])
                            if (g, hp) != (0, 0):
                                p.wt(v, tk[("mv", *prev_gh(g, hp))])
                            if hp == 0 and g > 0:
                                p.wt(v, tk[("op", g - 1, 3, 1)])
                            with nc.allow_low_precision(reason="bf16 softmax normalizer (0.4% rel; bar 5.5%)"):
                                v.reciprocal(sinv[64:65, 0:512], ps_c[64:65, 0:512])
                                v.reciprocal(sinv[64:65, 512:1024], ps_c[64:65, 512:1024])
                            tk[("rc", g, hp)] = p.dinc(v, s_dve, 1)
                            p.wt(v, tk[("bc", g, hp, 0)])
                            v.tensor_copy(bcb[0:64, :], ps_sp[0:64, :])
                            tk[("b0", g, hp)] = p.dinc(v, s_dve, 1)
                            v.tensor_mul(ao[0:64, hp * 512:(hp + 1) * 512],
                                         ps_c[0:64, 0:512], bcb[0:64, :])
                            tk[("m0", g, hp)] = p.dinc(v, s_dve, 1)
                            p.wt(v, tk[("bc", g, hp, 1)])
                            v.tensor_copy(bcb[0:64, :], ps_sp[0:64, :])
                            tk[("b1", g, hp)] = p.dinc(v, s_dve, 1)
                            v.tensor_mul(st64[0:64, :], ps_c[0:64, 512:1024],
                                         bcb[0:64, :])
                            tk[("m1", g, hp)] = p.dinc(v, s_dve, 1)
                        for tt in range(4):
                            for hf in range(2):
                                p.wt(v, tk[("op", g, tt, hf)])
                                p.wt(v, tk[("xl", g, tt)])
                                if hf == 0 and (g, tt) != (0, 0):
                                    pg, ptt = (g, tt - 1) if tt else (g - 1, 3)
                                    p.wt(v, tk[("x1", pg, ptt)])
                                v.tensor_add(ax1[:, hf * 512:(hf + 1) * 512],
                                             ps_sp[:, :],
                                             axb[:, hf * 512:(hf + 1) * 512])
                                tk[("oa", g, tt, hf)] = p.dinc(v, s_dve, 1)

                p.run_phase({"gpsimd": f_gp, "sync": f_sync, "tensor": f_tensor,
                             "scalar": f_scalar, "vector": f_vector}, tk)

        if dbg and dbg_stage == 4:
            def f_scalar_d4(a):
                p.wait_all(a)
                tkk = None
                for tt in range(NT):
                    ii = a.dma_start(out=x1_dbg[tt * 128:(tt + 1) * 128, :],
                                     in_=x1h[tt * 128:(tt + 1) * 128, :])
                    tkk = p.inc(ii, s_sc, 16)
                a.wait_ge(tkk[0], tkk[1])
            p.run_phase({"scalar": f_scalar_d4})

        if dbg_stage >= 5:
            with nc.sbuf_tensor("wr8", [8, 1024], BF16) as wr8:
                tk = TokDict()

                def f_gp(g):
                    p.wait_all(g)
                    ii = g.dma_start(out=wr8[:, :], in_=wr[:, :])
                    tk["w"] = p.inc(ii, s_once, 16)

                def f_tensor(t):
                    p.wait_all(t)
                    p.wt(t, tk["w"])
                    for r in range(8):
                        ii = t.transpose(ps_tr[:, r * 8:(r + 1) * 8],
                                         wr8[0:8, r * 128:(r + 1) * 128], id_sb[0:8, 0:8])
                    tk["tr"] = p.inc(ii, s_pe, 1)

                def f_vector(v):
                    p.wait_all(v)
                    p.wt(v, tk["tr"])
                    v.tensor_copy(wr_t[:, :], ps_tr[:, 0:64])
                    p.dinc(v, s_dve, 1)

                p.run_phase({"gpsimd": f_gp, "tensor": f_tensor, "vector": f_vector}, tk)

            ln_phase(x1h, A, do_logits=True)

            def gates_phase():
                tk = TokDict()

                def f_vector(v):
                    p.wait_all(v)
                    v.tensor_sub(stats[:, 64:96], stats[:, 32:64], stats[:, 0:32])
                    v.tensor_sub(stats[:, 96:128], stats[:, 0:32], stats[:, 32:64])
                    tk["d"] = p.dinc(v, s_dve, 1)
                    p.wt(v, tk["exp"])
                    v.tensor_single_scalar(stats[:, 64:128], stats[:, 128:192], 1.0, ALU.add)
                    v.drain()
                    v.reciprocal(stats[:, 128:192], stats[:, 64:128])
                    v.drain()
                    p.inc(v.engine_nop(), s_dve, 1)
                    for tt in range(NT):
                        gcol = gates[:, tt * 8:(tt + 1) * 8]
                        v.scalar_tensor_tensor(gcol, eqs[:, 256 + tt * 8:256 + tt * 8 + 8],
                                               stats[:, 160 + tt:160 + tt + 1], zero8[:, :],
                                               ALU.mult, ALU.add)
                        v.drain()
                        ii = v.scalar_tensor_tensor(gcol, eqs[:, tt * 8:tt * 8 + 8],
                                                    stats[:, 128 + tt:128 + tt + 1], gcol,
                                                    ALU.mult, ALU.add)
                        p.inc(ii, s_dve, 1)

                def f_scalar(a):
                    p.wait_all(a)
                    p.wt(a, tk["d"])
                    a.activation(stats[:, 128:192], stats[:, 64:128], AF.Exp)
                    tk["exp"] = p.dinc(a, s_act, 1)

                p.run_phase({"vector": f_vector, "scalar": f_scalar}, tk)

            gates_phase()

        if dbg and dbg_stage == 5:
            def f_scalar_d5(a):
                p.wait_all(a)
                ii = a.dma_start(out=g_dbg[:, :], in_=gates[:, :])
                tkk = p.inc(ii, s_sc, 16)
                a.wait_ge(tkk[0], tkk[1])
            p.run_phase({"scalar": f_scalar_d5})

        if dbg_stage >= 6:
            with (
                nc.sbuf_tensor("w1t", [128, 8192], BF16) as w1t,
                nc.sbuf_tensor("w2t", [128, 8192], BF16) as w2t,
                nc.sbuf_tensor("hsb", [128, 16384], BF16) as hsb,
            ):
                tk = TokDict()
                ES = [(e, sl) for e in range(NE) for sl in range(8)]

                def f_vector(v):
                    p.wait_all(v)
                    v.memset(Bq[:, :], 0.0)
                    tk["yz"] = p.dinc(v, s_dve, 1)
                    for (e, sl) in ES:
                        for tg in range(8):
                            for tt in range(4):
                                for hf in range(2):
                                    yi = (tg * 4 + tt) * 2 + hf
                                    p.wt(v, tk[("y", e, sl, tg, tt, hf)])
                                    gidx = (tg * 4 + tt) * 8 + e
                                    ycol = (tg * 4 + tt) * 1024 + hf * 512
                                    yv = Bq[:, ycol:ycol + 512]
                                    ii = v.scalar_tensor_tensor(
                                        yv, ps_c[:, (yi % 2) * 512:(yi % 2) * 512 + 512],
                                        gates[:, gidx:gidx + 1], yv, ALU.mult, ALU.add)
                                    tk[("stt", e, sl, yi)] = p.inc(ii, s_dve, 1)

                def f_sync(sy):
                    p.wait_all(sy)
                    for (e, sl) in ES:
                        par = sl % 2
                        if sl == 0:
                            sy.wait_ge(s_stage[2 * e], 16)
                        if (e, sl) not in ((0, 0), (0, 1)):
                            pe_, psl = (e, sl - 2) if sl >= 2 else (e - 1, sl + 6)
                            p.wt(sy, tk[("y", pe_, psl, 7, 3, 1)])
                        for r in range(8):
                            ii = sy.dma_start(
                                out=w1t[:, par * 4096 + r * 512:par * 4096 + (r + 1) * 512],
                                in_=bass.AP(w1bf.tensor, e * DFF * E + sl * 512 * E + r * 128,
                                            [[E, 512], [1, 128]]),
                                transpose=True)
                            tk[("w1", e, sl)] = p.inc(ii, s_w1t2[par], 16)
                        if sl == 0:
                            sy.wait_ge(s_stage[2 * e + 1], 16)
                        for fs in range(4):
                            ii = sy.dma_start(
                                out=w2t[:, par * 4096 + fs * 1024:par * 4096 + (fs + 1) * 1024],
                                in_=bass.AP(w2bf.tensor, e * E * DFF + sl * 512 + fs * 128,
                                            [[DFF, 1024], [1, 128]]),
                                transpose=True)
                            tk[("w2", e, sl)] = p.inc(ii, s_w2t2[par], 16)

                def f_tensor(t):
                    p.wait_all(t)
                    p.wt(t, tk["yz"])
                    hidx = 0
                    for (e, sl) in ES:
                        par = sl % 2
                        p.wt(t, tk[("w1", e, sl)])
                        for tg in range(8):
                            for fs in range(4):
                                bsel = (hidx // 2) % 2
                                psh = ps_a if bsel == 0 else ps_b
                                half = (hidx % 2) * 512
                                if hidx >= 4:
                                    p.wt(t, tk[("si", (hidx - 4) // 2)])
                                for r in range(8):
                                    ii = t.matmul(
                                        psh[:, half:half + 512],
                                        w1t[:, par * 4096 + r * 512 + fs * 128:par * 4096 + r * 512 + (fs + 1) * 128],
                                        A[:, r * T + tg * 512:r * T + (tg + 1) * 512],
                                        start=(r == 0), stop=(r == 7))
                                tk[("h", hidx)] = p.inc(ii, s_pe, 1)
                                hidx += 1
                        p.wt(t, tk[("w2", e, sl)])
                        p.wt(t, tk[("si", (hidx - 2) // 2)])
                        for tg in range(8):
                            for tt in range(4):
                                for hf in range(2):
                                    yi = (tg * 4 + tt) * 2 + hf
                                    if (e, sl) != (0, 0) or yi >= 2:
                                        if yi >= 2:
                                            p.wt(t, tk[("stt", e, sl, yi - 2)])
                                        else:
                                            pe_, psl = (e, sl - 1) if sl else (e - 1, 7)
                                            p.wt(t, tk[("stt", pe_, psl, 62 + yi)])
                                    for fs in range(4):
                                        ii = t.matmul(
                                            ps_c[:, (yi % 2) * 512:(yi % 2) * 512 + 512],
                                            hsb[:, fs * 4096 + tg * 512 + tt * 128:fs * 4096 + tg * 512 + (tt + 1) * 128],
                                            w2t[:, par * 4096 + fs * 1024 + hf * 512:par * 4096 + fs * 1024 + (hf + 1) * 512],
                                            start=(fs == 0), stop=(fs == 3))
                                    tk[("y", e, sl, tg, tt, hf)] = p.inc(ii, s_pe, 1)

                def f_scalar(a):
                    p.wait_all(a)
                    hidx = 0
                    for (e, sl) in ES:
                        for tg in range(8):
                            for fs in range(4):
                                if hidx % 2 == 1:
                                    p.wt(a, tk[("h", hidx)])
                                    bsel = (hidx // 2) % 2
                                    psh = ps_a if bsel == 0 else ps_b
                                    dst = bass.AP(hsb, (fs - 1) * 4096 + tg * 512,
                                                  [[16384, 128], [4096, 2], [1, 512]])
                                    ii = a.activation(dst, psh[:, :], AF.Silu)
                                    tk[("si", hidx // 2)] = p.inc(ii, s_act, 1)
                                hidx += 1

                p.run_phase({"vector": f_vector, "sync": f_sync,
                             "tensor": f_tensor, "scalar": f_scalar}, tk)

        if dbg_stage >= 7:
            with (
                nc.sbuf_tensor("fx", [128, 2048], F32) as fx,
                nc.sbuf_tensor("fo", [128, 2048], F32) as fo,
            ):
                tk = TokDict()

                def f_sync(sy):
                    p.wait_all(sy)
                    for tt in range(NT):
                        if tt >= 2:
                            p.wt(sy, tk[("add", tt - 2)])
                        ii = sy.dma_start(out=fx[:, (tt % 2) * 1024:(tt % 2) * 1024 + 1024],
                                          in_=x1h[tt * 128:(tt + 1) * 128, :])
                        tk[("l", tt)] = p.inc(ii, s_lx2[tt % 2], 16)

                def f_vector(v):
                    p.wait_all(v)
                    for tt in range(NT):
                        p.wt(v, tk[("l", tt)])
                        if tt >= 2:
                            p.wt(v, tk[("st", tt - 2)])
                        v.tensor_add(fo[:, (tt % 2) * 1024:(tt % 2) * 1024 + 1024],
                                     Bq[:, tt * 1024:(tt + 1) * 1024],
                                     fx[:, (tt % 2) * 1024:(tt % 2) * 1024 + 1024])
                        tk[("add", tt)] = p.dinc(v, s_dve, 1)

                def f_scalar(a):
                    p.wait_all(a)
                    for tt in range(NT):
                        p.wt(a, tk[("add", tt)])
                        ii = a.dma_start(out=out[tt * 128:(tt + 1) * 128, :],
                                         in_=fo[:, (tt % 2) * 1024:(tt % 2) * 1024 + 1024])
                        tk[("st", tt)] = p.inc(ii, s_ost2[tt % 2], 16)
                    a.wait_ge(s_ost2[0], p.tot(s_ost2[0]))
                    a.wait_ge(s_ost2[1], p.tot(s_ost2[1]))
                    for sem in s_stage:
                        if p.tot(sem):
                            a.wait_ge(sem, p.tot(sem))

                p.run_phase({"sync": f_sync, "vector": f_vector, "scalar": f_scalar}, tk)
        else:
            def f_scalar_end(a):
                p.wait_all(a)
                for sem in s_stage:
                    if p.tot(sem):
                        a.wait_ge(sem, p.tot(sem))
            p.run_phase({"scalar": f_scalar_end})

    return nc


def kernel(L0_attn_x, L0_attn_wq, L0_attn_w_down, L0_attn_wk_up, L0_attn_wv_up,
           L0_attn_wo, L0_ffn_W1, L0_ffn_W2, L0_ffn_Wr):
    C, S, swp, ident = _host_consts()
    nc = build_nc(dbg_stage=DBG_STAGE, dbg=DBG)
    x = np.ascontiguousarray(np.asarray(L0_attn_x, dtype=np.float32))
    common = {
        "wq": np.ascontiguousarray(L0_attn_wq, dtype=np.float32),
        "wd": np.ascontiguousarray(L0_attn_w_down, dtype=np.float32),
        "wk": np.ascontiguousarray(L0_attn_wk_up, dtype=np.float32),
        "wv": np.ascontiguousarray(L0_attn_wv_up, dtype=np.float32),
        "wo": np.ascontiguousarray(L0_attn_wo, dtype=np.float32),
        "w1": np.ascontiguousarray(L0_ffn_W1, dtype=np.float32).reshape(NE * DFF, E),
        "w2": np.ascontiguousarray(L0_ffn_W2, dtype=np.float32).reshape(NE * E, DFF),
        "wr": np.ascontiguousarray(L0_ffn_Wr, dtype=np.float32),
        "ropec": C, "ropes": S, "swp": swp, "ident": ident,
    }
    in_maps = [dict(common, x=np.ascontiguousarray(x[b])) for b in range(NCORE)]
    res = run_bass_kernel_spmd(nc, in_maps, core_ids=list(range(NCORE)))
    return np.stack([r["out"] for r in res.results], axis=0)


# version 03
# speedup vs baseline: 1.3088x; 1.2276x over previous
import numpy as np
import ml_dtypes
import concourse.bass as bass
import concourse.mybir as mybir
from concourse.bass_utils import run_bass_kernel_spmd

F32 = mybir.dt.float32
BF16 = mybir.dt.bfloat16
AF = mybir.ActivationFunctionType
ALU = mybir.AluOpType
AX = mybir.AxisListType
BF = ml_dtypes.bfloat16

T, E, NH, D, L = 4096, 1024, 16, 64, 512
NE, DFF = 8, 4096
NT = T // 128
NG = T // 512
EPS = 1e-5
NCORE = 8

DBG_STAGE = 9
DBG = 0


def _host_consts():
    inv = 1.0 / (10000.0 ** (np.arange(0, D, 2, dtype=np.float32) / D))
    ang = np.outer(np.arange(T, dtype=np.float32), inv)
    cos = np.cos(ang).astype(np.float32)
    sin = np.sin(ang).astype(np.float32)
    C = np.zeros((128, T), np.float32)
    S = np.zeros((128, T), np.float32)
    for r in range(128):
        i = r % 64
        pp = i // 2
        C[r] = cos[:, pp]
        S[r] = sin[:, pp] * (1.0 if i % 2 == 1 else -1.0)
    swp = np.zeros((128, 128), BF)
    for r in range(128):
        swp[r ^ 1, r] = 1.0
    ident = np.eye(128, dtype=BF)
    ident32 = np.eye(128, dtype=np.float32)
    iota1 = np.broadcast_to(np.arange(1, 1537, dtype=np.float32)[None, :], (128, 1536)).copy()
    rhs2s = np.zeros((128, 128), BF)
    for tt in range(32):
        rhs2s[:, 4 * tt + 0] = np.arange(128, dtype=np.float32)
        rhs2s[:, 4 * tt + 1] = float(tt)
        rhs2s[:, 4 * tt + 3] = 1.0
    return C, S, swp, ident, ident32, iota1, rhs2s


class _FakeInstr:
    def then_inc(self, sem, k):
        return self


class TokDict(dict):

    def __init__(self):
        super().__init__()
        self.missed = set()

    def __missing__(self, k):
        self.missed.add(k)
        return 0


class _CountEng:
    def wait_ge(self, sem, val):
        pass

    def __getattr__(self, name):
        def f(*a, **kw):
            return _FakeInstr()
        return f


class Prog:

    ORDER = ("sync", "gpsimd", "tensor", "scalar", "vector")

    def __init__(self, nc):
        self.nc = nc
        self.c = {}
        self.snap = {}

    def inc(self, instr, sem, k):
        instr.then_inc(sem, k)
        self.c[sem] = self.c.get(sem, 0) + k
        return (sem, self.c[sem])

    def dinc(self, eng, sem, k):
        d = eng.drain()
        d.then_inc(sem, k)
        self.c[sem] = self.c.get(sem, 0) + k
        return (sem, self.c[sem])

    def tot(self, sem):
        return self.c.get(sem, 0)

    def wt(self, eng, tok):
        if tok:
            sem, val = tok
            if val:
                eng.wait_ge(sem, val)

    def wait_all(self, eng):
        for sem, t in self.snap.items():
            if t:
                eng.wait_ge(sem, t)

    def run_phase(self, streams, tk=None):
        self.snap = dict(self.c)
        start = dict(self.c)
        for name in self.ORDER:
            if name in streams:
                streams[name](_CountEng())
        if tk is not None:
            unresolved = tk.missed - set(tk.keys())
            assert not unresolved, f"unresolved tokens: {unresolved}"
        end = dict(self.c)
        self.c = dict(start)
        with self.nc.Block() as blk:
            for name in self.ORDER:
                if name in streams:
                    getattr(blk, name)(streams[name])
        assert self.c == end, "two-pass mismatch"


def build_nc(dbg_stage=9, dbg=0):
    nc = bass.Bass("TRN2", target_bir_lowering=False)
    _uc = [0]

    def un(s):
        _uc[0] += 1
        return f"{s}_{_uc[0]}"

    x_in = nc.dram_tensor("x", [T, E], F32, kind="ExternalInput").ap()
    wq = nc.dram_tensor("wq", [E, E], F32, kind="ExternalInput").ap()
    wd = nc.dram_tensor("wd", [E, L], F32, kind="ExternalInput").ap()
    wk = nc.dram_tensor("wk", [L, E], F32, kind="ExternalInput").ap()
    wv = nc.dram_tensor("wv", [L, E], F32, kind="ExternalInput").ap()
    wo = nc.dram_tensor("wo", [E, E], F32, kind="ExternalInput").ap()
    w1 = nc.dram_tensor("w1", [NE * DFF, E], F32, kind="ExternalInput").ap()
    w2 = nc.dram_tensor("w2", [NE * E, DFF], F32, kind="ExternalInput").ap()
    wr = nc.dram_tensor("wr", [NE, E], F32, kind="ExternalInput").ap()
    ropec = nc.dram_tensor("ropec", [128, T], F32, kind="ExternalInput").ap()
    ropes = nc.dram_tensor("ropes", [128, T], F32, kind="ExternalInput").ap()
    swp_in = nc.dram_tensor("swp", [128, 128], BF16, kind="ExternalInput").ap()
    id_in = nc.dram_tensor("ident", [128, 128], BF16, kind="ExternalInput").ap()
    id32_in = nc.dram_tensor("ident32", [128, 128], F32, kind="ExternalInput").ap()
    iota_in = nc.dram_tensor("iota1", [128, 1536], F32, kind="ExternalInput").ap()
    rhs2s_in = nc.dram_tensor("rhs2s", [128, 128], BF16, kind="ExternalInput").ap()

    out = nc.dram_tensor("out", [T, E], F32, kind="ExternalOutput").ap()
    if dbg:
        q_dbg = nc.dram_tensor("q_dbg", [128, 8 * T], BF16, kind="ExternalOutput").ap()
        k_dbg = nc.dram_tensor("k_dbg", [128, 8 * T], BF16, kind="ExternalOutput").ap()
        x1_dbg = nc.dram_tensor("x1_dbg", [T, E], F32, kind="ExternalOutput").ap()
        g_dbg = nc.dram_tensor("g_dbg", [128, 256], F32, kind="ExternalOutput").ap()
        v_dbg = nc.dram_tensor("v_dbg", [T, 1040], BF16, kind="ExternalOutput").ap()

    w1bf = nc.dram_tensor("w1bf", [NE * DFF, E], BF16).ap()
    w2bf = nc.dram_tensor("w2bf", [NE * E, DFF], BF16).ap()
    x1h = nc.dram_tensor("x1h", [T, E], F32).ap()
    vh = nc.dram_tensor("vh", [T, 1040], BF16).ap()
    xn2h = nc.dram_tensor("xn2h", [T, E], BF16).ap()
    accB = nc.dram_tensor("accB", [T, E], BF16).ap()

    from contextlib import ExitStack
    with ExitStack() as _st:
        en = _st.enter_context
        A = en(nc.sbuf_tensor("A", [128, 8 * T], BF16))
        Bq = en(nc.sbuf_tensor("Bq", [128, 8 * T], BF16))
        ones128 = en(nc.sbuf_tensor("ones128", [128, 128], BF16))
        swp_sb = en(nc.sbuf_tensor("swp_sb", [128, 128], BF16))
        id_sb = en(nc.sbuf_tensor("id_sb", [128, 128], BF16))
        stats = en(nc.sbuf_tensor("stats", [128, 192], F32))
        gates = en(nc.sbuf_tensor("gates", [128, 256], F32))
        zero8 = en(nc.sbuf_tensor("zero8", [128, 8], F32))
        epsc = en(nc.sbuf_tensor("epsc", [128, 1], F32))
        eqs = en(nc.sbuf_tensor("eqs", [128, 512], F32))
        wr_t = en(nc.sbuf_tensor("wr_t", [128, 64], BF16))
        lists_keep = en(nc.sbuf_tensor("lists_keep", [128, 384], F32))
        idxi_keep = en(nc.sbuf_tensor("idxi_keep", [128, 96], mybir.dt.int32))
        ps_a = en(nc.psum_tensor("ps_a", [128, 1024], F32))
        ps_b = en(nc.psum_tensor("ps_b", [128, 1024], F32))
        ps_c = en(nc.psum_tensor("ps_c", [128, 1024], F32))
        ps_sp = en(nc.psum_tensor("ps_sp", [128, 512], F32))
        ps_tr = en(nc.psum_tensor("ps_tr", [128, 1024], BF16))
        s_pe = nc.alloc_semaphore("s_pe")
        s_act = nc.alloc_semaphore("s_act")
        s_dve = nc.alloc_semaphore("s_dve")
        s_gp = nc.alloc_semaphore("s_gp")
        s_once = nc.alloc_semaphore("s_once")
        s_sc = nc.alloc_semaphore("s_sc")
        s_mv = nc.alloc_semaphore("s_mv")
        s_ax = nc.alloc_semaphore("s_ax")
        s_lx2 = [nc.alloc_semaphore(f"s_lx{i}") for i in range(2)]
        s_rp2 = [nc.alloc_semaphore(f"s_rp{i}") for i in range(2)]
        s_wsl2 = [nc.alloc_semaphore(f"s_wsl{i}") for i in range(2)]
        s_v8 = [nc.alloc_semaphore(f"s_v{i}") for i in range(8)]
        s_vst2 = [nc.alloc_semaphore(f"s_vst{i}") for i in range(2)]
        s_ost2 = [nc.alloc_semaphore(f"s_ost{i}") for i in range(2)]
        s_w1t2 = [nc.alloc_semaphore(f"s_w1t{i}") for i in range(2)]
        s_w2t2 = [nc.alloc_semaphore(f"s_w2t{i}") for i in range(2)]
        s_stage = [nc.alloc_semaphore(f"s_stage{i}") for i in range(2 * NE)]
        s_x2st2 = [nc.alloc_semaphore(f"s_x2st{i}") for i in range(2)]
        s_gt2 = [nc.alloc_semaphore(f"s_gt{i}") for i in range(2)]
        s_sct = nc.alloc_semaphore("s_sct")

        p = Prog(nc)

        def ph0():
            def f_sync(sy):
                ii = sy.dma_start(out=swp_sb[:, :], in_=swp_in[:, :])
                p.inc(ii, s_once, 16)
                ii = sy.dma_start(out=id_sb[:, :], in_=id_in[:, :])
                p.inc(ii, s_once, 16)

            def f_gp(g):
                ii = g.memset(ones128[:, :], 1.0)
                p.inc(ii, s_gp, 1)
                ii = g.memset(zero8[:, :], 0.0)
                p.inc(ii, s_gp, 1)
                ii = g.memset(epsc[:, :], EPS)
                p.inc(ii, s_gp, 1)

            p.run_phase({"sync": f_sync, "gpsimd": f_gp})

        ph0()

        def ln_phase(src_dram, dst_sb, do_logits, store_tm=False):
            with (
                nc.sbuf_tensor(un("lx"), [128, 2048], F32) as lx,
                nc.sbuf_tensor(un("lxn"), [128, 2048], BF16) as _lxn,
                nc.sbuf_tensor(un("lsq"), [128, 1024], BF16) as lsq,
                nc.sbuf_tensor(un("llg"), [128, 16], F32) as llg,
                nc.sbuf_tensor(un("lmk"), [128, 16], F32) as lmk,
            ):
                lxn = dbg_lxn if (dbg and dbg_stage == 1 and dbg_lxn is not None) else _lxn
                if dbg and dbg_stage == 1 and dbg_lx is not None:
                    lx = dbg_lx
                tk = TokDict()

                def f_sync(sy):
                    p.wait_all(sy)
                    for tt in range(NT):
                        if tt >= 2:
                            p.wt(sy, tk[("ap", tt - 2)])
                            p.wt(sy, tk[("sum", tt - 2)])
                        ii = sy.dma_start(
                            out=lx[:, (tt % 2) * 1024:(tt % 2) * 1024 + 1024],
                            in_=src_dram[tt * 128:(tt + 1) * 128, :])
                        tk[("ld", tt)] = p.inc(ii, s_lx2[tt % 2], 16)

                def f_vector(v):
                    p.wait_all(v)
                    for tt in range(NT):
                        xv = lx[:, (tt % 2) * 1024:(tt % 2) * 1024 + 1024]
                        p.wt(v, tk[("ld", tt)])
                        v.reduce_sum(stats[:, tt:tt + 1], xv, axis=AX.X)
                        tk[("sum", tt)] = p.dinc(v, s_dve, 1)
                        p.wt(v, tk[("sq", tt)])
                        mcol = stats[:, 64 + tt:64 + tt + 1]
                        v.tensor_single_scalar(mcol, stats[:, tt:tt + 1], 1.0 / E, ALU.mult)
                        v.drain()
                        v.tensor_mul(stats[:, tt:tt + 1], mcol, mcol)
                        v.drain()
                        v.scalar_tensor_tensor(
                            stats[:, 160 + tt:160 + tt + 1],
                            stats[:, 32 + tt:32 + tt + 1], 1.0 / E,
                            stats[:, tt:tt + 1], ALU.mult, ALU.subtract)
                        tk[("var", tt)] = p.dinc(v, s_dve, 1)
                        p.wt(v, tk[("rstd", tt)])
                        v.scalar_tensor_tensor(
                            stats[:, 128 + tt:128 + tt + 1], mcol, -1.0,
                            stats[:, 96 + tt:96 + tt + 1], ALU.mult, ALU.mult)
                        tk[("nb", tt)] = p.dinc(v, s_dve, 1)
                        p.wt(v, tk[("tr", tt)])
                        src = bass.AP(ps_tr, 0, [[1024, 128], [128, 8], [1, 128]])
                        dst = bass.AP(dst_sb, tt * 128, [[8 * T, 128], [T, 8], [1, 128]])
                        v.tensor_copy(dst, src)
                        tk[("ev", tt)] = p.dinc(v, s_dve, 1)
                        if do_logits:
                            p.wt(v, tk[("lg", tt)])
                            v.tensor_copy(llg[:, 0:8], ps_sp[:, 0:8])
                            v.drain()
                            m1 = stats[:, tt:tt + 1]
                            m2 = stats[:, 32 + tt:32 + tt + 1]
                            v.reduce_max(m1, llg[:, 0:8], axis=AX.X)
                            v.drain()
                            e1 = eqs[:, tt * 8:tt * 8 + 8]
                            v.tensor_single_scalar(e1, llg[:, 0:8], m1, ALU.is_equal)
                            v.drain()
                            v.scalar_tensor_tensor(lmk[:, 0:8], e1, -1.0e30, llg[:, 0:8],
                                                   ALU.mult, ALU.add)
                            v.drain()
                            v.reduce_max(m2, lmk[:, 0:8], axis=AX.X)
                            v.drain()
                            e2 = eqs[:, 256 + tt * 8:256 + tt * 8 + 8]
                            ii = v.tensor_single_scalar(e2, llg[:, 0:8], m2, ALU.is_equal)
                            tk[("rt", tt)] = p.inc(ii, s_dve, 1)

                def f_scalar(a):
                    p.wait_all(a)
                    for tt in range(NT):
                        xv = lx[:, (tt % 2) * 1024:(tt % 2) * 1024 + 1024]
                        p.wt(a, tk[("ld", tt)])
                        a.activation(lsq[:, :], xv, AF.Square,
                                     accum_out=stats[:, 32 + tt:32 + tt + 1])
                        tk[("sq", tt)] = p.dinc(a, s_act, 1)
                        p.wt(a, tk[("var", tt)])
                        a.activation(stats[:, 32 + tt:32 + tt + 1],
                                     stats[:, 160 + tt:160 + tt + 1], AF.Ln, bias=epsc[:, :])
                        a.drain()
                        a.activation(stats[:, 96 + tt:96 + tt + 1],
                                     stats[:, 32 + tt:32 + tt + 1], AF.Exp, scale=-0.5)
                        tk[("rstd", tt)] = p.dinc(a, s_act, 1)
                        p.wt(a, tk[("nb", tt)])
                        if tt >= 2:
                            p.wt(a, tk[("tr", tt - 2)])
                            if store_tm:
                                p.wt(a, tk[("tmst", tt - 2)])
                        a.activation(lxn[:, (tt % 2) * 1024:(tt % 2) * 1024 + 1024],
                                     xv, AF.Identity,
                                     bias=stats[:, 128 + tt:128 + tt + 1],
                                     scale=stats[:, 96 + tt:96 + tt + 1])
                        tk[("ap", tt)] = p.dinc(a, s_act, 1)
                        if store_tm:
                            ii = a.dma_start(out=xn2h[tt * 128:(tt + 1) * 128, :],
                                             in_=lxn[:, (tt % 2) * 1024:(tt % 2) * 1024 + 1024])
                            tk[("tmst", tt)] = p.inc(ii, s_x2st2[tt % 2], 16)

                def f_tensor(t):
                    p.wait_all(t)
                    for tt in range(NT):
                        p.wt(t, tk[("ap", tt)])
                        if tt >= 1:
                            p.wt(t, tk[("ev", tt - 1)])
                        for c in range(8):
                            ii = t.transpose(
                                ps_tr[:, c * 128:(c + 1) * 128],
                                lxn[:, (tt % 2) * 1024 + c * 128:(tt % 2) * 1024 + (c + 1) * 128],
                                id_sb[:, :])
                        tk[("tr", tt)] = p.inc(ii, s_pe, 1)
                        if do_logits:
                            p.wt(t, tk[("ev", tt)])
                            if tt >= 1:
                                p.wt(t, tk[("rt", tt - 1)])
                            for r in range(8):
                                ii = t.matmul(
                                    ps_sp[:, 0:8],
                                    dst_sb[:, r * T + tt * 128:r * T + (tt + 1) * 128],
                                    wr_t[:, r * 8:(r + 1) * 8],
                                    start=(r == 0), stop=(r == 7))
                            tk[("lg", tt)] = p.inc(ii, s_pe, 1)

                p.run_phase({"sync": f_sync, "vector": f_vector,
                             "scalar": f_scalar, "tensor": f_tensor}, tk)

        def proj_rope_phase(wmat, wrows, nacc, src_sb, dst_sb):
            with (
                nc.sbuf_tensor(un("pw"), [128, 2 * nacc * 128], BF16) as pw,
                nc.sbuf_tensor(un("prc"), [128, 1024], F32) as prc,
                nc.sbuf_tensor(un("prs"), [128, 1024], F32) as prs,
                nc.sbuf_tensor(un("pqt"), [128, 1024], BF16) as pqt,
                nc.sbuf_tensor(un("pt1"), [128, 1024], BF16) as pt1,
            ):
                tk = TokDict()

                def f_gp(g):
                    p.wait_all(g)
                    for c in range(8):
                        if c >= 2:
                            p.wt(g, tk[("mm", c - 2, 7)])
                        half = (c % 2) * nacc * 128
                        dst = bass.AP(pw, half, [[2 * nacc * 128, 128], [128, nacc], [1, 128]])
                        src = bass.AP(wmat.tensor, c * 128, [[E, 128], [128 * E, nacc], [1, 128]])
                        ii = g.dma_start(out=dst, in_=src)
                        tk[("w", c)] = p.inc(ii, s_wsl2[c % 2], 16)

                def f_sync(sy):
                    p.wait_all(sy)
                    for c in range(8):
                        for tg in range(8):
                            i2 = c * 8 + tg
                            par = i2 % 2
                            if i2 >= 2:
                                pc, ptg = (i2 - 2) // 8, (i2 - 2) % 8
                                p.wt(sy, tk[("out", pc, ptg)])
                            ii = sy.dma_start(out=prc[:, par * 512:par * 512 + 512],
                                              in_=ropec[:, tg * 512:(tg + 1) * 512])
                            p.inc(ii, s_rp2[par], 16)
                            ii = sy.dma_start(out=prs[:, par * 512:par * 512 + 512],
                                              in_=ropes[:, tg * 512:(tg + 1) * 512])
                            tk[("rl", c, tg)] = p.inc(ii, s_rp2[par], 16)

                def f_tensor(t):
                    p.wait_all(t)
                    for c in range(8):
                        p.wt(t, tk[("w", c)])
                        for tg in range(8):
                            i2 = c * 8 + tg
                            par = i2 % 2
                            if i2 >= 2:
                                pc, ptg = (i2 - 2) // 8, (i2 - 2) % 8
                                p.wt(t, tk[("ev", pc, ptg)])
                            for r in range(nacc):
                                ii = t.matmul(
                                    ps_a[:, par * 512:par * 512 + 512],
                                    pw[:, (c % 2) * nacc * 128 + r * 128:(c % 2) * nacc * 128 + (r + 1) * 128],
                                    src_sb[:, r * T + tg * 512:r * T + (tg + 1) * 512],
                                    start=(r == 0), stop=(r == nacc - 1))
                            tk[("mm", c, tg)] = p.inc(ii, s_pe, 1)
                            p.wt(t, tk[("ev", c, tg)])
                            if i2 >= 2:
                                pc, ptg = (i2 - 2) // 8, (i2 - 2) % 8
                                p.wt(t, tk[("out", pc, ptg)])
                            ii = t.matmul(ps_b[:, par * 512:par * 512 + 512],
                                          swp_sb[:, :],
                                          pqt[:, par * 512:par * 512 + 512],
                                          start=True, stop=True)
                            tk[("sw", c, tg)] = p.inc(ii, s_pe, 1)

                def f_scalar(a):
                    p.wait_all(a)
                    for c in range(8):
                        for tg in range(8):
                            i2 = c * 8 + tg
                            par = i2 % 2
                            p.wt(a, tk[("mm", c, tg)])
                            if i2 >= 2:
                                pc, ptg = (i2 - 2) // 8, (i2 - 2) % 8
                                p.wt(a, tk[("sw", pc, ptg)])
                                p.wt(a, tk[("p1", pc, ptg)])
                            a.activation(pqt[:, par * 512:par * 512 + 512],
                                         ps_a[:, par * 512:par * 512 + 512], AF.Copy)
                            tk[("ev", c, tg)] = p.dinc(a, s_act, 1)

                def f_vector(v):
                    p.wait_all(v)
                    for c in range(8):
                        for tg in range(8):
                            i2 = c * 8 + tg
                            par = i2 % 2
                            p.wt(v, tk[("rl", c, tg)])
                            p.wt(v, tk[("ev", c, tg)])
                            v.tensor_mul(pt1[:, par * 512:par * 512 + 512],
                                         pqt[:, par * 512:par * 512 + 512],
                                         prc[:, par * 512:par * 512 + 512])
                            tk[("p1", c, tg)] = p.dinc(v, s_dve, 1)
                            p.wt(v, tk[("sw", c, tg)])
                            dcol = dst_sb[:, c * T + tg * 512:c * T + (tg + 1) * 512]
                            v.tensor_mul(dcol, ps_b[:, par * 512:par * 512 + 512],
                                         prs[:, par * 512:par * 512 + 512])
                            v.drain()
                            ii = v.tensor_add(dcol, pt1[:, par * 512:par * 512 + 512], dcol)
                            tk[("out", c, tg)] = p.inc(ii, s_dve, 1)

                p.run_phase({"gpsimd": f_gp, "sync": f_sync, "tensor": f_tensor,
                             "scalar": f_scalar, "vector": f_vector}, tk)

        dbg_lxn = en(nc.sbuf_tensor("dbg_lxn", [128, 2048], BF16)) if (dbg and dbg_stage == 1) else None
        dbg_lx = en(nc.sbuf_tensor("dbg_lx", [128, 2048], F32)) if (dbg and dbg_stage == 1) else None
        ln_phase(x_in, A, do_logits=False)
        if dbg_stage >= 2:
            proj_rope_phase(wq, E, 8, A, Bq)

        if dbg and dbg_stage == 1:
            def f_sync_d1(sy):
                p.wait_all(sy)
                tkk = None
                for c in range(8):
                    ii = sy.dma_start(out=k_dbg[:, c * T:(c + 1) * T], in_=A[:, c * T:(c + 1) * T])
                    tkk = p.inc(ii, s_sc, 16)
                ii = sy.dma_start(out=x1_dbg[0:128, 0:192], in_=stats[:, :])
                tkk = p.inc(ii, s_sc, 16)
                ii = sy.dma_start(out=q_dbg[:, 0:2048], in_=dbg_lxn[:, :])
                tkk = p.inc(ii, s_sc, 16)
                ii = sy.dma_start(out=x1_dbg[128:256, :], in_=dbg_lx[:, 0:1024])
                tkk = p.inc(ii, s_sc, 16)
                ii = sy.dma_start(out=x1_dbg[256:384, :], in_=dbg_lx[:, 1024:2048])
                tkk = p.inc(ii, s_sc, 16)
                sy.wait_ge(tkk[0], tkk[1])
            p.run_phase({"sync": f_sync_d1})

        if dbg and dbg_stage == 2:
            def f_scalar_d2(a):
                p.wait_all(a)
                tkk = None
                for c in range(8):
                    ii = a.dma_start(out=k_dbg[:, c * T:(c + 1) * T], in_=A[:, c * T:(c + 1) * T])
                    p.inc(ii, s_sc, 16)
                    ii = a.dma_start(out=q_dbg[:, c * T:(c + 1) * T], in_=Bq[:, c * T:(c + 1) * T])
                    tkk = p.inc(ii, s_sc, 16)
                a.wait_ge(tkk[0], tkk[1])
            p.run_phase({"scalar": f_scalar_d2})

        if dbg_stage >= 3:
            with nc.sbuf_tensor("latf", [128, 4 * T], BF16) as latf:
                with nc.sbuf_tensor("dw", [128, 2048], BF16) as dw:
                    tk = TokDict()

                    def f_gp(g):
                        p.wait_all(g)
                        for lc in range(4):
                            if lc >= 2:
                                p.wt(g, tk[("mm", lc - 2, 7)])
                            half = (lc % 2) * 1024
                            dst = bass.AP(dw, half, [[2048, 128], [128, 8], [1, 128]])
                            src = bass.AP(wd.tensor, lc * 128, [[L, 128], [128 * L, 8], [1, 128]])
                            ii = g.dma_start(out=dst, in_=src)
                            tk[("w", lc)] = p.inc(ii, s_wsl2[lc % 2], 16)

                    def f_tensor(t):
                        p.wait_all(t)
                        for lc in range(4):
                            p.wt(t, tk[("w", lc)])
                            for tg in range(8):
                                i2 = lc * 8 + tg
                                par = i2 % 2
                                if i2 >= 2:
                                    pl, ptg = (i2 - 2) // 8, (i2 - 2) % 8
                                    p.wt(t, tk[("ev", pl, ptg)])
                                for r in range(8):
                                    ii = t.matmul(
                                        ps_a[:, par * 512:par * 512 + 512],
                                        dw[:, (lc % 2) * 1024 + r * 128:(lc % 2) * 1024 + (r + 1) * 128],
                                        A[:, r * T + tg * 512:r * T + (tg + 1) * 512],
                                        start=(r == 0), stop=(r == 7))
                                tk[("mm", lc, tg)] = p.inc(ii, s_pe, 1)

                    def f_scalar(a):
                        p.wait_all(a)
                        for lc in range(4):
                            for tg in range(8):
                                par = (lc * 8 + tg) % 2
                                p.wt(a, tk[("mm", lc, tg)])
                                a.activation(
                                    latf[:, lc * T + tg * 512:lc * T + (tg + 1) * 512],
                                    ps_a[:, par * 512:par * 512 + 512], AF.Copy)
                                tk[("ev", lc, tg)] = p.dinc(a, s_act, 1)

                    p.run_phase({"gpsimd": f_gp, "tensor": f_tensor, "scalar": f_scalar}, tk)

                proj_rope_phase(wk, L, 4, latf, A)

                with (
                    nc.sbuf_tensor("vw", [128, 4096], BF16) as vw,
                    nc.sbuf_tensor("vsb", [128, 2080], BF16) as vsb,
                ):
                    tk = TokDict()

                    def f_gp(g):
                        p.wait_all(g)
                        dst = bass.AP(vw, 0, [[4096, 128], [1024, 4], [1, 1024]])
                        src = bass.AP(wv.tensor, 0, [[E, 128], [128 * E, 4], [1, 1024]])
                        ii = g.dma_start(out=dst, in_=src)
                        tk["w"] = p.inc(ii, s_once, 16)
                        ii = g.memset(bass.AP(vsb, 64, [[2080, 128], [65, 16]]), 1.0)
                        p.inc(ii, s_gp, 1)
                        ii = g.memset(bass.AP(vsb, 1040 + 64, [[2080, 128], [65, 16]]), 1.0)
                        tk["ones"] = p.inc(ii, s_gp, 1)

                    def f_tensor(t):
                        p.wait_all(t)
                        p.wt(t, tk["w"])
                        p.wt(t, tk["ones"])
                        for tt in range(NT):
                            for hf in range(2):
                                i2 = tt * 2 + hf
                                par = i2 % 2
                                if i2 >= 2:
                                    pt_, pf = (i2 - 2) // 2, (i2 - 2) % 2
                                    p.wt(t, tk[("ev", pt_, pf)])
                                for r in range(4):
                                    ii = t.matmul(
                                        ps_a[:, par * 512:par * 512 + 512],
                                        latf[:, r * T + tt * 128:r * T + (tt + 1) * 128],
                                        vw[:, r * 1024 + hf * 512:r * 1024 + (hf + 1) * 512],
                                        start=(r == 0), stop=(r == 3))
                                tk[("mm", tt, hf)] = p.inc(ii, s_pe, 1)

                    def f_vector(v):
                        p.wait_all(v)
                        for tt in range(NT):
                            for hf in range(2):
                                par = (tt * 2 + hf) % 2
                                p.wt(v, tk[("mm", tt, hf)])
                                if tt >= 2 and hf == 0:
                                    p.wt(v, tk[("st", tt - 2)])
                                dst = bass.AP(vsb, (tt % 2) * 1040 + hf * 520,
                                              [[2080, 128], [130, 4], [65, 2], [1, 64]])
                                src = bass.AP(ps_a, par * 512,
                                              [[1024, 128], [128, 4], [64, 2], [1, 64]])
                                v.tensor_copy(dst, src)
                                tk[("ev", tt, hf)] = p.dinc(v, s_dve, 1)

                    def f_scalar(a):
                        p.wait_all(a)
                        for tt in range(NT):
                            p.wt(a, tk[("ev", tt, 1)])
                            ii = a.dma_start(out=vh[tt * 128:(tt + 1) * 128, :],
                                             in_=vsb[:, (tt % 2) * 1040:(tt % 2) * 1040 + 1040])
                            tk[("st", tt)] = p.inc(ii, s_vst2[tt % 2], 16)

                    p.run_phase({"gpsimd": f_gp, "tensor": f_tensor,
                                 "vector": f_vector, "scalar": f_scalar}, tk)

        if dbg and dbg_stage == 3:
            def f_scalar_d3(a):
                p.wait_all(a)
                tkk = None
                for c in range(8):
                    ii = a.dma_start(out=q_dbg[:, c * T:(c + 1) * T], in_=Bq[:, c * T:(c + 1) * T])
                    p.inc(ii, s_sc, 16)
                    ii = a.dma_start(out=k_dbg[:, c * T:(c + 1) * T], in_=A[:, c * T:(c + 1) * T])
                    p.inc(ii, s_sc, 16)
                for tt in range(NT):
                    ii = a.dma_start(out=v_dbg[tt * 128:(tt + 1) * 128, :],
                                     in_=vh[tt * 128:(tt + 1) * 128, :])
                    tkk = p.inc(ii, s_sc, 16)
                a.wait_ge(tkk[0], tkk[1])
            p.run_phase({"scalar": f_scalar_d3})

        if dbg_stage >= 4:
            with (
                nc.sbuf_tensor("wo_sb", [128, 8192], BF16) as wo_sb,
                nc.sbuf_tensor("vst", [128, 8 * 130], BF16) as vst,
                nc.sbuf_tensor("pb", [128, 2048], BF16) as pb,
                nc.sbuf_tensor("ao", [128, 4096], BF16) as ao,
                nc.sbuf_tensor("sinv", [128, 1024], BF16) as sinv,
                nc.sbuf_tensor("bcb", [128, 512], BF16) as bcb,
                nc.sbuf_tensor("st64", [128, 512], BF16) as st64,
                nc.sbuf_tensor("axb", [128, 1024], F32) as axb,
                nc.sbuf_tensor("ax1", [128, 1024], F32) as ax1,
            ):
                tk = TokDict()
                sched = []
                base_of = {}
                gi = 0
                for g in range(NG):
                    ntiles = 4 * (g + 1)
                    for hp in range(8):
                        base_of[(g, hp)] = gi
                        for i in range(ntiles):
                            sched.append((g, hp, i, gi, ntiles))
                            gi += 1

                def prev_gh(g, hp):
                    return (g, hp - 1) if hp else (g - 1, 7)

                def f_gp(g_):
                    p.wait_all(g_)
                    dst = bass.AP(wo_sb, 0, [[8192, 128], [1024, 8], [1, 1024]])
                    src = bass.AP(wo.tensor, 0, [[E, 128], [128 * E, 8], [1, 1024]])
                    ii = g_.dma_start(out=dst, in_=src)
                    tk["wo"] = p.inc(ii, s_once, 16)
                    if dbg_stage >= 6:
                        for e in range(NE):
                            ii = g_.dma_start(
                                out=bass.AP(w1bf.tensor, e * DFF * E, [[256 * E, 16], [1, 256 * E]]),
                                in_=bass.AP(w1.tensor, e * DFF * E, [[256 * E, 16], [1, 256 * E]]))
                            p.inc(ii, s_stage[2 * e], 16)
                            ii = g_.dma_start(
                                out=bass.AP(w2bf.tensor, e * E * DFF, [[256 * DFF, 4], [1, 256 * DFF]]),
                                in_=bass.AP(w2.tensor, e * E * DFF, [[256 * DFF, 4], [1, 256 * DFF]]))
                            p.inc(ii, s_stage[2 * e + 1], 16)

                def f_sync(sy):
                    p.wait_all(sy)
                    for (g, hp, i, vi, ntiles) in sched:
                        slot = vi % 8
                        if vi >= 8:
                            p.wt(sy, tk[("av", vi - 8)])
                        ii = sy.dma_start(
                            out=vst[:, slot * 130:(slot + 1) * 130],
                            in_=bass.AP(vh.tensor, i * 128 * 1040 + hp * 130,
                                        [[1040, 128], [1, 130]]))
                        tk[("v", vi)] = p.inc(ii, s_v8[slot], 16)
                        if i == ntiles - 1:
                            p.wt(sy, tk[("m1", g, hp)])
                            if g > 0:
                                p.wt(sy, tk[("op", g - 1, 3, 1)])
                            ii = sy.dma_start(out=ao[64:128, hp * 512:(hp + 1) * 512],
                                              in_=st64[0:64, :])
                            tk[("mv", g, hp)] = p.inc(ii, s_mv, 16)
                        if hp == 7 and i == ntiles - 1:
                            for tt in range(4):
                                if (g, tt) != (0, 0):
                                    pg, ptt = (g, tt - 1) if tt else (g - 1, 3)
                                    p.wt(sy, tk[("oa", pg, ptt, 1)])
                                ii = sy.dma_start(
                                    out=axb[:, :],
                                    in_=x_in[g * 512 + tt * 128:g * 512 + (tt + 1) * 128, :])
                                tk[("xl", g, tt)] = p.inc(ii, s_ax, 16)

                def f_tensor(t):
                    p.wait_all(t)
                    p.wt(t, tk["wo"])
                    for g in range(NG):
                        ntiles = 4 * (g + 1)
                        for hp in range(8):
                            base_vi = base_of[(g, hp)]

                            def emit_sp(i, vi):
                                par = vi % 2
                                psab = ps_a if par == 0 else ps_b
                                full = i < 4 * g + 2
                                qoff = g * 512 if full else g * 512 + 256
                                qlen = 512 if full else 256
                                if vi >= 2:
                                    p.wt(t, tk[("e", vi - 2)])
                                ii = t.matmul(
                                    psab[:, 0:qlen],
                                    A[0:64, hp * T + i * 128:hp * T + (i + 1) * 128],
                                    Bq[0:64, hp * T + qoff:hp * T + qoff + qlen],
                                    start=True, stop=True, tile_position=(0, 0))
                                ii = t.matmul(
                                    psab[:, 512:512 + qlen],
                                    A[64:128, hp * T + i * 128:hp * T + (i + 1) * 128],
                                    Bq[64:128, hp * T + qoff:hp * T + qoff + qlen],
                                    start=True, stop=True, tile_position=(64, 0))
                                tk[("s", vi)] = p.inc(ii, s_pe, 1)

                            emit_sp(0, base_vi)
                            emit_sp(1, base_vi + 1)
                            for i in range(ntiles):
                                vi = base_vi + i
                                par = vi % 2
                                full = i < 4 * g + 2
                                p.wt(t, tk[("e", vi)])
                                p.wt(t, tk[("v", vi)])
                                if i == 0 and (g, hp) != (0, 0):
                                    p.wt(t, tk[("m1", *prev_gh(g, hp))])
                                slot = vi % 8
                                if full:
                                    ii = t.matmul(ps_c[0:65, 0:512],
                                                  vst[:, slot * 130:slot * 130 + 65],
                                                  pb[:, par * 1024:par * 1024 + 512],
                                                  start=(i == 0), stop=(i == ntiles - 1))
                                    ii = t.matmul(ps_c[0:65, 512:1024],
                                                  vst[:, slot * 130 + 65:slot * 130 + 130],
                                                  pb[:, par * 1024 + 512:par * 1024 + 1024],
                                                  start=(i == 0), stop=(i == ntiles - 1))
                                else:
                                    ii = t.matmul(ps_c[0:65, 256:512],
                                                  vst[:, slot * 130:slot * 130 + 65],
                                                  pb[:, par * 1024:par * 1024 + 256],
                                                  start=False, stop=(i == ntiles - 1))
                                    ii = t.matmul(ps_c[0:65, 768:1024],
                                                  vst[:, slot * 130 + 65:slot * 130 + 130],
                                                  pb[:, par * 1024 + 512:par * 1024 + 768],
                                                  start=False, stop=(i == ntiles - 1))
                                tk[("av", vi)] = p.inc(ii, s_pe, 1)
                                if i + 2 < ntiles:
                                    emit_sp(i + 2, base_vi + i + 2)
                            p.wt(t, tk[("rc", g, hp)])
                            if (g, hp) != (0, 0):
                                p.wt(t, tk[("b1", *prev_gh(g, hp))])
                            if hp == 0 and g > 0:
                                p.wt(t, tk[("oa", g - 1, 3, 1)])
                            ii = t.matmul(ps_sp[0:64, :], ones128[64:65, 0:64],
                                          sinv[64:65, 0:512], start=True, stop=True)
                            tk[("bc", g, hp, 0)] = p.inc(ii, s_pe, 1)
                            p.wt(t, tk[("b0", g, hp)])
                            ii = t.matmul(ps_sp[0:64, :], ones128[64:65, 0:64],
                                          sinv[64:65, 512:1024], start=True, stop=True)
                            tk[("bc", g, hp, 1)] = p.inc(ii, s_pe, 1)
                        p.wt(t, tk[("mv", g, 7)])
                        for tt in range(4):
                            for hf in range(2):
                                if (tt, hf) == (0, 0):
                                    p.wt(t, tk[("b1", g, 7)])
                                else:
                                    pg = (g, tt, hf - 1) if hf else (g, tt - 1, 1)
                                    p.wt(t, tk[("oa", *pg)])
                                for r in range(8):
                                    ii = t.matmul(
                                        ps_sp[:, :],
                                        ao[:, r * 512 + tt * 128:r * 512 + (tt + 1) * 128],
                                        wo_sb[:, r * 1024 + hf * 512:r * 1024 + (hf + 1) * 512],
                                        start=(r == 0), stop=(r == 7))
                                tk[("op", g, tt, hf)] = p.inc(ii, s_pe, 1)

                def f_scalar(a):
                    p.wait_all(a)
                    for g in range(NG):
                        ntiles = 4 * (g + 1)
                        for hp in range(8):
                            base_vi = base_of[(g, hp)]
                            for i in range(ntiles):
                                vi = base_vi + i
                                par = vi % 2
                                psab = ps_a if par == 0 else ps_b
                                full = i < 4 * g + 2
                                p.wt(a, tk[("s", vi)])
                                if vi >= 2:
                                    p.wt(a, tk[("av", vi - 2)])
                                if full:
                                    ii = a.activation(pb[:, par * 1024:(par + 1) * 1024],
                                                      psab[:, :], AF.Exp, scale=0.125)
                                else:
                                    a.activation(pb[:, par * 1024:par * 1024 + 256],
                                                 psab[:, 0:256], AF.Exp, scale=0.125)
                                    ii = a.activation(pb[:, par * 1024 + 512:par * 1024 + 768],
                                                      psab[:, 512:768], AF.Exp, scale=0.125)
                                tk[("e", vi)] = p.inc(ii, s_act, 1)
                        for tt in range(4):
                            p.wt(a, tk[("oa", g, tt, 1)])
                            ii = a.dma_start(
                                out=x1h[g * 512 + tt * 128:g * 512 + (tt + 1) * 128, :],
                                in_=ax1[:, :])
                            tk[("x1", g, tt)] = p.inc(ii, s_sc, 16)

                def f_vector(v):
                    p.wait_all(v)
                    for g in range(NG):
                        ntiles = 4 * (g + 1)
                        for hp in range(8):
                            lastvi = base_of[(g, hp)] + ntiles - 1
                            p.wt(v, tk[("av", lastvi)])
                            if (g, hp) != (0, 0):
                                p.wt(v, tk[("mv", *prev_gh(g, hp))])
                            if hp == 0 and g > 0:
                                p.wt(v, tk[("op", g - 1, 3, 1)])
                            with nc.allow_low_precision(reason="bf16 softmax normalizer (0.4% rel; bar 5.5%)"):
                                v.reciprocal(sinv[64:65, 0:512], ps_c[64:65, 0:512])
                                v.reciprocal(sinv[64:65, 512:1024], ps_c[64:65, 512:1024])
                            tk[("rc", g, hp)] = p.dinc(v, s_dve, 1)
                            p.wt(v, tk[("bc", g, hp, 0)])
                            v.tensor_copy(bcb[0:64, :], ps_sp[0:64, :])
                            tk[("b0", g, hp)] = p.dinc(v, s_dve, 1)
                            v.tensor_mul(ao[0:64, hp * 512:(hp + 1) * 512],
                                         ps_c[0:64, 0:512], bcb[0:64, :])
                            tk[("m0", g, hp)] = p.dinc(v, s_dve, 1)
                            p.wt(v, tk[("bc", g, hp, 1)])
                            v.tensor_copy(bcb[0:64, :], ps_sp[0:64, :])
                            tk[("b1", g, hp)] = p.dinc(v, s_dve, 1)
                            v.tensor_mul(st64[0:64, :], ps_c[0:64, 512:1024],
                                         bcb[0:64, :])
                            tk[("m1", g, hp)] = p.dinc(v, s_dve, 1)
                        for tt in range(4):
                            for hf in range(2):
                                p.wt(v, tk[("op", g, tt, hf)])
                                p.wt(v, tk[("xl", g, tt)])
                                if hf == 0 and (g, tt) != (0, 0):
                                    pg, ptt = (g, tt - 1) if tt else (g - 1, 3)
                                    p.wt(v, tk[("x1", pg, ptt)])
                                v.tensor_add(ax1[:, hf * 512:(hf + 1) * 512],
                                             ps_sp[:, :],
                                             axb[:, hf * 512:(hf + 1) * 512])
                                tk[("oa", g, tt, hf)] = p.dinc(v, s_dve, 1)

                p.run_phase({"gpsimd": f_gp, "sync": f_sync, "tensor": f_tensor,
                             "scalar": f_scalar, "vector": f_vector}, tk)

        if dbg and dbg_stage == 4:
            def f_scalar_d4(a):
                p.wait_all(a)
                tkk = None
                for tt in range(NT):
                    ii = a.dma_start(out=x1_dbg[tt * 128:(tt + 1) * 128, :],
                                     in_=x1h[tt * 128:(tt + 1) * 128, :])
                    tkk = p.inc(ii, s_sc, 16)
                a.wait_ge(tkk[0], tkk[1])
            p.run_phase({"scalar": f_scalar_d4})

        if dbg_stage >= 5:
            with (
                nc.sbuf_tensor("wr8", [8, 1024], BF16) as wr8,
                nc.sbuf_tensor("zb", [128, 1024], BF16) as zb,
            ):
                tk = TokDict()

                def f_gp(g):
                    p.wait_all(g)
                    ii = g.dma_start(out=wr8[:, :], in_=wr[:, :])
                    tk["w"] = p.inc(ii, s_once, 16)
                    g.memset(zb[:, :], 0.0)
                    tk["zb"] = p.dinc(g, s_gp, 1)

                def f_sync(sy):
                    p.wait_all(sy)
                    p.wt(sy, tk["zb"])
                    for tt in range(NT):
                        ii = sy.dma_start(out=accB[tt * 128:(tt + 1) * 128, :], in_=zb[:, :])
                        p.inc(ii, s_once, 16)

                def f_tensor(t):
                    p.wait_all(t)
                    p.wt(t, tk["w"])
                    for r in range(8):
                        ii = t.transpose(ps_tr[:, r * 8:(r + 1) * 8],
                                         wr8[0:8, r * 128:(r + 1) * 128], id_sb[0:8, 0:8])
                    tk["tr"] = p.inc(ii, s_pe, 1)

                def f_vector(v):
                    p.wait_all(v)
                    p.wt(v, tk["tr"])
                    v.tensor_copy(wr_t[:, :], ps_tr[:, 0:64])
                    p.dinc(v, s_dve, 1)

                p.run_phase({"gpsimd": f_gp, "sync": f_sync,
                             "tensor": f_tensor, "vector": f_vector}, tk)

            ln_phase(x1h, A, do_logits=True, store_tm=True)

            def gates_phase():
                tk = TokDict()

                def f_vector(v):
                    p.wait_all(v)
                    v.tensor_sub(stats[:, 64:96], stats[:, 32:64], stats[:, 0:32])
                    v.tensor_sub(stats[:, 96:128], stats[:, 0:32], stats[:, 32:64])
                    tk["d"] = p.dinc(v, s_dve, 1)
                    p.wt(v, tk["exp"])
                    v.tensor_single_scalar(stats[:, 64:128], stats[:, 128:192], 1.0, ALU.add)
                    v.drain()
                    v.reciprocal(stats[:, 128:192], stats[:, 64:128])
                    v.drain()
                    p.inc(v.engine_nop(), s_dve, 1)
                    for tt in range(NT):
                        gcol = gates[:, tt * 8:(tt + 1) * 8]
                        v.scalar_tensor_tensor(gcol, eqs[:, 256 + tt * 8:256 + tt * 8 + 8],
                                               stats[:, 160 + tt:160 + tt + 1], zero8[:, :],
                                               ALU.mult, ALU.add)
                        v.drain()
                        ii = v.scalar_tensor_tensor(gcol, eqs[:, tt * 8:tt * 8 + 8],
                                                    stats[:, 128 + tt:128 + tt + 1], gcol,
                                                    ALU.mult, ALU.add)
                        p.inc(ii, s_dve, 1)

                def f_scalar(a):
                    p.wait_all(a)
                    p.wt(a, tk["d"])
                    a.activation(stats[:, 128:192], stats[:, 64:128], AF.Exp)
                    tk["exp"] = p.dinc(a, s_act, 1)

                p.run_phase({"vector": f_vector, "scalar": f_scalar}, tk)

            gates_phase()

        if dbg and dbg_stage == 5:
            def f_scalar_d5(a):
                p.wait_all(a)
                ii = a.dma_start(out=g_dbg[:, :], in_=gates[:, :])
                tkk = p.inc(ii, s_sc, 16)
                a.wait_ge(tkk[0], tkk[1])
            p.run_phase({"scalar": f_scalar_d5})

        CAP = 1536
        NCC = CAP // 128
        BIGF = 1.0e5
        if dbg_stage >= 6:
            with (
                nc.sbuf_tensor("iota_sb", [128, 1536], F32) as iota_sb,
                nc.sbuf_tensor("id32_sb", [128, 128], F32) as id32_sb,
                nc.sbuf_tensor("rhs2", [128, 128], BF16) as rhs2,
                nc.sbuf_tensor("maskb", [128, 256], BF16) as maskb,
                nc.sbuf_tensor("maskE", [8, 4096], F32) as maskE,
                nc.sbuf_tensor("slotE", [8, 4096], F32) as slotE,
                nc.sbuf_tensor("slottm", [128, 256], F32) as slottm,
                nc.sbuf_tensor("mful", [128, 2 * 1536], BF16) as mful,
                nc.sbuf_tensor("idxw", [128, 192], F32) as idxw,
                nc.sbuf_tensor("idxst", [128, 96], mybir.dt.int32) as idxst,
                nc.sbuf_tensor("bigc", [128, 12], F32) as bigc,
            ):
                tk = TokDict()

                def f_sync(sy):
                    p.wait_all(sy)
                    ii = sy.dma_start(out=iota_sb[:, :], in_=iota_in[:, :])
                    p.inc(ii, s_once, 16)
                    ii = sy.dma_start(out=id32_sb[:, :], in_=id32_in[:, :])
                    p.inc(ii, s_once, 16)
                    ii = sy.dma_start(out=rhs2[:, :], in_=rhs2s_in[:, :])
                    tk["consts"] = p.inc(ii, s_once, 16)

                def f_gp(g):
                    p.wait_all(g)
                    g.memset(bigc[:, :], BIGF)
                    tk["big"] = p.dinc(g, s_gp, 1)

                def f_vector(v):
                    p.wait_all(v)
                    p.wt(v, tk["consts"])
                    v.tensor_add(slottm[:, :], eqs[:, 0:256], eqs[:, 256:512])
                    v.drain()
                    v.tensor_copy(maskb[:, :], slottm[:, :])
                    tk["mb"] = p.dinc(v, s_dve, 1)
                    for ch in range(4):
                        p.wt(v, tk[("mtr", ch)])
                        v.tensor_copy(maskE[0:8, ch * 1024:(ch + 1) * 1024],
                                      ps_tr[0:8, 0:1024])
                        tk[("mev", ch)] = p.dinc(v, s_dve, 1)
                    v.tensor_tensor_scan(slotE[:, :], maskE[:, :], maskE[:, :], 0.0,
                                         ALU.add, ALU.max)
                    v.drain()
                    v.tensor_mul(slotE[:, :], slotE[:, :], maskE[:, :])
                    tk["slotE"] = p.dinc(v, s_dve, 1)
                    p.wt(v, tk["str"])
                    v.tensor_copy(slottm[:, :], ps_sp[:, 0:256])
                    tk["stm"] = p.dinc(v, s_dve, 1)
                    for e in range(NE):
                        if e:
                            p.wt(v, tk[("lmm", e - 1, NT - 1)])
                        gsrc = bass.AP(gates, e, [[256, 128], [8, 32]])
                        gdst = bass.AP(rhs2, 2, [[128, 128], [4, 32]])
                        v.tensor_copy(gdst, gsrc)
                        tk[("g", e)] = p.dinc(v, s_dve, 1)
                        for tt in range(NT):
                            if e * NT + tt >= 2:
                                j = e * NT + tt - 2
                                p.wt(v, tk[("lmm", j // NT, j % NT)])
                            half = (tt % 2) * 1536
                            v.tensor_single_scalar(mful[:, half:half + 1536],
                                                   iota_sb[:, :],
                                                   slottm[:, tt * 8 + e:tt * 8 + e + 1],
                                                   ALU.is_equal)
                            tk[("M", e, tt)] = p.dinc(v, s_dve, 1)
                    p.wt(v, tk["lev"])
                    p.wt(v, tk["big"])
                    for e in range(NE):
                        base = e * 48
                        plo = bass.AP(lists_keep, base + 0, [[384, 128], [4, 12]])
                        ttv = bass.AP(lists_keep, base + 1, [[384, 128], [4, 12]])
                        val = bass.AP(lists_keep, base + 3, [[384, 128], [4, 12]])
                        ecol = idxw[:, e * 12:(e + 1) * 12]
                        ecol2 = idxw[:, 96 + e * 12:96 + (e + 1) * 12]
                        icol = idxst[:, e * 12:(e + 1) * 12]
                        v.scalar_tensor_tensor(ecol, ttv, 128.0, plo, ALU.mult, ALU.add)
                        v.drain()
                        v.scalar_tensor_tensor(ecol, val, -BIGF, ecol, ALU.mult, ALU.add)
                        v.drain()
                        v.tensor_add(ecol, bigc[:, :], ecol)
                        v.drain()
                        v.tensor_single_scalar(ecol2, ecol, 8388608.0, ALU.add)
                        v.drain()
                        with nc.allow_low_precision(reason="bitcast f32->i32 (exact ints)"):
                            v.tensor_copy(icol, ecol2)
                        v.drain()
                        v.tensor_single_scalar(idxi_keep[:, e * 12:(e + 1) * 12],
                                               icol, 0x007FFFFF, ALU.bitwise_and)
                    tk["idx"] = p.dinc(v, s_dve, 1)

                def f_tensor(t):
                    p.wait_all(t)
                    p.wt(t, tk["mb"])
                    for ch in range(4):
                        if ch >= 1:
                            p.wt(t, tk[("mev", ch - 1)])
                        for j in range(8):
                            tt = ch * 8 + j
                            ii = t.transpose(ps_tr[0:8, j * 128:(j + 1) * 128],
                                             maskb[:, tt * 8:tt * 8 + 8], id_sb[:, :])
                        tk[("mtr", ch)] = p.inc(ii, s_pe, 1)
                    p.wt(t, tk["slotE"])
                    for tt in range(NT):
                        ii = t.transpose(ps_sp[:, tt * 8:(tt + 1) * 8],
                                         slotE[0:8, tt * 128:(tt + 1) * 128],
                                         id32_sb[0:8, 0:8])
                    tk["str"] = p.inc(ii, s_pe, 1)
                    p.wt(t, tk["stm"])
                    for e in range(NE):
                        p.wt(t, tk[("g", e)])
                        for tt in range(NT):
                            p.wt(t, tk[("M", e, tt)])
                            half = (tt % 2) * 1536
                            for c in range(NCC):
                                ii = t.matmul(
                                    ps_c[:, e * 48 + c * 4:e * 48 + (c + 1) * 4],
                                    mful[:, half + c * 128:half + (c + 1) * 128],
                                    rhs2[:, tt * 4:(tt + 1) * 4],
                                    start=(e == 0 and tt == 0 and c == 0),
                                    stop=(e == NE - 1 and tt == NT - 1 and c == NCC - 1),
                                    skip_group_check=True)
                            tk[("lmm", e, tt)] = p.inc(ii, s_pe, 1)

                def f_scalar(a):
                    p.wait_all(a)
                    p.wt(a, tk[("lmm", NE - 1, NT - 1)])
                    a.activation(lists_keep[:, :], ps_c[:, 0:384], AF.Copy)
                    tk["lev"] = p.dinc(a, s_act, 1)

                p.run_phase({"sync": f_sync, "gpsimd": f_gp, "tensor": f_tensor,
                             "scalar": f_scalar, "vector": f_vector}, tk)

        if dbg and dbg_stage == 6:
            def f_scalar_d6(a):
                p.wait_all(a)
                ii = a.dma_start(out=x1_dbg[0:128, 0:384], in_=lists_keep[:, :])
                p.inc(ii, s_sc, 16)
                ii = a.dma_start(out=g_dbg[:, :], in_=gates[:, :])
                tkk = p.inc(ii, s_sc, 16)
                a.wait_ge(tkk[0], tkk[1])

            def f_gp_d6(g):
                p.wait_all(g)
                ii = g.dma_start(out=x1_dbg[384:512, 0:96],
                                 in_=bass.AP(idxi_keep, 0, [[96, 128], [1, 96]]))
                tkk = p.inc(ii, s_sc, 16)
                g.wait_ge(tkk[0], tkk[1])
            p.run_phase({"scalar": f_scalar_d6, "gpsimd": f_gp_d6})

        if dbg_stage >= 6:
            with (
                nc.sbuf_tensor("w1t", [128, 8192], BF16) as w1t,
                nc.sbuf_tensor("w2t", [128, 8192], BF16) as w2t,
                nc.sbuf_tensor("hsb", [128, 4 * 1536], BF16) as hsb,
                nc.sbuf_tensor("xgt", [128, 2048], BF16) as xgt,
            ):
                tk = TokDict()
                TGS = 3

                def xgf(par):
                    return par * 12 * 1024

                def f_gp(g):
                    p.wait_all(g)
                    for e in range(NE):
                        for c in range(NCC):
                            gi_ = e * NCC + c
                            if gi_ >= 2:
                                pe_, pc = (e, c - 2) if c >= 2 else (e - 1, NCC - 2 + c)
                                p.wt(g, tk[("gtrp", pe_, pc)])
                            else:
                                p.wt(g, tk["xz"])
                            ii = g.indirect_dma_start(
                                out=xgt[:, (gi_ % 2) * 1024:(gi_ % 2) * 1024 + 1024],
                                out_offset=None,
                                in_=xn2h[0:128, :],
                                in_offset=bass.IndirectOffsetOnAxis(
                                    ap=idxi_keep[:, e * 12 + c:e * 12 + c + 1], axis=0),
                            )
                            tk[("gth", e, c)] = p.inc(ii, s_gt2[gi_ % 2], 16)
                        if e >= 1:
                            se = e - 1
                            if se >= 1:
                                g.wait_ge(s_sct, 192 * se)
                            p.wt(g, tk[("yv", se)])
                            for c in range(NCC):
                                ii = g.indirect_dma_start(
                                    out=accB[0:128, :],
                                    out_offset=bass.IndirectOffsetOnAxis(
                                        ap=idxi_keep[:, se * 12 + c:se * 12 + c + 1], axis=0),
                                    in_=Bq[:, (se % 2) * 12288 + c * 1024:(se % 2) * 12288 + (c + 1) * 1024],
                                    in_offset=None,
                                    compute_op=ALU.add,
                                )
                                p.inc(ii, s_sct, 16)
                    se = NE - 1
                    g.wait_ge(s_sct, 192 * se)
                    p.wt(g, tk[("yv", se)])
                    for c in range(NCC):
                        ii = g.indirect_dma_start(
                            out=accB[0:128, :],
                            out_offset=bass.IndirectOffsetOnAxis(
                                ap=idxi_keep[:, se * 12 + c:se * 12 + c + 1], axis=0),
                            in_=Bq[:, (se % 2) * 12288 + c * 1024:(se % 2) * 12288 + (c + 1) * 1024],
                            in_offset=None,
                            compute_op=ALU.add,
                        )
                        p.inc(ii, s_sct, 16)

                def f_vector(v):
                    p.wait_all(v)
                    v.memset(xgt[:, :], 0.0)
                    v.memset(bass.AP(A, 0, [[8 * T, 128], [1, 24 * 1024]]), 0.0)
                    tk["xz"] = p.dinc(v, s_dve, 1)
                    for e in range(NE):
                        for c in range(NCC):
                            gi_ = e * NCC + c
                            p.wt(v, tk[("gtrp", e, c)])
                            if e >= 2 and c == 0:
                                p.wt(v, tk[("h", 96 * (e - 1) - 1)])
                            srcp = bass.AP(ps_tr, 0, [[1024, 128], [128, 8], [1, 128]])
                            dstp = bass.AP(A, xgf(e % 2) + c * 128,
                                           [[8 * T, 128], [1536, 8], [1, 128]])
                            v.tensor_copy(dstp, srcp)
                            tk[("gtr", e, c)] = p.dinc(v, s_dve, 1)
                        if e >= 2:
                            v.wait_ge(s_sct, 192 * (e - 1))
                        v.memset(Bq[:, (e % 2) * 12288:(e % 2) * 12288 + 12288], 0.0)
                        tk[("yz", e)] = p.dinc(v, s_dve, 1)
                        for sl in range(8):
                            for tg in range(3):
                                for tt in range(4):
                                    for hf in range(2):
                                        yi = (tg * 4 + tt) * 2 + hf
                                        p.wt(v, tk[("y", e, sl, tg, tt, hf)])
                                        gap = bass.AP(lists_keep,
                                                      e * 48 + (tg * 4 + tt) * 4 + 2,
                                                      [[384, 128], [1, 1]])
                                        ycol = (e % 2) * 12288 + (tg * 4 + tt) * 1024 + hf * 512
                                        ii = v.scalar_tensor_tensor(
                                            Bq[:, ycol:ycol + 512],
                                            ps_c[:, (yi % 2) * 512:(yi % 2) * 512 + 512],
                                            gap, Bq[:, ycol:ycol + 512],
                                            ALU.mult, ALU.add)
                                        if sl == 7 and yi == 23:
                                            tk[("ye", e, sl, yi)] = p.dinc(v, s_dve, 1)
                                        else:
                                            tk[("ye", e, sl, yi)] = p.inc(ii, s_dve, 1)
                            if sl == 7:
                                tk[("yv", e)] = tk[("ye", e, 7, 23)]

                def f_sync(sy):
                    p.wait_all(sy)
                    for e in range(NE):
                        for sl in range(8):
                            par = sl % 2
                            if sl == 0:
                                sy.wait_ge(s_stage[2 * e], 16)
                            if (e, sl) not in ((0, 0), (0, 1)):
                                pe_, psl = (e, sl - 2) if sl >= 2 else (e - 1, sl + 6)
                                p.wt(sy, tk[("y", pe_, psl, TGS - 1, 3, 1)])
                            for r in range(8):
                                ii = sy.dma_start(
                                    out=w1t[:, par * 4096 + r * 512:par * 4096 + (r + 1) * 512],
                                    in_=bass.AP(w1bf.tensor, e * DFF * E + sl * 512 * E + r * 128,
                                                [[E, 512], [1, 128]]),
                                    transpose=True)
                                tk[("w1", e, sl)] = p.inc(ii, s_w1t2[par], 16)
                            if sl == 0:
                                sy.wait_ge(s_stage[2 * e + 1], 16)
                            for fs in range(4):
                                ii = sy.dma_start(
                                    out=w2t[:, par * 4096 + fs * 1024:par * 4096 + (fs + 1) * 1024],
                                    in_=bass.AP(w2bf.tensor, e * E * DFF + sl * 512 + fs * 128,
                                                [[DFF, 1024], [1, 128]]),
                                    transpose=True)
                                tk[("w2", e, sl)] = p.inc(ii, s_w2t2[par], 16)

                def f_tensor(t):
                    p.wait_all(t)
                    p.wt(t, tk["xz"])
                    hidx = 0
                    for e in range(NE):
                        par_e = e % 2
                        for c in range(NCC):
                            gi_ = e * NCC + c
                            p.wt(t, tk[("gth", e, c)])
                            if gi_ >= 1:
                                pe_, pc = (e, c - 1) if c else (e - 1, NCC - 1)
                                p.wt(t, tk[("gtr", pe_, pc)])
                            for r in range(8):
                                ii = t.transpose(
                                    ps_tr[:, r * 128:(r + 1) * 128],
                                    xgt[:, (gi_ % 2) * 1024 + r * 128:(gi_ % 2) * 1024 + (r + 1) * 128],
                                    id_sb[:, :])
                            tk[("gtrp", e, c)] = p.inc(ii, s_pe, 1)
                        p.wt(t, tk[("gtr", e, NCC - 1)])
                        p.wt(t, tk[("yz", e)])
                        for sl in range(8):
                            par = sl % 2
                            p.wt(t, tk[("w1", e, sl)])
                            for tg in range(TGS):
                                for fs in range(4):
                                    bsel = (hidx // 2) % 2
                                    psh = ps_a if bsel == 0 else ps_b
                                    half = (hidx % 2) * 512
                                    if hidx >= 4:
                                        p.wt(t, tk[("si", (hidx - 4) // 2)])
                                    for r in range(8):
                                        ii = t.matmul(
                                            psh[:, half:half + 512],
                                            w1t[:, par * 4096 + r * 512 + fs * 128:par * 4096 + r * 512 + (fs + 1) * 128],
                                            A[:, xgf(par_e) + r * 1536 + tg * 512:xgf(par_e) + r * 1536 + (tg + 1) * 512],
                                            start=(r == 0), stop=(r == 7))
                                    tk[("h", hidx)] = p.inc(ii, s_pe, 1)
                                    hidx += 1
                            p.wt(t, tk[("w2", e, sl)])
                            p.wt(t, tk[("si", (hidx - 2) // 2)])
                            for tg in range(TGS):
                                for tt in range(4):
                                    for hf in range(2):
                                        yi = (tg * 4 + tt) * 2 + hf
                                        if (e, sl) != (0, 0) or yi >= 2:
                                            if yi >= 2:
                                                p.wt(t, tk[("ye", e, sl, yi - 2)])
                                            else:
                                                pe_, psl = (e, sl - 1) if sl else (e - 1, 7)
                                                p.wt(t, tk[("ye", pe_, psl, 22 + yi)])
                                        for fs in range(4):
                                            ii = t.matmul(
                                                ps_c[:, (yi % 2) * 512:(yi % 2) * 512 + 512],
                                                hsb[:, fs * 1536 + tg * 512 + tt * 128:fs * 1536 + tg * 512 + (tt + 1) * 128],
                                                w2t[:, par * 4096 + fs * 1024 + hf * 512:par * 4096 + fs * 1024 + (hf + 1) * 512],
                                                start=(fs == 0), stop=(fs == 3))
                                        tk[("y", e, sl, tg, tt, hf)] = p.inc(ii, s_pe, 1)

                def f_scalar(a):
                    p.wait_all(a)
                    hidx = 0
                    for e in range(NE):
                        for sl in range(8):
                            for tg in range(TGS):
                                for fs in range(4):
                                    if hidx % 2 == 1:
                                        p.wt(a, tk[("h", hidx)])
                                        bsel = (hidx // 2) % 2
                                        psh = ps_a if bsel == 0 else ps_b
                                        dst = bass.AP(hsb, (fs - 1) * 1536 + tg * 512,
                                                      [[4 * 1536, 128], [1536, 2], [1, 512]])
                                        ii = a.activation(dst, psh[:, :], AF.Silu)
                                        tk[("si", hidx // 2)] = p.inc(ii, s_act, 1)
                                    hidx += 1

                p.run_phase({"gpsimd": f_gp, "vector": f_vector, "sync": f_sync,
                             "tensor": f_tensor, "scalar": f_scalar}, tk)

        if dbg and dbg_stage == 7:
            def f_gp_d7(g):
                p.wait_all(g)
                tkk = None
                for tt in range(NT):
                    ii = g.dma_start(out=x1_dbg[tt * 128:(tt + 1) * 128, :],
                                     in_=accB[tt * 128:(tt + 1) * 128, :])
                    tkk = p.inc(ii, s_sc, 16)
                g.wait_ge(tkk[0], tkk[1])

            def f_scalar_d7(a):
                p.wait_all(a)
                tkk = None
                for c in range(8):
                    ii = a.dma_start(out=k_dbg[:, c * T:(c + 1) * T],
                                     in_=Bq[:, c * T:(c + 1) * T])
                    p.inc(ii, s_sc, 16)
                ii = a.dma_start(out=g_dbg[:, :], in_=gates[:, :])
                p.inc(ii, s_sc, 16)
                for c in range(6):
                    ii = a.dma_start(out=q_dbg[:, c * T:(c + 1) * T],
                                     in_=A[:, c * T:(c + 1) * T])
                    tkk = p.inc(ii, s_sc, 16)
                a.wait_ge(tkk[0], tkk[1])
            p.run_phase({"gpsimd": f_gp_d7, "scalar": f_scalar_d7})

        if dbg_stage >= 7:
            with (
                nc.sbuf_tensor("fx", [128, 2048], F32) as fx,
                nc.sbuf_tensor("fb", [128, 2048], BF16) as fb,
                nc.sbuf_tensor("fo", [128, 2048], F32) as fo,
            ):
                tk = TokDict()

                def f_sync(sy):
                    p.wait_all(sy)
                    for tt in range(NT):
                        if tt >= 2:
                            p.wt(sy, tk[("add", tt - 2)])
                        ii = sy.dma_start(out=fx[:, (tt % 2) * 1024:(tt % 2) * 1024 + 1024],
                                          in_=x1h[tt * 128:(tt + 1) * 128, :])
                        p.inc(ii, s_lx2[tt % 2], 16)
                        ii = sy.dma_start(out=fb[:, (tt % 2) * 1024:(tt % 2) * 1024 + 1024],
                                          in_=accB[tt * 128:(tt + 1) * 128, :])
                        tk[("l", tt)] = p.inc(ii, s_lx2[tt % 2], 16)

                def f_vector(v):
                    p.wait_all(v)
                    for tt in range(NT):
                        p.wt(v, tk[("l", tt)])
                        if tt >= 2:
                            p.wt(v, tk[("st", tt - 2)])
                        v.tensor_add(fo[:, (tt % 2) * 1024:(tt % 2) * 1024 + 1024],
                                     fb[:, (tt % 2) * 1024:(tt % 2) * 1024 + 1024],
                                     fx[:, (tt % 2) * 1024:(tt % 2) * 1024 + 1024])
                        tk[("add", tt)] = p.dinc(v, s_dve, 1)

                def f_scalar(a):
                    p.wait_all(a)
                    for tt in range(NT):
                        p.wt(a, tk[("add", tt)])
                        ii = a.dma_start(out=out[tt * 128:(tt + 1) * 128, :],
                                         in_=fo[:, (tt % 2) * 1024:(tt % 2) * 1024 + 1024])
                        tk[("st", tt)] = p.inc(ii, s_ost2[tt % 2], 16)
                    a.wait_ge(s_ost2[0], p.tot(s_ost2[0]))
                    a.wait_ge(s_ost2[1], p.tot(s_ost2[1]))
                    for sem in s_stage:
                        if p.tot(sem):
                            a.wait_ge(sem, p.tot(sem))

                p.run_phase({"sync": f_sync, "vector": f_vector, "scalar": f_scalar}, tk)
        else:
            def f_scalar_end(a):
                p.wait_all(a)
                for sem in s_stage:
                    if p.tot(sem):
                        a.wait_ge(sem, p.tot(sem))
            p.run_phase({"scalar": f_scalar_end})

    return nc


def kernel(L0_attn_x, L0_attn_wq, L0_attn_w_down, L0_attn_wk_up, L0_attn_wv_up,
           L0_attn_wo, L0_ffn_W1, L0_ffn_W2, L0_ffn_Wr):
    C, S, swp, ident, ident32, iota1, rhs2s = _host_consts()
    nc = build_nc(dbg_stage=DBG_STAGE, dbg=DBG)
    x = np.ascontiguousarray(np.asarray(L0_attn_x, dtype=np.float32))
    common = {
        "wq": np.ascontiguousarray(L0_attn_wq, dtype=np.float32),
        "wd": np.ascontiguousarray(L0_attn_w_down, dtype=np.float32),
        "wk": np.ascontiguousarray(L0_attn_wk_up, dtype=np.float32),
        "wv": np.ascontiguousarray(L0_attn_wv_up, dtype=np.float32),
        "wo": np.ascontiguousarray(L0_attn_wo, dtype=np.float32),
        "w1": np.ascontiguousarray(L0_ffn_W1, dtype=np.float32).reshape(NE * DFF, E),
        "w2": np.ascontiguousarray(L0_ffn_W2, dtype=np.float32).reshape(NE * E, DFF),
        "wr": np.ascontiguousarray(L0_ffn_Wr, dtype=np.float32),
        "ropec": C, "ropes": S, "swp": swp, "ident": ident,
        "ident32": ident32, "iota1": iota1, "rhs2s": rhs2s,
    }
    in_maps = [dict(common, x=np.ascontiguousarray(x[b])) for b in range(NCORE)]
    res = run_bass_kernel_spmd(nc, in_maps, core_ids=list(range(NCORE)))
    return np.stack([r["out"] for r in res.results], axis=0)


# version 04
# speedup vs baseline: 1.5078x; 1.1520x over previous
import numpy as np
import ml_dtypes
import concourse.bass as bass
import concourse.mybir as mybir
from concourse.bass_utils import run_bass_kernel_spmd

F32 = mybir.dt.float32
BF16 = mybir.dt.bfloat16
AF = mybir.ActivationFunctionType
ALU = mybir.AluOpType
AX = mybir.AxisListType
BF = ml_dtypes.bfloat16

T, E, NH, D, L = 4096, 1024, 16, 64, 512
NE, DFF = 8, 4096
NT = T // 128
NG = T // 512
EPS = 1e-5
NCORE = 8

DBG_STAGE = 9
DBG = 0


def _host_consts():
    inv = 1.0 / (10000.0 ** (np.arange(0, D, 2, dtype=np.float32) / D))
    ang = np.outer(np.arange(T, dtype=np.float32), inv)
    cos = np.cos(ang).astype(np.float32)
    sin = np.sin(ang).astype(np.float32)
    C = np.zeros((128, T), np.float32)
    S = np.zeros((128, T), np.float32)
    for r in range(128):
        i = r % 64
        pp = i // 2
        C[r] = cos[:, pp]
        S[r] = sin[:, pp] * (1.0 if i % 2 == 1 else -1.0)
    swp = np.zeros((128, 128), BF)
    for r in range(128):
        swp[r ^ 1, r] = 1.0
    ident = np.eye(128, dtype=BF)
    ident32 = np.eye(128, dtype=np.float32)
    iota1 = np.broadcast_to(np.arange(1, 1537, dtype=np.float32)[None, :], (128, 1536)).copy()
    rhs2s = np.zeros((128, 128), BF)
    for tt in range(32):
        rhs2s[:, 4 * tt + 0] = np.arange(128, dtype=np.float32)
        rhs2s[:, 4 * tt + 1] = float(tt)
        rhs2s[:, 4 * tt + 3] = 1.0
    return C, S, swp, ident, ident32, iota1, rhs2s


class _FakeInstr:
    def then_inc(self, sem, k):
        return self


class TokDict(dict):

    def __init__(self):
        super().__init__()
        self.missed = set()

    def __missing__(self, k):
        self.missed.add(k)
        return 0


class _CountEng:
    def wait_ge(self, sem, val):
        pass

    def __getattr__(self, name):
        def f(*a, **kw):
            return _FakeInstr()
        return f


class Prog:

    ORDER = ("sync", "gpsimd", "tensor", "scalar", "vector")

    def __init__(self, nc):
        self.nc = nc
        self.c = {}
        self.snap = {}

    def inc(self, instr, sem, k):
        instr.then_inc(sem, k)
        self.c[sem] = self.c.get(sem, 0) + k
        return (sem, self.c[sem])

    def dinc(self, eng, sem, k):
        d = eng.drain()
        d.then_inc(sem, k)
        self.c[sem] = self.c.get(sem, 0) + k
        return (sem, self.c[sem])

    def tot(self, sem):
        return self.c.get(sem, 0)

    def wt(self, eng, tok):
        if tok:
            sem, val = tok
            if val:
                eng.wait_ge(sem, val)

    def wait_all(self, eng):
        for sem, t in self.snap.items():
            if t:
                eng.wait_ge(sem, t)

    def run_phase(self, streams, tk=None):
        self.snap = dict(self.c)
        start = dict(self.c)
        for name in self.ORDER:
            if name in streams:
                streams[name](_CountEng())
        if tk is not None:
            unresolved = tk.missed - set(tk.keys())
            assert not unresolved, f"unresolved tokens: {unresolved}"
        end = dict(self.c)
        self.c = dict(start)
        with self.nc.Block() as blk:
            for name in self.ORDER:
                if name in streams:
                    getattr(blk, name)(streams[name])
        assert self.c == end, "two-pass mismatch"


def build_nc(dbg_stage=9, dbg=0):
    nc = bass.Bass("TRN2", target_bir_lowering=False)
    _uc = [0]

    def un(s):
        _uc[0] += 1
        return f"{s}_{_uc[0]}"

    x_in = nc.dram_tensor("x", [T, E], F32, kind="ExternalInput").ap()
    wq = nc.dram_tensor("wq", [E, E], F32, kind="ExternalInput").ap()
    wd = nc.dram_tensor("wd", [E, L], F32, kind="ExternalInput").ap()
    wk = nc.dram_tensor("wk", [L, E], F32, kind="ExternalInput").ap()
    wv = nc.dram_tensor("wv", [L, E], F32, kind="ExternalInput").ap()
    wo = nc.dram_tensor("wo", [E, E], F32, kind="ExternalInput").ap()
    w1 = nc.dram_tensor("w1", [NE * DFF, E], F32, kind="ExternalInput").ap()
    w2 = nc.dram_tensor("w2", [NE * E, DFF], F32, kind="ExternalInput").ap()
    wr = nc.dram_tensor("wr", [NE, E], F32, kind="ExternalInput").ap()
    ropec = nc.dram_tensor("ropec", [128, T], F32, kind="ExternalInput").ap()
    ropes = nc.dram_tensor("ropes", [128, T], F32, kind="ExternalInput").ap()
    swp_in = nc.dram_tensor("swp", [128, 128], BF16, kind="ExternalInput").ap()
    id_in = nc.dram_tensor("ident", [128, 128], BF16, kind="ExternalInput").ap()
    id32_in = nc.dram_tensor("ident32", [128, 128], F32, kind="ExternalInput").ap()
    iota_in = nc.dram_tensor("iota1", [128, 1536], F32, kind="ExternalInput").ap()
    rhs2s_in = nc.dram_tensor("rhs2s", [128, 128], BF16, kind="ExternalInput").ap()

    out = nc.dram_tensor("out", [T, E], F32, kind="ExternalOutput").ap()
    if dbg:
        q_dbg = nc.dram_tensor("q_dbg", [128, 8 * T], BF16, kind="ExternalOutput").ap()
        k_dbg = nc.dram_tensor("k_dbg", [128, 8 * T], BF16, kind="ExternalOutput").ap()
        x1_dbg = nc.dram_tensor("x1_dbg", [T, E], F32, kind="ExternalOutput").ap()
        g_dbg = nc.dram_tensor("g_dbg", [128, 256], F32, kind="ExternalOutput").ap()
        v_dbg = nc.dram_tensor("v_dbg", [T, 1040], BF16, kind="ExternalOutput").ap()

    w1bf = nc.dram_tensor("w1bf", [NE * DFF, E], BF16).ap()
    w2bf = nc.dram_tensor("w2bf", [NE * E, DFF], BF16).ap()
    x1h = nc.dram_tensor("x1h", [T, E], F32).ap()
    vh = nc.dram_tensor("vh", [T, 1040], BF16).ap()
    xn2h = nc.dram_tensor("xn2h", [T, E], BF16).ap()
    accB = nc.dram_tensor("accB", [T, E], BF16).ap()

    from contextlib import ExitStack
    with ExitStack() as _st:
        en = _st.enter_context
        A = en(nc.sbuf_tensor("A", [128, 8 * T], BF16))
        Bq = en(nc.sbuf_tensor("Bq", [128, 8 * T], BF16))
        ones128 = en(nc.sbuf_tensor("ones128", [128, 128], BF16))
        swp_sb = en(nc.sbuf_tensor("swp_sb", [128, 128], BF16))
        id_sb = en(nc.sbuf_tensor("id_sb", [128, 128], BF16))
        stats = en(nc.sbuf_tensor("stats", [128, 192], F32))
        gates = en(nc.sbuf_tensor("gates", [128, 256], F32))
        zero8 = en(nc.sbuf_tensor("zero8", [128, 8], F32))
        epsc = en(nc.sbuf_tensor("epsc", [128, 1], F32))
        eqs = en(nc.sbuf_tensor("eqs", [128, 512], F32))
        wr_t = en(nc.sbuf_tensor("wr_t", [128, 64], BF16))
        lists_keep = en(nc.sbuf_tensor("lists_keep", [128, 384], F32))
        idxi_keep = en(nc.sbuf_tensor("idxi_keep", [128, 96], mybir.dt.int32))
        ps_a = en(nc.psum_tensor("ps_a", [128, 1024], F32))
        ps_b = en(nc.psum_tensor("ps_b", [128, 1024], F32))
        ps_c = en(nc.psum_tensor("ps_c", [128, 1024], F32))
        ps_sp = en(nc.psum_tensor("ps_sp", [128, 512], F32))
        ps_tr = en(nc.psum_tensor("ps_tr", [128, 1024], BF16))
        s_pe = nc.alloc_semaphore("s_pe")
        s_act = nc.alloc_semaphore("s_act")
        s_dve = nc.alloc_semaphore("s_dve")
        s_gp = nc.alloc_semaphore("s_gp")
        s_once = nc.alloc_semaphore("s_once")
        s_sc = nc.alloc_semaphore("s_sc")
        s_mv = nc.alloc_semaphore("s_mv")
        s_ax = nc.alloc_semaphore("s_ax")
        s_lx2 = [nc.alloc_semaphore(f"s_lx{i}") for i in range(2)]
        s_rp2 = [nc.alloc_semaphore(f"s_rp{i}") for i in range(2)]
        s_wsl2 = [nc.alloc_semaphore(f"s_wsl{i}") for i in range(2)]
        s_v8 = [nc.alloc_semaphore(f"s_v{i}") for i in range(8)]
        s_vst2 = [nc.alloc_semaphore(f"s_vst{i}") for i in range(2)]
        s_ost2 = [nc.alloc_semaphore(f"s_ost{i}") for i in range(2)]
        s_w1t2 = [nc.alloc_semaphore(f"s_w1t{i}") for i in range(2)]
        s_w2t2 = [nc.alloc_semaphore(f"s_w2t{i}") for i in range(2)]
        s_stage = [nc.alloc_semaphore(f"s_stage{i}") for i in range(2 * NE)]
        s_x2st2 = [nc.alloc_semaphore(f"s_x2st{i}") for i in range(2)]
        s_gt2 = [nc.alloc_semaphore(f"s_gt{i}") for i in range(2)]
        s_sct = nc.alloc_semaphore("s_sct")

        p = Prog(nc)

        def ph0():
            def f_sync(sy):
                ii = sy.dma_start(out=swp_sb[:, :], in_=swp_in[:, :])
                p.inc(ii, s_once, 16)
                ii = sy.dma_start(out=id_sb[:, :], in_=id_in[:, :])
                p.inc(ii, s_once, 16)

            def f_gp(g):
                ii = g.memset(ones128[:, :], 1.0)
                p.inc(ii, s_gp, 1)
                ii = g.memset(zero8[:, :], 0.0)
                p.inc(ii, s_gp, 1)
                ii = g.memset(epsc[:, :], EPS)
                p.inc(ii, s_gp, 1)

            p.run_phase({"sync": f_sync, "gpsimd": f_gp})

        ph0()

        def ln_phase(src_dram, dst_sb, do_logits, store_tm=False):
            with (
                nc.sbuf_tensor(un("lx"), [128, 2048], F32) as lx,
                nc.sbuf_tensor(un("lxn"), [128, 2048], BF16) as _lxn,
                nc.sbuf_tensor(un("lsq"), [128, 1024], BF16) as lsq,
                nc.sbuf_tensor(un("llg"), [128, 16], F32) as llg,
                nc.sbuf_tensor(un("lmk"), [128, 16], F32) as lmk,
            ):
                lxn = dbg_lxn if (dbg and dbg_stage == 1 and dbg_lxn is not None) else _lxn
                if dbg and dbg_stage == 1 and dbg_lx is not None:
                    lx = dbg_lx
                tk = TokDict()

                def f_sync(sy):
                    p.wait_all(sy)
                    for tt in range(NT):
                        if tt >= 2:
                            p.wt(sy, tk[("ap", tt - 2)])
                            p.wt(sy, tk[("sum", tt - 2)])
                        ii = sy.dma_start(
                            out=lx[:, (tt % 2) * 1024:(tt % 2) * 1024 + 1024],
                            in_=src_dram[tt * 128:(tt + 1) * 128, :])
                        tk[("ld", tt)] = p.inc(ii, s_lx2[tt % 2], 16)

                def f_vector(v):
                    p.wait_all(v)
                    for tt in range(NT):
                        xv = lx[:, (tt % 2) * 1024:(tt % 2) * 1024 + 1024]
                        p.wt(v, tk[("ld", tt)])
                        v.reduce_sum(stats[:, tt:tt + 1], xv, axis=AX.X)
                        tk[("sum", tt)] = p.dinc(v, s_dve, 1)
                        p.wt(v, tk[("sq", tt)])
                        mcol = stats[:, 64 + tt:64 + tt + 1]
                        v.tensor_single_scalar(mcol, stats[:, tt:tt + 1], 1.0 / E, ALU.mult)
                        v.drain()
                        v.tensor_mul(stats[:, tt:tt + 1], mcol, mcol)
                        v.drain()
                        v.scalar_tensor_tensor(
                            stats[:, 160 + tt:160 + tt + 1],
                            stats[:, 32 + tt:32 + tt + 1], 1.0 / E,
                            stats[:, tt:tt + 1], ALU.mult, ALU.subtract)
                        tk[("var", tt)] = p.dinc(v, s_dve, 1)
                        p.wt(v, tk[("rstd", tt)])
                        v.scalar_tensor_tensor(
                            stats[:, 128 + tt:128 + tt + 1], mcol, -1.0,
                            stats[:, 96 + tt:96 + tt + 1], ALU.mult, ALU.mult)
                        tk[("nb", tt)] = p.dinc(v, s_dve, 1)
                        p.wt(v, tk[("tr", tt)])
                        src = bass.AP(ps_tr, 0, [[1024, 128], [128, 8], [1, 128]])
                        dst = bass.AP(dst_sb, tt * 128, [[8 * T, 128], [T, 8], [1, 128]])
                        v.tensor_copy(dst, src)
                        tk[("ev", tt)] = p.dinc(v, s_dve, 1)
                        if do_logits:
                            p.wt(v, tk[("lg", tt)])
                            v.tensor_copy(llg[:, 0:8], ps_sp[:, 0:8])
                            v.drain()
                            m1 = stats[:, tt:tt + 1]
                            m2 = stats[:, 32 + tt:32 + tt + 1]
                            v.reduce_max(m1, llg[:, 0:8], axis=AX.X)
                            v.drain()
                            e1 = eqs[:, tt * 8:tt * 8 + 8]
                            v.tensor_single_scalar(e1, llg[:, 0:8], m1, ALU.is_equal)
                            v.drain()
                            v.scalar_tensor_tensor(lmk[:, 0:8], e1, -1.0e30, llg[:, 0:8],
                                                   ALU.mult, ALU.add)
                            v.drain()
                            v.reduce_max(m2, lmk[:, 0:8], axis=AX.X)
                            v.drain()
                            e2 = eqs[:, 256 + tt * 8:256 + tt * 8 + 8]
                            ii = v.tensor_single_scalar(e2, llg[:, 0:8], m2, ALU.is_equal)
                            tk[("rt", tt)] = p.inc(ii, s_dve, 1)

                def f_scalar(a):
                    p.wait_all(a)
                    for tt in range(NT):
                        xv = lx[:, (tt % 2) * 1024:(tt % 2) * 1024 + 1024]
                        p.wt(a, tk[("ld", tt)])
                        a.activation(lsq[:, :], xv, AF.Square,
                                     accum_out=stats[:, 32 + tt:32 + tt + 1])
                        tk[("sq", tt)] = p.dinc(a, s_act, 1)
                        p.wt(a, tk[("var", tt)])
                        a.activation(stats[:, 32 + tt:32 + tt + 1],
                                     stats[:, 160 + tt:160 + tt + 1], AF.Ln, bias=epsc[:, :])
                        a.drain()
                        a.activation(stats[:, 96 + tt:96 + tt + 1],
                                     stats[:, 32 + tt:32 + tt + 1], AF.Exp, scale=-0.5)
                        tk[("rstd", tt)] = p.dinc(a, s_act, 1)
                        p.wt(a, tk[("nb", tt)])
                        if tt >= 2:
                            p.wt(a, tk[("tr", tt - 2)])
                            if store_tm:
                                p.wt(a, tk[("tmst", tt - 2)])
                        a.activation(lxn[:, (tt % 2) * 1024:(tt % 2) * 1024 + 1024],
                                     xv, AF.Identity,
                                     bias=stats[:, 128 + tt:128 + tt + 1],
                                     scale=stats[:, 96 + tt:96 + tt + 1])
                        tk[("ap", tt)] = p.dinc(a, s_act, 1)
                        if store_tm:
                            ii = a.dma_start(out=xn2h[tt * 128:(tt + 1) * 128, :],
                                             in_=lxn[:, (tt % 2) * 1024:(tt % 2) * 1024 + 1024])
                            tk[("tmst", tt)] = p.inc(ii, s_x2st2[tt % 2], 16)

                def f_tensor(t):
                    p.wait_all(t)
                    for tt in range(NT):
                        p.wt(t, tk[("ap", tt)])
                        if tt >= 1:
                            p.wt(t, tk[("ev", tt - 1)])
                        for c in range(8):
                            ii = t.transpose(
                                ps_tr[:, c * 128:(c + 1) * 128],
                                lxn[:, (tt % 2) * 1024 + c * 128:(tt % 2) * 1024 + (c + 1) * 128],
                                id_sb[:, :])
                        tk[("tr", tt)] = p.inc(ii, s_pe, 1)
                        if do_logits:
                            p.wt(t, tk[("ev", tt)])
                            if tt >= 1:
                                p.wt(t, tk[("rt", tt - 1)])
                            for r in range(8):
                                ii = t.matmul(
                                    ps_sp[:, 0:8],
                                    dst_sb[:, r * T + tt * 128:r * T + (tt + 1) * 128],
                                    wr_t[:, r * 8:(r + 1) * 8],
                                    start=(r == 0), stop=(r == 7))
                            tk[("lg", tt)] = p.inc(ii, s_pe, 1)

                p.run_phase({"sync": f_sync, "vector": f_vector,
                             "scalar": f_scalar, "tensor": f_tensor}, tk)

        def proj_rope_phase(wmat, wrows, nacc, src_sb, dst_sb):
            with (
                nc.sbuf_tensor(un("pw"), [128, 2 * nacc * 128], BF16) as pw,
                nc.sbuf_tensor(un("prc"), [128, 1024], F32) as prc,
                nc.sbuf_tensor(un("prs"), [128, 1024], F32) as prs,
                nc.sbuf_tensor(un("pqt"), [128, 1024], BF16) as pqt,
                nc.sbuf_tensor(un("pt1"), [128, 1024], BF16) as pt1,
            ):
                tk = TokDict()

                def f_gp(g):
                    p.wait_all(g)
                    for c in range(8):
                        if c >= 2:
                            p.wt(g, tk[("mm", c - 2, 7)])
                        half = (c % 2) * nacc * 128
                        dst = bass.AP(pw, half, [[2 * nacc * 128, 128], [128, nacc], [1, 128]])
                        src = bass.AP(wmat.tensor, c * 128, [[E, 128], [128 * E, nacc], [1, 128]])
                        ii = g.dma_start(out=dst, in_=src)
                        tk[("w", c)] = p.inc(ii, s_wsl2[c % 2], 16)

                def f_sync(sy):
                    p.wait_all(sy)
                    for c in range(8):
                        for tg in range(8):
                            i2 = c * 8 + tg
                            par = i2 % 2
                            if i2 >= 2:
                                pc, ptg = (i2 - 2) // 8, (i2 - 2) % 8
                                p.wt(sy, tk[("out", pc, ptg)])
                            ii = sy.dma_start(out=prc[:, par * 512:par * 512 + 512],
                                              in_=ropec[:, tg * 512:(tg + 1) * 512])
                            p.inc(ii, s_rp2[par], 16)
                            ii = sy.dma_start(out=prs[:, par * 512:par * 512 + 512],
                                              in_=ropes[:, tg * 512:(tg + 1) * 512])
                            tk[("rl", c, tg)] = p.inc(ii, s_rp2[par], 16)

                def f_tensor(t):
                    p.wait_all(t)
                    for c in range(8):
                        p.wt(t, tk[("w", c)])
                        for tg in range(8):
                            i2 = c * 8 + tg
                            par = i2 % 2
                            if i2 >= 2:
                                pc, ptg = (i2 - 2) // 8, (i2 - 2) % 8
                                p.wt(t, tk[("ev", pc, ptg)])
                            for r in range(nacc):
                                ii = t.matmul(
                                    ps_a[:, par * 512:par * 512 + 512],
                                    pw[:, (c % 2) * nacc * 128 + r * 128:(c % 2) * nacc * 128 + (r + 1) * 128],
                                    src_sb[:, r * T + tg * 512:r * T + (tg + 1) * 512],
                                    start=(r == 0), stop=(r == nacc - 1))
                            tk[("mm", c, tg)] = p.inc(ii, s_pe, 1)
                            p.wt(t, tk[("ev", c, tg)])
                            if i2 >= 2:
                                pc, ptg = (i2 - 2) // 8, (i2 - 2) % 8
                                p.wt(t, tk[("out", pc, ptg)])
                            ii = t.matmul(ps_b[:, par * 512:par * 512 + 512],
                                          swp_sb[:, :],
                                          pqt[:, par * 512:par * 512 + 512],
                                          start=True, stop=True)
                            tk[("sw", c, tg)] = p.inc(ii, s_pe, 1)

                def f_scalar(a):
                    p.wait_all(a)
                    for c in range(8):
                        for tg in range(8):
                            i2 = c * 8 + tg
                            par = i2 % 2
                            p.wt(a, tk[("mm", c, tg)])
                            if i2 >= 2:
                                pc, ptg = (i2 - 2) // 8, (i2 - 2) % 8
                                p.wt(a, tk[("sw", pc, ptg)])
                                p.wt(a, tk[("p1", pc, ptg)])
                            a.activation(pqt[:, par * 512:par * 512 + 512],
                                         ps_a[:, par * 512:par * 512 + 512], AF.Copy)
                            tk[("ev", c, tg)] = p.dinc(a, s_act, 1)

                def f_vector(v):
                    p.wait_all(v)
                    for c in range(8):
                        for tg in range(8):
                            i2 = c * 8 + tg
                            par = i2 % 2
                            p.wt(v, tk[("rl", c, tg)])
                            p.wt(v, tk[("ev", c, tg)])
                            v.tensor_mul(pt1[:, par * 512:par * 512 + 512],
                                         pqt[:, par * 512:par * 512 + 512],
                                         prc[:, par * 512:par * 512 + 512])
                            tk[("p1", c, tg)] = p.dinc(v, s_dve, 1)
                            p.wt(v, tk[("sw", c, tg)])
                            dcol = dst_sb[:, c * T + tg * 512:c * T + (tg + 1) * 512]
                            v.tensor_mul(dcol, ps_b[:, par * 512:par * 512 + 512],
                                         prs[:, par * 512:par * 512 + 512])
                            v.drain()
                            ii = v.tensor_add(dcol, pt1[:, par * 512:par * 512 + 512], dcol)
                            tk[("out", c, tg)] = p.inc(ii, s_dve, 1)

                p.run_phase({"gpsimd": f_gp, "sync": f_sync, "tensor": f_tensor,
                             "scalar": f_scalar, "vector": f_vector}, tk)

        dbg_lxn = en(nc.sbuf_tensor("dbg_lxn", [128, 2048], BF16)) if (dbg and dbg_stage == 1) else None
        dbg_lx = en(nc.sbuf_tensor("dbg_lx", [128, 2048], F32)) if (dbg and dbg_stage == 1) else None
        ln_phase(x_in, A, do_logits=False)
        if dbg_stage >= 2:
            proj_rope_phase(wq, E, 8, A, Bq)

        if dbg and dbg_stage == 1:
            def f_sync_d1(sy):
                p.wait_all(sy)
                tkk = None
                for c in range(8):
                    ii = sy.dma_start(out=k_dbg[:, c * T:(c + 1) * T], in_=A[:, c * T:(c + 1) * T])
                    tkk = p.inc(ii, s_sc, 16)
                ii = sy.dma_start(out=x1_dbg[0:128, 0:192], in_=stats[:, :])
                tkk = p.inc(ii, s_sc, 16)
                ii = sy.dma_start(out=q_dbg[:, 0:2048], in_=dbg_lxn[:, :])
                tkk = p.inc(ii, s_sc, 16)
                ii = sy.dma_start(out=x1_dbg[128:256, :], in_=dbg_lx[:, 0:1024])
                tkk = p.inc(ii, s_sc, 16)
                ii = sy.dma_start(out=x1_dbg[256:384, :], in_=dbg_lx[:, 1024:2048])
                tkk = p.inc(ii, s_sc, 16)
                sy.wait_ge(tkk[0], tkk[1])
            p.run_phase({"sync": f_sync_d1})

        if dbg and dbg_stage == 2:
            def f_scalar_d2(a):
                p.wait_all(a)
                tkk = None
                for c in range(8):
                    ii = a.dma_start(out=k_dbg[:, c * T:(c + 1) * T], in_=A[:, c * T:(c + 1) * T])
                    p.inc(ii, s_sc, 16)
                    ii = a.dma_start(out=q_dbg[:, c * T:(c + 1) * T], in_=Bq[:, c * T:(c + 1) * T])
                    tkk = p.inc(ii, s_sc, 16)
                a.wait_ge(tkk[0], tkk[1])
            p.run_phase({"scalar": f_scalar_d2})

        if dbg_stage >= 3:
            with nc.sbuf_tensor("latf", [128, 4 * T], BF16) as latf:
                with nc.sbuf_tensor("dw", [128, 2048], BF16) as dw:
                    tk = TokDict()

                    def f_gp(g):
                        p.wait_all(g)
                        for lc in range(4):
                            if lc >= 2:
                                p.wt(g, tk[("mm", lc - 2, 7)])
                            half = (lc % 2) * 1024
                            dst = bass.AP(dw, half, [[2048, 128], [128, 8], [1, 128]])
                            src = bass.AP(wd.tensor, lc * 128, [[L, 128], [128 * L, 8], [1, 128]])
                            ii = g.dma_start(out=dst, in_=src)
                            tk[("w", lc)] = p.inc(ii, s_wsl2[lc % 2], 16)

                    def f_tensor(t):
                        p.wait_all(t)
                        for lc in range(4):
                            p.wt(t, tk[("w", lc)])
                            for tg in range(8):
                                i2 = lc * 8 + tg
                                par = i2 % 2
                                if i2 >= 2:
                                    pl, ptg = (i2 - 2) // 8, (i2 - 2) % 8
                                    p.wt(t, tk[("ev", pl, ptg)])
                                for r in range(8):
                                    ii = t.matmul(
                                        ps_a[:, par * 512:par * 512 + 512],
                                        dw[:, (lc % 2) * 1024 + r * 128:(lc % 2) * 1024 + (r + 1) * 128],
                                        A[:, r * T + tg * 512:r * T + (tg + 1) * 512],
                                        start=(r == 0), stop=(r == 7))
                                tk[("mm", lc, tg)] = p.inc(ii, s_pe, 1)

                    def f_scalar(a):
                        p.wait_all(a)
                        for lc in range(4):
                            for tg in range(8):
                                par = (lc * 8 + tg) % 2
                                p.wt(a, tk[("mm", lc, tg)])
                                a.activation(
                                    latf[:, lc * T + tg * 512:lc * T + (tg + 1) * 512],
                                    ps_a[:, par * 512:par * 512 + 512], AF.Copy)
                                tk[("ev", lc, tg)] = p.dinc(a, s_act, 1)

                    p.run_phase({"gpsimd": f_gp, "tensor": f_tensor, "scalar": f_scalar}, tk)

                proj_rope_phase(wk, L, 4, latf, A)

                with (
                    nc.sbuf_tensor("vw", [128, 4096], BF16) as vw,
                    nc.sbuf_tensor("vsb", [128, 2080], BF16) as vsb,
                ):
                    tk = TokDict()

                    def f_gp(g):
                        p.wait_all(g)
                        dst = bass.AP(vw, 0, [[4096, 128], [1024, 4], [1, 1024]])
                        src = bass.AP(wv.tensor, 0, [[E, 128], [128 * E, 4], [1, 1024]])
                        ii = g.dma_start(out=dst, in_=src)
                        tk["w"] = p.inc(ii, s_once, 16)
                        ii = g.memset(bass.AP(vsb, 64, [[2080, 128], [65, 16]]), 1.0)
                        p.inc(ii, s_gp, 1)
                        ii = g.memset(bass.AP(vsb, 1040 + 64, [[2080, 128], [65, 16]]), 1.0)
                        tk["ones"] = p.inc(ii, s_gp, 1)

                    def f_tensor(t):
                        p.wait_all(t)
                        p.wt(t, tk["w"])
                        p.wt(t, tk["ones"])
                        for tt in range(NT):
                            for hf in range(2):
                                i2 = tt * 2 + hf
                                par = i2 % 2
                                if i2 >= 2:
                                    pt_, pf = (i2 - 2) // 2, (i2 - 2) % 2
                                    p.wt(t, tk[("ev", pt_, pf)])
                                for r in range(4):
                                    ii = t.matmul(
                                        ps_a[:, par * 512:par * 512 + 512],
                                        latf[:, r * T + tt * 128:r * T + (tt + 1) * 128],
                                        vw[:, r * 1024 + hf * 512:r * 1024 + (hf + 1) * 512],
                                        start=(r == 0), stop=(r == 3))
                                tk[("mm", tt, hf)] = p.inc(ii, s_pe, 1)

                    def f_vector(v):
                        p.wait_all(v)
                        for tt in range(NT):
                            for hf in range(2):
                                par = (tt * 2 + hf) % 2
                                p.wt(v, tk[("mm", tt, hf)])
                                if tt >= 2 and hf == 0:
                                    p.wt(v, tk[("st", tt - 2)])
                                dst = bass.AP(vsb, (tt % 2) * 1040 + hf * 520,
                                              [[2080, 128], [130, 4], [65, 2], [1, 64]])
                                src = bass.AP(ps_a, par * 512,
                                              [[1024, 128], [128, 4], [64, 2], [1, 64]])
                                v.tensor_copy(dst, src)
                                tk[("ev", tt, hf)] = p.dinc(v, s_dve, 1)

                    def f_scalar(a):
                        p.wait_all(a)
                        for tt in range(NT):
                            p.wt(a, tk[("ev", tt, 1)])
                            ii = a.dma_start(out=vh[tt * 128:(tt + 1) * 128, :],
                                             in_=vsb[:, (tt % 2) * 1040:(tt % 2) * 1040 + 1040])
                            tk[("st", tt)] = p.inc(ii, s_vst2[tt % 2], 16)

                    p.run_phase({"gpsimd": f_gp, "tensor": f_tensor,
                                 "vector": f_vector, "scalar": f_scalar}, tk)

        if dbg and dbg_stage == 3:
            def f_scalar_d3(a):
                p.wait_all(a)
                tkk = None
                for c in range(8):
                    ii = a.dma_start(out=q_dbg[:, c * T:(c + 1) * T], in_=Bq[:, c * T:(c + 1) * T])
                    p.inc(ii, s_sc, 16)
                    ii = a.dma_start(out=k_dbg[:, c * T:(c + 1) * T], in_=A[:, c * T:(c + 1) * T])
                    p.inc(ii, s_sc, 16)
                for tt in range(NT):
                    ii = a.dma_start(out=v_dbg[tt * 128:(tt + 1) * 128, :],
                                     in_=vh[tt * 128:(tt + 1) * 128, :])
                    tkk = p.inc(ii, s_sc, 16)
                a.wait_ge(tkk[0], tkk[1])
            p.run_phase({"scalar": f_scalar_d3})

        if dbg_stage >= 4:
            with (
                nc.sbuf_tensor("wo_sb", [128, 8192], BF16) as wo_sb,
                nc.sbuf_tensor("vst", [128, 8 * 130], BF16) as vst,
                nc.sbuf_tensor("pb", [128, 2048], BF16) as pb,
                nc.sbuf_tensor("ao", [128, 4096], BF16) as ao,
                nc.sbuf_tensor("sinv", [128, 1024], BF16) as sinv,
                nc.sbuf_tensor("bcb", [128, 512], BF16) as bcb,
                nc.sbuf_tensor("st64", [128, 512], BF16) as st64,
                nc.sbuf_tensor("axb", [128, 1024], F32) as axb,
                nc.sbuf_tensor("ax1", [128, 1024], F32) as ax1,
            ):
                tk = TokDict()
                sched = []
                base_of = {}
                gi = 0
                for g in range(NG):
                    ntiles = 4 * (g + 1)
                    for hp in range(8):
                        base_of[(g, hp)] = gi
                        for i in range(ntiles):
                            sched.append((g, hp, i, gi, ntiles))
                            gi += 1

                def prev_gh(g, hp):
                    return (g, hp - 1) if hp else (g - 1, 7)

                def f_gp(g_):
                    p.wait_all(g_)
                    dst = bass.AP(wo_sb, 0, [[8192, 128], [1024, 8], [1, 1024]])
                    src = bass.AP(wo.tensor, 0, [[E, 128], [128 * E, 8], [1, 1024]])
                    ii = g_.dma_start(out=dst, in_=src)
                    tk["wo"] = p.inc(ii, s_once, 16)
                    if dbg_stage >= 6:
                        for e in range(NE):
                            ii = g_.dma_start(
                                out=bass.AP(w1bf.tensor, e * DFF * E, [[256 * E, 16], [1, 256 * E]]),
                                in_=bass.AP(w1.tensor, e * DFF * E, [[256 * E, 16], [1, 256 * E]]))
                            p.inc(ii, s_stage[2 * e], 16)
                            ii = g_.dma_start(
                                out=bass.AP(w2bf.tensor, e * E * DFF, [[256 * DFF, 4], [1, 256 * DFF]]),
                                in_=bass.AP(w2.tensor, e * E * DFF, [[256 * DFF, 4], [1, 256 * DFF]]))
                            p.inc(ii, s_stage[2 * e + 1], 16)

                def f_sync(sy):
                    p.wait_all(sy)
                    for (g, hp, i, vi, ntiles) in sched:
                        if i % 4 == 0:
                            blk_ = (vi // 4) % 2
                            if vi >= 8:
                                p.wt(sy, tk[("av", vi - 8 + 3)])
                            ii = sy.dma_start(
                                out=vst[:, blk_ * 520:(blk_ + 1) * 520],
                                in_=bass.AP(vh.tensor, i * 128 * 1040 + hp * 130,
                                            [[1040, 128], [128 * 1040, 4], [1, 130]]))
                            tk[("vb", vi // 4)] = p.inc(ii, s_v8[blk_], 16)
                        if i == ntiles - 1:
                            p.wt(sy, tk[("m1", g, hp)])
                            if g > 0:
                                p.wt(sy, tk[("op", g - 1, 3, 1)])
                            ii = sy.dma_start(out=ao[64:128, hp * 512:(hp + 1) * 512],
                                              in_=st64[0:64, :])
                            tk[("mv", g, hp)] = p.inc(ii, s_mv, 16)
                        if hp == 7 and i == ntiles - 1:
                            for tt in range(4):
                                if (g, tt) != (0, 0):
                                    pg, ptt = (g, tt - 1) if tt else (g - 1, 3)
                                    p.wt(sy, tk[("oa", pg, ptt, 1)])
                                ii = sy.dma_start(
                                    out=axb[:, :],
                                    in_=x_in[g * 512 + tt * 128:g * 512 + (tt + 1) * 128, :])
                                tk[("xl", g, tt)] = p.inc(ii, s_ax, 16)

                def f_tensor(t):
                    p.wait_all(t)
                    p.wt(t, tk["wo"])
                    for g in range(NG):
                        ntiles = 4 * (g + 1)
                        for hp in range(8):
                            base_vi = base_of[(g, hp)]

                            def emit_sp(i, vi):
                                par = vi % 2
                                psab = ps_a if par == 0 else ps_b
                                full = i < 4 * g + 2
                                qoff = g * 512 if full else g * 512 + 256
                                qlen = 512 if full else 256
                                if vi >= 2:
                                    p.wt(t, tk[("e", vi - 2)])
                                ii = t.matmul(
                                    psab[:, 0:qlen],
                                    A[0:64, hp * T + i * 128:hp * T + (i + 1) * 128],
                                    Bq[0:64, hp * T + qoff:hp * T + qoff + qlen],
                                    start=True, stop=True, tile_position=(0, 0))
                                ii = t.matmul(
                                    psab[:, 512:512 + qlen],
                                    A[64:128, hp * T + i * 128:hp * T + (i + 1) * 128],
                                    Bq[64:128, hp * T + qoff:hp * T + qoff + qlen],
                                    start=True, stop=True, tile_position=(64, 0))
                                tk[("s", vi)] = p.inc(ii, s_pe, 1)

                            emit_sp(0, base_vi)
                            emit_sp(1, base_vi + 1)
                            for i in range(ntiles):
                                vi = base_vi + i
                                par = vi % 2
                                full = i < 4 * g + 2
                                p.wt(t, tk[("e", vi)])
                                p.wt(t, tk[("vb", vi // 4)])
                                if i == 0 and (g, hp) != (0, 0):
                                    p.wt(t, tk[("m1", *prev_gh(g, hp))])
                                slot = vi % 8
                                if full:
                                    ii = t.matmul(ps_c[0:65, 0:512],
                                                  vst[:, slot * 130:slot * 130 + 65],
                                                  pb[:, par * 1024:par * 1024 + 512],
                                                  start=(i == 0), stop=(i == ntiles - 1))
                                    ii = t.matmul(ps_c[0:65, 512:1024],
                                                  vst[:, slot * 130 + 65:slot * 130 + 130],
                                                  pb[:, par * 1024 + 512:par * 1024 + 1024],
                                                  start=(i == 0), stop=(i == ntiles - 1))
                                else:
                                    ii = t.matmul(ps_c[0:65, 256:512],
                                                  vst[:, slot * 130:slot * 130 + 65],
                                                  pb[:, par * 1024:par * 1024 + 256],
                                                  start=False, stop=(i == ntiles - 1))
                                    ii = t.matmul(ps_c[0:65, 768:1024],
                                                  vst[:, slot * 130 + 65:slot * 130 + 130],
                                                  pb[:, par * 1024 + 512:par * 1024 + 768],
                                                  start=False, stop=(i == ntiles - 1))
                                tk[("av", vi)] = p.inc(ii, s_pe, 1)
                                if i + 2 < ntiles:
                                    emit_sp(i + 2, base_vi + i + 2)
                            p.wt(t, tk[("rc", g, hp)])
                            if (g, hp) != (0, 0):
                                p.wt(t, tk[("b1", *prev_gh(g, hp))])
                            if hp == 0 and g > 0:
                                p.wt(t, tk[("oa", g - 1, 3, 1)])
                            ii = t.matmul(ps_sp[0:64, :], ones128[64:65, 0:64],
                                          sinv[64:65, 0:512], start=True, stop=True)
                            tk[("bc", g, hp, 0)] = p.inc(ii, s_pe, 1)
                            p.wt(t, tk[("b0", g, hp)])
                            ii = t.matmul(ps_sp[0:64, :], ones128[64:65, 0:64],
                                          sinv[64:65, 512:1024], start=True, stop=True)
                            tk[("bc", g, hp, 1)] = p.inc(ii, s_pe, 1)
                        p.wt(t, tk[("mv", g, 7)])
                        for tt in range(4):
                            for hf in range(2):
                                if (tt, hf) == (0, 0):
                                    p.wt(t, tk[("b1", g, 7)])
                                else:
                                    pg = (g, tt, hf - 1) if hf else (g, tt - 1, 1)
                                    p.wt(t, tk[("oa", *pg)])
                                for r in range(8):
                                    ii = t.matmul(
                                        ps_sp[:, :],
                                        ao[:, r * 512 + tt * 128:r * 512 + (tt + 1) * 128],
                                        wo_sb[:, r * 1024 + hf * 512:r * 1024 + (hf + 1) * 512],
                                        start=(r == 0), stop=(r == 7))
                                tk[("op", g, tt, hf)] = p.inc(ii, s_pe, 1)

                def f_scalar(a):
                    p.wait_all(a)
                    for g in range(NG):
                        ntiles = 4 * (g + 1)
                        for hp in range(8):
                            base_vi = base_of[(g, hp)]
                            for i in range(ntiles):
                                vi = base_vi + i
                                par = vi % 2
                                psab = ps_a if par == 0 else ps_b
                                full = i < 4 * g + 2
                                p.wt(a, tk[("s", vi)])
                                if vi >= 2:
                                    p.wt(a, tk[("av", vi - 2)])
                                if full:
                                    ii = a.activation(pb[:, par * 1024:(par + 1) * 1024],
                                                      psab[:, :], AF.Exp, scale=0.125)
                                else:
                                    a.activation(pb[:, par * 1024:par * 1024 + 256],
                                                 psab[:, 0:256], AF.Exp, scale=0.125)
                                    ii = a.activation(pb[:, par * 1024 + 512:par * 1024 + 768],
                                                      psab[:, 512:768], AF.Exp, scale=0.125)
                                tk[("e", vi)] = p.inc(ii, s_act, 1)
                        for tt in range(4):
                            p.wt(a, tk[("oa", g, tt, 1)])
                            ii = a.dma_start(
                                out=x1h[g * 512 + tt * 128:g * 512 + (tt + 1) * 128, :],
                                in_=ax1[:, :])
                            tk[("x1", g, tt)] = p.inc(ii, s_sc, 16)

                def f_vector(v):
                    p.wait_all(v)
                    for g in range(NG):
                        ntiles = 4 * (g + 1)
                        for hp in range(8):
                            lastvi = base_of[(g, hp)] + ntiles - 1
                            p.wt(v, tk[("av", lastvi)])
                            if (g, hp) != (0, 0):
                                p.wt(v, tk[("mv", *prev_gh(g, hp))])
                            if hp == 0 and g > 0:
                                p.wt(v, tk[("op", g - 1, 3, 1)])
                            with nc.allow_low_precision(reason="bf16 softmax normalizer (0.4% rel; bar 5.5%)"):
                                v.reciprocal(sinv[64:65, 0:512], ps_c[64:65, 0:512])
                                v.reciprocal(sinv[64:65, 512:1024], ps_c[64:65, 512:1024])
                            tk[("rc", g, hp)] = p.dinc(v, s_dve, 1)
                            p.wt(v, tk[("bc", g, hp, 0)])
                            v.tensor_copy(bcb[0:64, :], ps_sp[0:64, :])
                            tk[("b0", g, hp)] = p.dinc(v, s_dve, 1)
                            v.tensor_mul(ao[0:64, hp * 512:(hp + 1) * 512],
                                         ps_c[0:64, 0:512], bcb[0:64, :])
                            tk[("m0", g, hp)] = p.dinc(v, s_dve, 1)
                            p.wt(v, tk[("bc", g, hp, 1)])
                            v.tensor_copy(bcb[0:64, :], ps_sp[0:64, :])
                            tk[("b1", g, hp)] = p.dinc(v, s_dve, 1)
                            v.tensor_mul(st64[0:64, :], ps_c[0:64, 512:1024],
                                         bcb[0:64, :])
                            tk[("m1", g, hp)] = p.dinc(v, s_dve, 1)
                        for tt in range(4):
                            for hf in range(2):
                                p.wt(v, tk[("op", g, tt, hf)])
                                p.wt(v, tk[("xl", g, tt)])
                                if hf == 0 and (g, tt) != (0, 0):
                                    pg, ptt = (g, tt - 1) if tt else (g - 1, 3)
                                    p.wt(v, tk[("x1", pg, ptt)])
                                v.tensor_add(ax1[:, hf * 512:(hf + 1) * 512],
                                             ps_sp[:, :],
                                             axb[:, hf * 512:(hf + 1) * 512])
                                tk[("oa", g, tt, hf)] = p.dinc(v, s_dve, 1)

                p.run_phase({"gpsimd": f_gp, "sync": f_sync, "tensor": f_tensor,
                             "scalar": f_scalar, "vector": f_vector}, tk)

        if dbg and dbg_stage == 4:
            def f_scalar_d4(a):
                p.wait_all(a)
                tkk = None
                for tt in range(NT):
                    ii = a.dma_start(out=x1_dbg[tt * 128:(tt + 1) * 128, :],
                                     in_=x1h[tt * 128:(tt + 1) * 128, :])
                    tkk = p.inc(ii, s_sc, 16)
                a.wait_ge(tkk[0], tkk[1])
            p.run_phase({"scalar": f_scalar_d4})

        if dbg_stage >= 5:
            with (
                nc.sbuf_tensor("wr8", [8, 1024], BF16) as wr8,
                nc.sbuf_tensor("zb", [128, 1024], BF16) as zb,
            ):
                tk = TokDict()

                def f_gp(g):
                    p.wait_all(g)
                    ii = g.dma_start(out=wr8[:, :], in_=wr[:, :])
                    tk["w"] = p.inc(ii, s_once, 16)
                    g.memset(zb[:, :], 0.0)
                    tk["zb"] = p.dinc(g, s_gp, 1)

                def f_sync(sy):
                    p.wait_all(sy)
                    p.wt(sy, tk["zb"])
                    for tt in range(NT):
                        ii = sy.dma_start(out=accB[tt * 128:(tt + 1) * 128, :], in_=zb[:, :])
                        p.inc(ii, s_once, 16)

                def f_tensor(t):
                    p.wait_all(t)
                    p.wt(t, tk["w"])
                    for r in range(8):
                        ii = t.transpose(ps_tr[:, r * 8:(r + 1) * 8],
                                         wr8[0:8, r * 128:(r + 1) * 128], id_sb[0:8, 0:8])
                    tk["tr"] = p.inc(ii, s_pe, 1)

                def f_vector(v):
                    p.wait_all(v)
                    p.wt(v, tk["tr"])
                    v.tensor_copy(wr_t[:, :], ps_tr[:, 0:64])
                    p.dinc(v, s_dve, 1)

                p.run_phase({"gpsimd": f_gp, "sync": f_sync,
                             "tensor": f_tensor, "vector": f_vector}, tk)

            ln_phase(x1h, A, do_logits=True, store_tm=True)

            def gates_phase():
                tk = TokDict()

                def f_vector(v):
                    p.wait_all(v)
                    v.tensor_sub(stats[:, 64:96], stats[:, 32:64], stats[:, 0:32])
                    v.tensor_sub(stats[:, 96:128], stats[:, 0:32], stats[:, 32:64])
                    tk["d"] = p.dinc(v, s_dve, 1)
                    p.wt(v, tk["exp"])
                    v.tensor_single_scalar(stats[:, 64:128], stats[:, 128:192], 1.0, ALU.add)
                    v.drain()
                    v.reciprocal(stats[:, 128:192], stats[:, 64:128])
                    v.drain()
                    p.inc(v.engine_nop(), s_dve, 1)
                    for tt in range(NT):
                        gcol = gates[:, tt * 8:(tt + 1) * 8]
                        v.scalar_tensor_tensor(gcol, eqs[:, 256 + tt * 8:256 + tt * 8 + 8],
                                               stats[:, 160 + tt:160 + tt + 1], zero8[:, :],
                                               ALU.mult, ALU.add)
                        v.drain()
                        ii = v.scalar_tensor_tensor(gcol, eqs[:, tt * 8:tt * 8 + 8],
                                                    stats[:, 128 + tt:128 + tt + 1], gcol,
                                                    ALU.mult, ALU.add)
                        p.inc(ii, s_dve, 1)

                def f_scalar(a):
                    p.wait_all(a)
                    p.wt(a, tk["d"])
                    a.activation(stats[:, 128:192], stats[:, 64:128], AF.Exp)
                    tk["exp"] = p.dinc(a, s_act, 1)

                p.run_phase({"vector": f_vector, "scalar": f_scalar}, tk)

            gates_phase()

        if dbg and dbg_stage == 5:
            def f_scalar_d5(a):
                p.wait_all(a)
                ii = a.dma_start(out=g_dbg[:, :], in_=gates[:, :])
                tkk = p.inc(ii, s_sc, 16)
                a.wait_ge(tkk[0], tkk[1])
            p.run_phase({"scalar": f_scalar_d5})

        CAP = 1536
        NCC = CAP // 128
        BIGF = 1.0e5
        if dbg_stage >= 6:
            with (
                nc.sbuf_tensor("iota_sb", [128, 1536], F32) as iota_sb,
                nc.sbuf_tensor("id32_sb", [128, 128], F32) as id32_sb,
                nc.sbuf_tensor("rhs2", [128, 128], BF16) as rhs2,
                nc.sbuf_tensor("maskb", [128, 256], BF16) as maskb,
                nc.sbuf_tensor("maskE", [8, 4096], F32) as maskE,
                nc.sbuf_tensor("slotE", [8, 4096], F32) as slotE,
                nc.sbuf_tensor("slottm", [128, 256], F32) as slottm,
                nc.sbuf_tensor("mful", [128, 2 * 1536], BF16) as mful,
                nc.sbuf_tensor("idxw", [128, 192], F32) as idxw,
                nc.sbuf_tensor("idxst", [128, 96], mybir.dt.int32) as idxst,
                nc.sbuf_tensor("bigc", [128, 12], F32) as bigc,
            ):
                tk = TokDict()

                def f_sync(sy):
                    p.wait_all(sy)
                    ii = sy.dma_start(out=iota_sb[:, :], in_=iota_in[:, :])
                    p.inc(ii, s_once, 16)
                    ii = sy.dma_start(out=id32_sb[:, :], in_=id32_in[:, :])
                    p.inc(ii, s_once, 16)
                    ii = sy.dma_start(out=rhs2[:, :], in_=rhs2s_in[:, :])
                    tk["consts"] = p.inc(ii, s_once, 16)

                def f_gp(g):
                    p.wait_all(g)
                    g.memset(bigc[:, :], BIGF)
                    tk["big"] = p.dinc(g, s_gp, 1)

                def f_vector(v):
                    p.wait_all(v)
                    p.wt(v, tk["consts"])
                    v.tensor_add(slottm[:, :], eqs[:, 0:256], eqs[:, 256:512])
                    v.drain()
                    v.tensor_copy(maskb[:, :], slottm[:, :])
                    tk["mb"] = p.dinc(v, s_dve, 1)
                    for ch in range(4):
                        p.wt(v, tk[("mtr", ch)])
                        v.tensor_copy(maskE[0:8, ch * 1024:(ch + 1) * 1024],
                                      ps_tr[0:8, 0:1024])
                        tk[("mev", ch)] = p.dinc(v, s_dve, 1)
                    v.tensor_tensor_scan(slotE[:, :], maskE[:, :], maskE[:, :], 0.0,
                                         ALU.add, ALU.max)
                    v.drain()
                    v.tensor_mul(slotE[:, :], slotE[:, :], maskE[:, :])
                    tk["slotE"] = p.dinc(v, s_dve, 1)
                    p.wt(v, tk["str"])
                    v.tensor_copy(slottm[:, :], ps_sp[:, 0:256])
                    tk["stm"] = p.dinc(v, s_dve, 1)
                    for e in range(NE):
                        if e:
                            p.wt(v, tk[("lmm", e - 1, NT - 1)])
                        gsrc = bass.AP(gates, e, [[256, 128], [8, 32]])
                        gdst = bass.AP(rhs2, 2, [[128, 128], [4, 32]])
                        v.tensor_copy(gdst, gsrc)
                        tk[("g", e)] = p.dinc(v, s_dve, 1)
                        for tt in range(NT):
                            if e * NT + tt >= 2:
                                j = e * NT + tt - 2
                                p.wt(v, tk[("lmm", j // NT, j % NT)])
                            half = (tt % 2) * 1536
                            v.tensor_single_scalar(mful[:, half:half + 1536],
                                                   iota_sb[:, :],
                                                   slottm[:, tt * 8 + e:tt * 8 + e + 1],
                                                   ALU.is_equal)
                            tk[("M", e, tt)] = p.dinc(v, s_dve, 1)
                    p.wt(v, tk["lev"])
                    p.wt(v, tk["big"])
                    for e in range(NE):
                        base = e * 48
                        plo = bass.AP(lists_keep, base + 0, [[384, 128], [4, 12]])
                        ttv = bass.AP(lists_keep, base + 1, [[384, 128], [4, 12]])
                        val = bass.AP(lists_keep, base + 3, [[384, 128], [4, 12]])
                        ecol = idxw[:, e * 12:(e + 1) * 12]
                        ecol2 = idxw[:, 96 + e * 12:96 + (e + 1) * 12]
                        icol = idxst[:, e * 12:(e + 1) * 12]
                        v.scalar_tensor_tensor(ecol, ttv, 128.0, plo, ALU.mult, ALU.add)
                        v.drain()
                        v.scalar_tensor_tensor(ecol, val, -BIGF, ecol, ALU.mult, ALU.add)
                        v.drain()
                        v.tensor_add(ecol, bigc[:, :], ecol)
                        v.drain()
                        v.tensor_single_scalar(ecol2, ecol, 8388608.0, ALU.add)
                        v.drain()
                        with nc.allow_low_precision(reason="bitcast f32->i32 (exact ints)"):
                            v.tensor_copy(icol, ecol2)
                        v.drain()
                        v.tensor_single_scalar(idxi_keep[:, e * 12:(e + 1) * 12],
                                               icol, 0x007FFFFF, ALU.bitwise_and)
                    tk["idx"] = p.dinc(v, s_dve, 1)

                def f_tensor(t):
                    p.wait_all(t)
                    p.wt(t, tk["mb"])
                    for ch in range(4):
                        if ch >= 1:
                            p.wt(t, tk[("mev", ch - 1)])
                        for j in range(8):
                            tt = ch * 8 + j
                            ii = t.transpose(ps_tr[0:8, j * 128:(j + 1) * 128],
                                             maskb[:, tt * 8:tt * 8 + 8], id_sb[:, :])
                        tk[("mtr", ch)] = p.inc(ii, s_pe, 1)
                    p.wt(t, tk["slotE"])
                    for tt in range(NT):
                        ii = t.transpose(ps_sp[:, tt * 8:(tt + 1) * 8],
                                         slotE[0:8, tt * 128:(tt + 1) * 128],
                                         id32_sb[0:8, 0:8])
                    tk["str"] = p.inc(ii, s_pe, 1)
                    p.wt(t, tk["stm"])
                    for e in range(NE):
                        p.wt(t, tk[("g", e)])
                        for tt in range(NT):
                            p.wt(t, tk[("M", e, tt)])
                            half = (tt % 2) * 1536
                            for c in range(NCC):
                                ii = t.matmul(
                                    ps_c[:, e * 48 + c * 4:e * 48 + (c + 1) * 4],
                                    mful[:, half + c * 128:half + (c + 1) * 128],
                                    rhs2[:, tt * 4:(tt + 1) * 4],
                                    start=(e == 0 and tt == 0 and c == 0),
                                    stop=(e == NE - 1 and tt == NT - 1 and c == NCC - 1),
                                    skip_group_check=True)
                            tk[("lmm", e, tt)] = p.inc(ii, s_pe, 1)

                def f_scalar(a):
                    p.wait_all(a)
                    p.wt(a, tk[("lmm", NE - 1, NT - 1)])
                    a.activation(lists_keep[:, :], ps_c[:, 0:384], AF.Copy)
                    tk["lev"] = p.dinc(a, s_act, 1)

                p.run_phase({"sync": f_sync, "gpsimd": f_gp, "tensor": f_tensor,
                             "scalar": f_scalar, "vector": f_vector}, tk)

        if dbg and dbg_stage == 6:
            def f_scalar_d6(a):
                p.wait_all(a)
                ii = a.dma_start(out=x1_dbg[0:128, 0:384], in_=lists_keep[:, :])
                p.inc(ii, s_sc, 16)
                ii = a.dma_start(out=g_dbg[:, :], in_=gates[:, :])
                tkk = p.inc(ii, s_sc, 16)
                a.wait_ge(tkk[0], tkk[1])

            def f_gp_d6(g):
                p.wait_all(g)
                ii = g.dma_start(out=x1_dbg[384:512, 0:96],
                                 in_=bass.AP(idxi_keep, 0, [[96, 128], [1, 96]]))
                tkk = p.inc(ii, s_sc, 16)
                g.wait_ge(tkk[0], tkk[1])
            p.run_phase({"scalar": f_scalar_d6, "gpsimd": f_gp_d6})

        if dbg_stage >= 6:
            with (
                nc.sbuf_tensor("w1t", [128, 8192], BF16) as w1t,
                nc.sbuf_tensor("w2t", [128, 8192], BF16) as w2t,
                nc.sbuf_tensor("hsb", [128, 4 * 1536], BF16) as hsb,
                nc.sbuf_tensor("xgt", [128, 2048], BF16) as xgt,
            ):
                tk = TokDict()
                TGS = 3

                def xgf(par):
                    return par * 12 * 1024

                def f_gp(g):
                    p.wait_all(g)
                    for e in range(NE):
                        for c in range(NCC):
                            gi_ = e * NCC + c
                            if gi_ >= 2:
                                pe_, pc = (e, c - 2) if c >= 2 else (e - 1, NCC - 2 + c)
                                p.wt(g, tk[("gtrp", pe_, pc)])
                            else:
                                p.wt(g, tk["xz"])
                            ii = g.indirect_dma_start(
                                out=xgt[:, (gi_ % 2) * 1024:(gi_ % 2) * 1024 + 1024],
                                out_offset=None,
                                in_=xn2h[0:128, :],
                                in_offset=bass.IndirectOffsetOnAxis(
                                    ap=idxi_keep[:, e * 12 + c:e * 12 + c + 1], axis=0),
                            )
                            tk[("gth", e, c)] = p.inc(ii, s_gt2[gi_ % 2], 16)
                        if e >= 1:
                            se = e - 1
                            if se >= 1:
                                g.wait_ge(s_sct, 192 * se)
                            p.wt(g, tk[("yv", se)])
                            for c in range(NCC):
                                ii = g.indirect_dma_start(
                                    out=accB[0:128, :],
                                    out_offset=bass.IndirectOffsetOnAxis(
                                        ap=idxi_keep[:, se * 12 + c:se * 12 + c + 1], axis=0),
                                    in_=Bq[:, (se % 2) * 12288 + c * 1024:(se % 2) * 12288 + (c + 1) * 1024],
                                    in_offset=None,
                                    compute_op=ALU.add,
                                )
                                p.inc(ii, s_sct, 16)
                    se = NE - 1
                    g.wait_ge(s_sct, 192 * se)
                    p.wt(g, tk[("yv", se)])
                    for c in range(NCC):
                        ii = g.indirect_dma_start(
                            out=accB[0:128, :],
                            out_offset=bass.IndirectOffsetOnAxis(
                                ap=idxi_keep[:, se * 12 + c:se * 12 + c + 1], axis=0),
                            in_=Bq[:, (se % 2) * 12288 + c * 1024:(se % 2) * 12288 + (c + 1) * 1024],
                            in_offset=None,
                            compute_op=ALU.add,
                        )
                        p.inc(ii, s_sct, 16)

                def f_vector(v):
                    p.wait_all(v)
                    v.memset(xgt[:, :], 0.0)
                    v.memset(bass.AP(A, 0, [[8 * T, 128], [1, 24 * 1024]]), 0.0)
                    tk["xz"] = p.dinc(v, s_dve, 1)
                    for e in range(NE):
                        for c in range(NCC):
                            gi_ = e * NCC + c
                            p.wt(v, tk[("gtrp", e, c)])
                            if e >= 2 and c == 0:
                                p.wt(v, tk[("h", 96 * (e - 1) - 1)])
                            srcp = bass.AP(ps_tr, 0, [[1024, 128], [128, 8], [1, 128]])
                            dstp = bass.AP(A, xgf(e % 2) + c * 128,
                                           [[8 * T, 128], [1536, 8], [1, 128]])
                            v.tensor_copy(dstp, srcp)
                            tk[("gtr", e, c)] = p.dinc(v, s_dve, 1)
                        if e >= 2:
                            v.wait_ge(s_sct, 192 * (e - 1))
                        v.memset(Bq[:, (e % 2) * 12288:(e % 2) * 12288 + 12288], 0.0)
                        tk[("yz", e)] = p.dinc(v, s_dve, 1)
                        for sl in range(8):
                            for tg in range(3):
                                for tt in range(4):
                                    for hf in range(2):
                                        yi = (tg * 4 + tt) * 2 + hf
                                        p.wt(v, tk[("y", e, sl, tg, tt, hf)])
                                        gap = bass.AP(lists_keep,
                                                      e * 48 + (tg * 4 + tt) * 4 + 2,
                                                      [[384, 128], [1, 1]])
                                        ycol = (e % 2) * 12288 + (tg * 4 + tt) * 1024 + hf * 512
                                        ii = v.scalar_tensor_tensor(
                                            Bq[:, ycol:ycol + 512],
                                            ps_c[:, (yi % 2) * 512:(yi % 2) * 512 + 512],
                                            gap, Bq[:, ycol:ycol + 512],
                                            ALU.mult, ALU.add)
                                        if sl == 7 and yi == 23:
                                            tk[("ye", e, sl, yi)] = p.dinc(v, s_dve, 1)
                                        else:
                                            tk[("ye", e, sl, yi)] = p.inc(ii, s_dve, 1)
                            if sl == 7:
                                tk[("yv", e)] = tk[("ye", e, 7, 23)]

                def f_sync(sy):
                    p.wait_all(sy)
                    for e in range(NE):
                        for sl in range(8):
                            par = sl % 2
                            if sl == 0:
                                sy.wait_ge(s_stage[2 * e], 16)
                            if (e, sl) not in ((0, 0), (0, 1)):
                                pe_, psl = (e, sl - 2) if sl >= 2 else (e - 1, sl + 6)
                                p.wt(sy, tk[("y", pe_, psl, TGS - 1, 3, 1)])
                            for r in range(8):
                                ii = sy.dma_start(
                                    out=w1t[:, par * 4096 + r * 512:par * 4096 + (r + 1) * 512],
                                    in_=bass.AP(w1bf.tensor, e * DFF * E + sl * 512 * E + r * 128,
                                                [[E, 512], [1, 128]]),
                                    transpose=True)
                                tk[("w1", e, sl)] = p.inc(ii, s_w1t2[par], 16)
                            if sl == 0:
                                sy.wait_ge(s_stage[2 * e + 1], 16)
                            for fs in range(4):
                                ii = sy.dma_start(
                                    out=w2t[:, par * 4096 + fs * 1024:par * 4096 + (fs + 1) * 1024],
                                    in_=bass.AP(w2bf.tensor, e * E * DFF + sl * 512 + fs * 128,
                                                [[DFF, 1024], [1, 128]]),
                                    transpose=True)
                                tk[("w2", e, sl)] = p.inc(ii, s_w2t2[par], 16)

                def f_tensor(t):
                    p.wait_all(t)
                    p.wt(t, tk["xz"])
                    hidx = 0
                    for e in range(NE):
                        par_e = e % 2
                        for c in range(NCC):
                            gi_ = e * NCC + c
                            p.wt(t, tk[("gth", e, c)])
                            if gi_ >= 1:
                                pe_, pc = (e, c - 1) if c else (e - 1, NCC - 1)
                                p.wt(t, tk[("gtr", pe_, pc)])
                            for r in range(8):
                                ii = t.transpose(
                                    ps_tr[:, r * 128:(r + 1) * 128],
                                    xgt[:, (gi_ % 2) * 1024 + r * 128:(gi_ % 2) * 1024 + (r + 1) * 128],
                                    id_sb[:, :])
                            tk[("gtrp", e, c)] = p.inc(ii, s_pe, 1)
                        p.wt(t, tk[("gtr", e, NCC - 1)])
                        p.wt(t, tk[("yz", e)])
                        for sl in range(8):
                            par = sl % 2
                            p.wt(t, tk[("w1", e, sl)])
                            for tg in range(TGS):
                                for fs in range(4):
                                    bsel = (hidx // 2) % 2
                                    psh = ps_a if bsel == 0 else ps_b
                                    half = (hidx % 2) * 512
                                    if hidx >= 4:
                                        p.wt(t, tk[("si", (hidx - 4) // 2)])
                                    for r in range(8):
                                        ii = t.matmul(
                                            psh[:, half:half + 512],
                                            w1t[:, par * 4096 + r * 512 + fs * 128:par * 4096 + r * 512 + (fs + 1) * 128],
                                            A[:, xgf(par_e) + r * 1536 + tg * 512:xgf(par_e) + r * 1536 + (tg + 1) * 512],
                                            start=(r == 0), stop=(r == 7))
                                    tk[("h", hidx)] = p.inc(ii, s_pe, 1)
                                    hidx += 1
                            p.wt(t, tk[("w2", e, sl)])
                            p.wt(t, tk[("si", (hidx - 2) // 2)])
                            for tg in range(TGS):
                                for tt in range(4):
                                    for hf in range(2):
                                        yi = (tg * 4 + tt) * 2 + hf
                                        if (e, sl) != (0, 0) or yi >= 2:
                                            if yi >= 2:
                                                p.wt(t, tk[("ye", e, sl, yi - 2)])
                                            else:
                                                pe_, psl = (e, sl - 1) if sl else (e - 1, 7)
                                                p.wt(t, tk[("ye", pe_, psl, 22 + yi)])
                                        for fs in range(4):
                                            ii = t.matmul(
                                                ps_c[:, (yi % 2) * 512:(yi % 2) * 512 + 512],
                                                hsb[:, fs * 1536 + tg * 512 + tt * 128:fs * 1536 + tg * 512 + (tt + 1) * 128],
                                                w2t[:, par * 4096 + fs * 1024 + hf * 512:par * 4096 + fs * 1024 + (hf + 1) * 512],
                                                start=(fs == 0), stop=(fs == 3))
                                        tk[("y", e, sl, tg, tt, hf)] = p.inc(ii, s_pe, 1)

                def f_scalar(a):
                    p.wait_all(a)
                    hidx = 0
                    for e in range(NE):
                        for sl in range(8):
                            for tg in range(TGS):
                                for fs in range(4):
                                    if hidx % 2 == 1:
                                        p.wt(a, tk[("h", hidx)])
                                        bsel = (hidx // 2) % 2
                                        psh = ps_a if bsel == 0 else ps_b
                                        dst = bass.AP(hsb, (fs - 1) * 1536 + tg * 512,
                                                      [[4 * 1536, 128], [1536, 2], [1, 512]])
                                        ii = a.activation(dst, psh[:, :], AF.Silu)
                                        tk[("si", hidx // 2)] = p.inc(ii, s_act, 1)
                                    hidx += 1

                p.run_phase({"gpsimd": f_gp, "vector": f_vector, "sync": f_sync,
                             "tensor": f_tensor, "scalar": f_scalar}, tk)

        if dbg and dbg_stage == 7:
            def f_gp_d7(g):
                p.wait_all(g)
                tkk = None
                for tt in range(NT):
                    ii = g.dma_start(out=x1_dbg[tt * 128:(tt + 1) * 128, :],
                                     in_=accB[tt * 128:(tt + 1) * 128, :])
                    tkk = p.inc(ii, s_sc, 16)
                g.wait_ge(tkk[0], tkk[1])

            def f_scalar_d7(a):
                p.wait_all(a)
                tkk = None
                for c in range(8):
                    ii = a.dma_start(out=k_dbg[:, c * T:(c + 1) * T],
                                     in_=Bq[:, c * T:(c + 1) * T])
                    p.inc(ii, s_sc, 16)
                ii = a.dma_start(out=g_dbg[:, :], in_=gates[:, :])
                p.inc(ii, s_sc, 16)
                for c in range(6):
                    ii = a.dma_start(out=q_dbg[:, c * T:(c + 1) * T],
                                     in_=A[:, c * T:(c + 1) * T])
                    tkk = p.inc(ii, s_sc, 16)
                a.wait_ge(tkk[0], tkk[1])
            p.run_phase({"gpsimd": f_gp_d7, "scalar": f_scalar_d7})

        if dbg_stage >= 7:
            with (
                nc.sbuf_tensor("fx", [128, 2048], F32) as fx,
                nc.sbuf_tensor("fb", [128, 2048], BF16) as fb,
                nc.sbuf_tensor("fo", [128, 2048], F32) as fo,
            ):
                tk = TokDict()

                def f_sync(sy):
                    p.wait_all(sy)
                    for tt in range(NT):
                        if tt >= 2:
                            p.wt(sy, tk[("add", tt - 2)])
                        ii = sy.dma_start(out=fx[:, (tt % 2) * 1024:(tt % 2) * 1024 + 1024],
                                          in_=x1h[tt * 128:(tt + 1) * 128, :])
                        p.inc(ii, s_lx2[tt % 2], 16)
                        ii = sy.dma_start(out=fb[:, (tt % 2) * 1024:(tt % 2) * 1024 + 1024],
                                          in_=accB[tt * 128:(tt + 1) * 128, :])
                        tk[("l", tt)] = p.inc(ii, s_lx2[tt % 2], 16)

                def f_vector(v):
                    p.wait_all(v)
                    for tt in range(NT):
                        p.wt(v, tk[("l", tt)])
                        if tt >= 2:
                            p.wt(v, tk[("st", tt - 2)])
                        v.tensor_add(fo[:, (tt % 2) * 1024:(tt % 2) * 1024 + 1024],
                                     fb[:, (tt % 2) * 1024:(tt % 2) * 1024 + 1024],
                                     fx[:, (tt % 2) * 1024:(tt % 2) * 1024 + 1024])
                        tk[("add", tt)] = p.dinc(v, s_dve, 1)

                def f_scalar(a):
                    p.wait_all(a)
                    for tt in range(NT):
                        p.wt(a, tk[("add", tt)])
                        ii = a.dma_start(out=out[tt * 128:(tt + 1) * 128, :],
                                         in_=fo[:, (tt % 2) * 1024:(tt % 2) * 1024 + 1024])
                        tk[("st", tt)] = p.inc(ii, s_ost2[tt % 2], 16)
                    a.wait_ge(s_ost2[0], p.tot(s_ost2[0]))
                    a.wait_ge(s_ost2[1], p.tot(s_ost2[1]))
                    for sem in s_stage:
                        if p.tot(sem):
                            a.wait_ge(sem, p.tot(sem))

                p.run_phase({"sync": f_sync, "vector": f_vector, "scalar": f_scalar}, tk)
        else:
            def f_scalar_end(a):
                p.wait_all(a)
                for sem in s_stage:
                    if p.tot(sem):
                        a.wait_ge(sem, p.tot(sem))
            p.run_phase({"scalar": f_scalar_end})

    return nc


def kernel(L0_attn_x, L0_attn_wq, L0_attn_w_down, L0_attn_wk_up, L0_attn_wv_up,
           L0_attn_wo, L0_ffn_W1, L0_ffn_W2, L0_ffn_Wr):
    C, S, swp, ident, ident32, iota1, rhs2s = _host_consts()
    nc = build_nc(dbg_stage=DBG_STAGE, dbg=DBG)
    x = np.ascontiguousarray(np.asarray(L0_attn_x, dtype=np.float32))
    common = {
        "wq": np.ascontiguousarray(L0_attn_wq, dtype=np.float32),
        "wd": np.ascontiguousarray(L0_attn_w_down, dtype=np.float32),
        "wk": np.ascontiguousarray(L0_attn_wk_up, dtype=np.float32),
        "wv": np.ascontiguousarray(L0_attn_wv_up, dtype=np.float32),
        "wo": np.ascontiguousarray(L0_attn_wo, dtype=np.float32),
        "w1": np.ascontiguousarray(L0_ffn_W1, dtype=np.float32).reshape(NE * DFF, E),
        "w2": np.ascontiguousarray(L0_ffn_W2, dtype=np.float32).reshape(NE * E, DFF),
        "wr": np.ascontiguousarray(L0_ffn_Wr, dtype=np.float32),
        "ropec": C, "ropes": S, "swp": swp, "ident": ident,
        "ident32": ident32, "iota1": iota1, "rhs2s": rhs2s,
    }
    in_maps = [dict(common, x=np.ascontiguousarray(x[b])) for b in range(NCORE)]
    res = run_bass_kernel_spmd(nc, in_maps, core_ids=list(range(NCORE)))
    return np.stack([r["out"] for r in res.results], axis=0)


# version 05
# speedup vs baseline: 1.5252x; 1.0116x over previous
import numpy as np
import ml_dtypes
import concourse.bass as bass
import concourse.mybir as mybir
from concourse.bass_utils import run_bass_kernel_spmd

F32 = mybir.dt.float32
BF16 = mybir.dt.bfloat16
AF = mybir.ActivationFunctionType
ALU = mybir.AluOpType
AX = mybir.AxisListType
BF = ml_dtypes.bfloat16

T, E, NH, D, L = 4096, 1024, 16, 64, 512
NE, DFF = 8, 4096
NT = T // 128
NG = T // 512
EPS = 1e-5
NCORE = 8

DBG_STAGE = 9
DBG = 0


def _host_consts():
    inv = 1.0 / (10000.0 ** (np.arange(0, D, 2, dtype=np.float32) / D))
    ang = np.outer(np.arange(T, dtype=np.float32), inv)
    cos = np.cos(ang).astype(np.float32)
    sin = np.sin(ang).astype(np.float32)
    C = np.zeros((128, T), np.float32)
    S = np.zeros((128, T), np.float32)
    for r in range(128):
        i = r % 64
        pp = i // 2
        C[r] = cos[:, pp]
        S[r] = sin[:, pp] * (1.0 if i % 2 == 1 else -1.0)
    swp = np.zeros((128, 128), BF)
    for r in range(128):
        swp[r ^ 1, r] = 1.0
    ident = np.eye(128, dtype=BF)
    ident32 = np.eye(128, dtype=np.float32)
    iota1 = np.broadcast_to(np.arange(1, 1537, dtype=np.float32)[None, :], (128, 1536)).copy()
    rhs2s = np.zeros((128, 128), BF)
    for tt in range(32):
        rhs2s[:, 4 * tt + 0] = np.arange(128, dtype=np.float32)
        rhs2s[:, 4 * tt + 1] = float(tt)
        rhs2s[:, 4 * tt + 3] = 1.0
    return C, S, swp, ident, ident32, iota1, rhs2s


class _FakeInstr:
    def then_inc(self, sem, k):
        return self


class TokDict(dict):

    def __init__(self):
        super().__init__()
        self.missed = set()

    def __missing__(self, k):
        self.missed.add(k)
        return 0


class _CountEng:
    def wait_ge(self, sem, val):
        pass

    def __getattr__(self, name):
        def f(*a, **kw):
            return _FakeInstr()
        return f


class Prog:

    ORDER = ("sync", "gpsimd", "tensor", "scalar", "vector")

    def __init__(self, nc):
        self.nc = nc
        self.c = {}
        self.snap = {}

    def inc(self, instr, sem, k):
        instr.then_inc(sem, k)
        self.c[sem] = self.c.get(sem, 0) + k
        return (sem, self.c[sem])

    def dinc(self, eng, sem, k):
        d = eng.drain()
        d.then_inc(sem, k)
        self.c[sem] = self.c.get(sem, 0) + k
        return (sem, self.c[sem])

    def tot(self, sem):
        return self.c.get(sem, 0)

    def wt(self, eng, tok):
        if tok:
            sem, val = tok
            if val:
                eng.wait_ge(sem, val)

    def wait_all(self, eng):
        for sem, t in self.snap.items():
            if t:
                eng.wait_ge(sem, t)

    def run_phase(self, streams, tk=None):
        self.snap = dict(self.c)
        start = dict(self.c)
        for name in self.ORDER:
            if name in streams:
                streams[name](_CountEng())
        if tk is not None:
            unresolved = tk.missed - set(tk.keys())
            assert not unresolved, f"unresolved tokens: {unresolved}"
        end = dict(self.c)
        self.c = dict(start)
        with self.nc.Block() as blk:
            for name in self.ORDER:
                if name in streams:
                    getattr(blk, name)(streams[name])
        assert self.c == end, "two-pass mismatch"


def build_nc(dbg_stage=9, dbg=0):
    nc = bass.Bass("TRN2", target_bir_lowering=False)
    _uc = [0]

    def un(s):
        _uc[0] += 1
        return f"{s}_{_uc[0]}"

    x_in = nc.dram_tensor("x", [T, E], F32, kind="ExternalInput").ap()
    wq = nc.dram_tensor("wq", [E, E], F32, kind="ExternalInput").ap()
    wd = nc.dram_tensor("wd", [E, L], F32, kind="ExternalInput").ap()
    wk = nc.dram_tensor("wk", [L, E], F32, kind="ExternalInput").ap()
    wv = nc.dram_tensor("wv", [L, E], F32, kind="ExternalInput").ap()
    wo = nc.dram_tensor("wo", [E, E], F32, kind="ExternalInput").ap()
    w1 = nc.dram_tensor("w1", [NE * DFF, E], F32, kind="ExternalInput").ap()
    w2 = nc.dram_tensor("w2", [NE * E, DFF], F32, kind="ExternalInput").ap()
    wr = nc.dram_tensor("wr", [NE, E], F32, kind="ExternalInput").ap()
    ropec = nc.dram_tensor("ropec", [128, T], F32, kind="ExternalInput").ap()
    ropes = nc.dram_tensor("ropes", [128, T], F32, kind="ExternalInput").ap()
    swp_in = nc.dram_tensor("swp", [128, 128], BF16, kind="ExternalInput").ap()
    id_in = nc.dram_tensor("ident", [128, 128], BF16, kind="ExternalInput").ap()
    id32_in = nc.dram_tensor("ident32", [128, 128], F32, kind="ExternalInput").ap()
    iota_in = nc.dram_tensor("iota1", [128, 1536], F32, kind="ExternalInput").ap()
    rhs2s_in = nc.dram_tensor("rhs2s", [128, 128], BF16, kind="ExternalInput").ap()

    out = nc.dram_tensor("out", [T, E], F32, kind="ExternalOutput").ap()
    if dbg:
        q_dbg = nc.dram_tensor("q_dbg", [128, 8 * T], BF16, kind="ExternalOutput").ap()
        k_dbg = nc.dram_tensor("k_dbg", [128, 8 * T], BF16, kind="ExternalOutput").ap()
        x1_dbg = nc.dram_tensor("x1_dbg", [T, E], F32, kind="ExternalOutput").ap()
        g_dbg = nc.dram_tensor("g_dbg", [128, 256], F32, kind="ExternalOutput").ap()
        v_dbg = nc.dram_tensor("v_dbg", [T, 1040], BF16, kind="ExternalOutput").ap()

    w1bf = nc.dram_tensor("w1bf", [NE * DFF, E], BF16).ap()
    w2bf = nc.dram_tensor("w2bf", [NE * E, DFF], BF16).ap()
    x1h = nc.dram_tensor("x1h", [T, E], F32).ap()
    vh = nc.dram_tensor("vh", [T, 1040], BF16).ap()
    xn2h = nc.dram_tensor("xn2h", [T, E], BF16).ap()
    accB = nc.dram_tensor("accB", [T, E], BF16).ap()

    from contextlib import ExitStack
    with ExitStack() as _st:
        en = _st.enter_context
        A = en(nc.sbuf_tensor("A", [128, 8 * T], BF16))
        Bq = en(nc.sbuf_tensor("Bq", [128, 8 * T], BF16))
        ones128 = en(nc.sbuf_tensor("ones128", [128, 128], BF16))
        swp_sb = en(nc.sbuf_tensor("swp_sb", [128, 128], BF16))
        id_sb = en(nc.sbuf_tensor("id_sb", [128, 128], BF16))
        stats = en(nc.sbuf_tensor("stats", [128, 192], F32))
        gates = en(nc.sbuf_tensor("gates", [128, 256], F32))
        zero8 = en(nc.sbuf_tensor("zero8", [128, 8], F32))
        epsc = en(nc.sbuf_tensor("epsc", [128, 1], F32))
        eqs = en(nc.sbuf_tensor("eqs", [128, 512], F32))
        wr_t = en(nc.sbuf_tensor("wr_t", [128, 64], BF16))
        lists_keep = en(nc.sbuf_tensor("lists_keep", [128, 384], F32))
        idxi_keep = en(nc.sbuf_tensor("idxi_keep", [128, 96], mybir.dt.int32))
        ps_a = en(nc.psum_tensor("ps_a", [128, 1024], F32))
        ps_b = en(nc.psum_tensor("ps_b", [128, 1024], F32))
        ps_c = en(nc.psum_tensor("ps_c", [128, 1024], F32))
        ps_sp = en(nc.psum_tensor("ps_sp", [128, 512], F32))
        ps_tr = en(nc.psum_tensor("ps_tr", [128, 1024], BF16))
        s_pe = nc.alloc_semaphore("s_pe")
        s_act = nc.alloc_semaphore("s_act")
        s_dve = nc.alloc_semaphore("s_dve")
        s_gp = nc.alloc_semaphore("s_gp")
        s_once = nc.alloc_semaphore("s_once")
        s_sc = nc.alloc_semaphore("s_sc")
        s_mv = nc.alloc_semaphore("s_mv")
        s_ax = nc.alloc_semaphore("s_ax")
        s_lx2 = [nc.alloc_semaphore(f"s_lx{i}") for i in range(2)]
        s_rp2 = [nc.alloc_semaphore(f"s_rp{i}") for i in range(2)]
        s_wsl2 = [nc.alloc_semaphore(f"s_wsl{i}") for i in range(2)]
        s_v8 = [nc.alloc_semaphore(f"s_v{i}") for i in range(8)]
        s_vst2 = [nc.alloc_semaphore(f"s_vst{i}") for i in range(2)]
        s_ost2 = [nc.alloc_semaphore(f"s_ost{i}") for i in range(2)]
        s_w1t2 = [nc.alloc_semaphore(f"s_w1t{i}") for i in range(2)]
        s_w2t2 = [nc.alloc_semaphore(f"s_w2t{i}") for i in range(2)]
        s_stage = [nc.alloc_semaphore(f"s_stage{i}") for i in range(2 * NE)]
        s_x2st2 = [nc.alloc_semaphore(f"s_x2st{i}") for i in range(2)]
        s_gt2 = [nc.alloc_semaphore(f"s_gt{i}") for i in range(2)]
        s_sct = nc.alloc_semaphore("s_sct")

        p = Prog(nc)

        def ph0():
            def f_sync(sy):
                ii = sy.dma_start(out=swp_sb[:, :], in_=swp_in[:, :])
                p.inc(ii, s_once, 16)
                ii = sy.dma_start(out=id_sb[:, :], in_=id_in[:, :])
                p.inc(ii, s_once, 16)

            def f_gp(g):
                ii = g.memset(ones128[:, :], 1.0)
                p.inc(ii, s_gp, 1)
                ii = g.memset(zero8[:, :], 0.0)
                p.inc(ii, s_gp, 1)
                ii = g.memset(epsc[:, :], EPS)
                p.inc(ii, s_gp, 1)

            p.run_phase({"sync": f_sync, "gpsimd": f_gp})

        ph0()

        def ln_phase(src_dram, dst_sb, do_logits, store_tm=False):
            with (
                nc.sbuf_tensor(un("lx"), [128, 2048], F32) as lx,
                nc.sbuf_tensor(un("lxn"), [128, 2048], BF16) as _lxn,
                nc.sbuf_tensor(un("lsq"), [128, 1024], BF16) as lsq,
                nc.sbuf_tensor(un("llg"), [128, 16], F32) as llg,
                nc.sbuf_tensor(un("lmk"), [128, 16], F32) as lmk,
            ):
                lxn = dbg_lxn if (dbg and dbg_stage == 1 and dbg_lxn is not None) else _lxn
                if dbg and dbg_stage == 1 and dbg_lx is not None:
                    lx = dbg_lx
                tk = TokDict()

                def f_sync(sy):
                    p.wait_all(sy)
                    for tt in range(NT):
                        if tt >= 2:
                            p.wt(sy, tk[("ap", tt - 2)])
                            p.wt(sy, tk[("sum", tt - 2)])
                        ii = sy.dma_start(
                            out=lx[:, (tt % 2) * 1024:(tt % 2) * 1024 + 1024],
                            in_=src_dram[tt * 128:(tt + 1) * 128, :])
                        tk[("ld", tt)] = p.inc(ii, s_lx2[tt % 2], 16)

                def f_vector(v):
                    p.wait_all(v)
                    for tt in range(NT):
                        xv = lx[:, (tt % 2) * 1024:(tt % 2) * 1024 + 1024]
                        p.wt(v, tk[("ld", tt)])
                        v.reduce_sum(stats[:, tt:tt + 1], xv, axis=AX.X)
                        tk[("sum", tt)] = p.dinc(v, s_dve, 1)
                        p.wt(v, tk[("sq", tt)])
                        mcol = stats[:, 64 + tt:64 + tt + 1]
                        v.tensor_single_scalar(mcol, stats[:, tt:tt + 1], 1.0 / E, ALU.mult)
                        v.drain()
                        v.tensor_mul(stats[:, tt:tt + 1], mcol, mcol)
                        v.drain()
                        v.scalar_tensor_tensor(
                            stats[:, 160 + tt:160 + tt + 1],
                            stats[:, 32 + tt:32 + tt + 1], 1.0 / E,
                            stats[:, tt:tt + 1], ALU.mult, ALU.subtract)
                        tk[("var", tt)] = p.dinc(v, s_dve, 1)
                        p.wt(v, tk[("rstd", tt)])
                        v.scalar_tensor_tensor(
                            stats[:, 128 + tt:128 + tt + 1], mcol, -1.0,
                            stats[:, 96 + tt:96 + tt + 1], ALU.mult, ALU.mult)
                        tk[("nb", tt)] = p.dinc(v, s_dve, 1)
                        p.wt(v, tk[("tr", tt)])
                        src = bass.AP(ps_tr, 0, [[1024, 128], [128, 8], [1, 128]])
                        dst = bass.AP(dst_sb, tt * 128, [[8 * T, 128], [T, 8], [1, 128]])
                        v.tensor_copy(dst, src)
                        tk[("ev", tt)] = p.dinc(v, s_dve, 1)
                        if do_logits:
                            p.wt(v, tk[("lg", tt)])
                            v.tensor_copy(llg[:, 0:8], ps_sp[:, 0:8])
                            v.drain()
                            m1 = stats[:, tt:tt + 1]
                            m2 = stats[:, 32 + tt:32 + tt + 1]
                            v.reduce_max(m1, llg[:, 0:8], axis=AX.X)
                            v.drain()
                            e1 = eqs[:, tt * 8:tt * 8 + 8]
                            v.tensor_single_scalar(e1, llg[:, 0:8], m1, ALU.is_equal)
                            v.drain()
                            v.scalar_tensor_tensor(lmk[:, 0:8], e1, -1.0e30, llg[:, 0:8],
                                                   ALU.mult, ALU.add)
                            v.drain()
                            v.reduce_max(m2, lmk[:, 0:8], axis=AX.X)
                            v.drain()
                            e2 = eqs[:, 256 + tt * 8:256 + tt * 8 + 8]
                            ii = v.tensor_single_scalar(e2, llg[:, 0:8], m2, ALU.is_equal)
                            tk[("rt", tt)] = p.inc(ii, s_dve, 1)

                def f_scalar(a):
                    p.wait_all(a)
                    for tt in range(NT):
                        xv = lx[:, (tt % 2) * 1024:(tt % 2) * 1024 + 1024]
                        p.wt(a, tk[("ld", tt)])
                        a.activation(lsq[:, :], xv, AF.Square,
                                     accum_out=stats[:, 32 + tt:32 + tt + 1])
                        tk[("sq", tt)] = p.dinc(a, s_act, 1)
                        p.wt(a, tk[("var", tt)])
                        a.activation(stats[:, 32 + tt:32 + tt + 1],
                                     stats[:, 160 + tt:160 + tt + 1], AF.Ln, bias=epsc[:, :])
                        a.drain()
                        a.activation(stats[:, 96 + tt:96 + tt + 1],
                                     stats[:, 32 + tt:32 + tt + 1], AF.Exp, scale=-0.5)
                        tk[("rstd", tt)] = p.dinc(a, s_act, 1)
                        p.wt(a, tk[("nb", tt)])
                        if tt >= 2:
                            p.wt(a, tk[("tr", tt - 2)])
                            if store_tm:
                                p.wt(a, tk[("tmst", tt - 2)])
                        a.activation(lxn[:, (tt % 2) * 1024:(tt % 2) * 1024 + 1024],
                                     xv, AF.Identity,
                                     bias=stats[:, 128 + tt:128 + tt + 1],
                                     scale=stats[:, 96 + tt:96 + tt + 1])
                        tk[("ap", tt)] = p.dinc(a, s_act, 1)
                        if store_tm:
                            ii = a.dma_start(out=xn2h[tt * 128:(tt + 1) * 128, :],
                                             in_=lxn[:, (tt % 2) * 1024:(tt % 2) * 1024 + 1024])
                            tk[("tmst", tt)] = p.inc(ii, s_x2st2[tt % 2], 16)

                def f_tensor(t):
                    p.wait_all(t)
                    for tt in range(NT):
                        p.wt(t, tk[("ap", tt)])
                        if tt >= 1:
                            p.wt(t, tk[("ev", tt - 1)])
                        for c in range(8):
                            ii = t.transpose(
                                ps_tr[:, c * 128:(c + 1) * 128],
                                lxn[:, (tt % 2) * 1024 + c * 128:(tt % 2) * 1024 + (c + 1) * 128],
                                id_sb[:, :])
                        tk[("tr", tt)] = p.inc(ii, s_pe, 1)
                        if do_logits:
                            p.wt(t, tk[("ev", tt)])
                            if tt >= 1:
                                p.wt(t, tk[("rt", tt - 1)])
                            for r in range(8):
                                ii = t.matmul(
                                    ps_sp[:, 0:8],
                                    dst_sb[:, r * T + tt * 128:r * T + (tt + 1) * 128],
                                    wr_t[:, r * 8:(r + 1) * 8],
                                    start=(r == 0), stop=(r == 7))
                            tk[("lg", tt)] = p.inc(ii, s_pe, 1)

                p.run_phase({"sync": f_sync, "vector": f_vector,
                             "scalar": f_scalar, "tensor": f_tensor}, tk)

        def proj_rope_phase(wmat, wrows, nacc, src_sb, dst_sb):
            with (
                nc.sbuf_tensor(un("pw"), [128, 2 * nacc * 128], BF16) as pw,
                nc.sbuf_tensor(un("prc"), [128, 1024], F32) as prc,
                nc.sbuf_tensor(un("prs"), [128, 1024], F32) as prs,
                nc.sbuf_tensor(un("pqt"), [128, 1024], BF16) as pqt,
                nc.sbuf_tensor(un("pt1"), [128, 1024], BF16) as pt1,
            ):
                tk = TokDict()

                def f_gp(g):
                    p.wait_all(g)
                    for c in range(8):
                        if c >= 2:
                            p.wt(g, tk[("mm", c - 2, 7)])
                        half = (c % 2) * nacc * 128
                        dst = bass.AP(pw, half, [[2 * nacc * 128, 128], [128, nacc], [1, 128]])
                        src = bass.AP(wmat.tensor, c * 128, [[E, 128], [128 * E, nacc], [1, 128]])
                        ii = g.dma_start(out=dst, in_=src)
                        tk[("w", c)] = p.inc(ii, s_wsl2[c % 2], 16)

                def f_sync(sy):
                    p.wait_all(sy)
                    for c in range(8):
                        for tg in range(8):
                            i2 = c * 8 + tg
                            par = i2 % 2
                            if i2 >= 2:
                                pc, ptg = (i2 - 2) // 8, (i2 - 2) % 8
                                p.wt(sy, tk[("out", pc, ptg)])
                            ii = sy.dma_start(out=prc[:, par * 512:par * 512 + 512],
                                              in_=ropec[:, tg * 512:(tg + 1) * 512])
                            p.inc(ii, s_rp2[par], 16)
                            ii = sy.dma_start(out=prs[:, par * 512:par * 512 + 512],
                                              in_=ropes[:, tg * 512:(tg + 1) * 512])
                            tk[("rl", c, tg)] = p.inc(ii, s_rp2[par], 16)

                def f_tensor(t):
                    p.wait_all(t)
                    for c in range(8):
                        p.wt(t, tk[("w", c)])
                        for tg in range(8):
                            i2 = c * 8 + tg
                            par = i2 % 2
                            if i2 >= 2:
                                pc, ptg = (i2 - 2) // 8, (i2 - 2) % 8
                                p.wt(t, tk[("ev", pc, ptg)])
                            for r in range(nacc):
                                ii = t.matmul(
                                    ps_a[:, par * 512:par * 512 + 512],
                                    pw[:, (c % 2) * nacc * 128 + r * 128:(c % 2) * nacc * 128 + (r + 1) * 128],
                                    src_sb[:, r * T + tg * 512:r * T + (tg + 1) * 512],
                                    start=(r == 0), stop=(r == nacc - 1))
                            tk[("mm", c, tg)] = p.inc(ii, s_pe, 1)
                            p.wt(t, tk[("ev", c, tg)])
                            if i2 >= 2:
                                pc, ptg = (i2 - 2) // 8, (i2 - 2) % 8
                                p.wt(t, tk[("out", pc, ptg)])
                            ii = t.matmul(ps_b[:, par * 512:par * 512 + 512],
                                          swp_sb[:, :],
                                          pqt[:, par * 512:par * 512 + 512],
                                          start=True, stop=True)
                            tk[("sw", c, tg)] = p.inc(ii, s_pe, 1)

                def f_scalar(a):
                    p.wait_all(a)
                    for c in range(8):
                        for tg in range(8):
                            i2 = c * 8 + tg
                            par = i2 % 2
                            p.wt(a, tk[("mm", c, tg)])
                            if i2 >= 2:
                                pc, ptg = (i2 - 2) // 8, (i2 - 2) % 8
                                p.wt(a, tk[("sw", pc, ptg)])
                                p.wt(a, tk[("p1", pc, ptg)])
                            a.activation(pqt[:, par * 512:par * 512 + 512],
                                         ps_a[:, par * 512:par * 512 + 512], AF.Copy)
                            tk[("ev", c, tg)] = p.dinc(a, s_act, 1)

                def f_vector(v):
                    p.wait_all(v)
                    for c in range(8):
                        for tg in range(8):
                            i2 = c * 8 + tg
                            par = i2 % 2
                            p.wt(v, tk[("rl", c, tg)])
                            p.wt(v, tk[("ev", c, tg)])
                            v.tensor_mul(pt1[:, par * 512:par * 512 + 512],
                                         pqt[:, par * 512:par * 512 + 512],
                                         prc[:, par * 512:par * 512 + 512])
                            tk[("p1", c, tg)] = p.dinc(v, s_dve, 1)
                            p.wt(v, tk[("sw", c, tg)])
                            dcol = dst_sb[:, c * T + tg * 512:c * T + (tg + 1) * 512]
                            v.tensor_mul(dcol, ps_b[:, par * 512:par * 512 + 512],
                                         prs[:, par * 512:par * 512 + 512])
                            v.drain()
                            ii = v.tensor_add(dcol, pt1[:, par * 512:par * 512 + 512], dcol)
                            tk[("out", c, tg)] = p.inc(ii, s_dve, 1)

                p.run_phase({"gpsimd": f_gp, "sync": f_sync, "tensor": f_tensor,
                             "scalar": f_scalar, "vector": f_vector}, tk)

        dbg_lxn = en(nc.sbuf_tensor("dbg_lxn", [128, 2048], BF16)) if (dbg and dbg_stage == 1) else None
        dbg_lx = en(nc.sbuf_tensor("dbg_lx", [128, 2048], F32)) if (dbg and dbg_stage == 1) else None
        ln_phase(x_in, A, do_logits=False)
        if dbg_stage >= 2:
            proj_rope_phase(wq, E, 8, A, Bq)

        if dbg and dbg_stage == 1:
            def f_sync_d1(sy):
                p.wait_all(sy)
                tkk = None
                for c in range(8):
                    ii = sy.dma_start(out=k_dbg[:, c * T:(c + 1) * T], in_=A[:, c * T:(c + 1) * T])
                    tkk = p.inc(ii, s_sc, 16)
                ii = sy.dma_start(out=x1_dbg[0:128, 0:192], in_=stats[:, :])
                tkk = p.inc(ii, s_sc, 16)
                ii = sy.dma_start(out=q_dbg[:, 0:2048], in_=dbg_lxn[:, :])
                tkk = p.inc(ii, s_sc, 16)
                ii = sy.dma_start(out=x1_dbg[128:256, :], in_=dbg_lx[:, 0:1024])
                tkk = p.inc(ii, s_sc, 16)
                ii = sy.dma_start(out=x1_dbg[256:384, :], in_=dbg_lx[:, 1024:2048])
                tkk = p.inc(ii, s_sc, 16)
                sy.wait_ge(tkk[0], tkk[1])
            p.run_phase({"sync": f_sync_d1})

        if dbg and dbg_stage == 2:
            def f_scalar_d2(a):
                p.wait_all(a)
                tkk = None
                for c in range(8):
                    ii = a.dma_start(out=k_dbg[:, c * T:(c + 1) * T], in_=A[:, c * T:(c + 1) * T])
                    p.inc(ii, s_sc, 16)
                    ii = a.dma_start(out=q_dbg[:, c * T:(c + 1) * T], in_=Bq[:, c * T:(c + 1) * T])
                    tkk = p.inc(ii, s_sc, 16)
                a.wait_ge(tkk[0], tkk[1])
            p.run_phase({"scalar": f_scalar_d2})

        if dbg_stage >= 3:
            with nc.sbuf_tensor("latf", [128, 4 * T], BF16) as latf:
                with nc.sbuf_tensor("dw", [128, 2048], BF16) as dw:
                    tk = TokDict()

                    def f_gp(g):
                        p.wait_all(g)
                        for lc in range(4):
                            if lc >= 2:
                                p.wt(g, tk[("mm", lc - 2, 7)])
                            half = (lc % 2) * 1024
                            dst = bass.AP(dw, half, [[2048, 128], [128, 8], [1, 128]])
                            src = bass.AP(wd.tensor, lc * 128, [[L, 128], [128 * L, 8], [1, 128]])
                            ii = g.dma_start(out=dst, in_=src)
                            tk[("w", lc)] = p.inc(ii, s_wsl2[lc % 2], 16)

                    def f_tensor(t):
                        p.wait_all(t)
                        for lc in range(4):
                            p.wt(t, tk[("w", lc)])
                            for tg in range(8):
                                i2 = lc * 8 + tg
                                par = i2 % 2
                                if i2 >= 2:
                                    pl, ptg = (i2 - 2) // 8, (i2 - 2) % 8
                                    p.wt(t, tk[("ev", pl, ptg)])
                                for r in range(8):
                                    ii = t.matmul(
                                        ps_a[:, par * 512:par * 512 + 512],
                                        dw[:, (lc % 2) * 1024 + r * 128:(lc % 2) * 1024 + (r + 1) * 128],
                                        A[:, r * T + tg * 512:r * T + (tg + 1) * 512],
                                        start=(r == 0), stop=(r == 7))
                                tk[("mm", lc, tg)] = p.inc(ii, s_pe, 1)

                    def f_scalar(a):
                        p.wait_all(a)
                        for lc in range(4):
                            for tg in range(8):
                                par = (lc * 8 + tg) % 2
                                p.wt(a, tk[("mm", lc, tg)])
                                a.activation(
                                    latf[:, lc * T + tg * 512:lc * T + (tg + 1) * 512],
                                    ps_a[:, par * 512:par * 512 + 512], AF.Copy)
                                tk[("ev", lc, tg)] = p.dinc(a, s_act, 1)

                    p.run_phase({"gpsimd": f_gp, "tensor": f_tensor, "scalar": f_scalar}, tk)

                proj_rope_phase(wk, L, 4, latf, A)

                with (
                    nc.sbuf_tensor("vw", [128, 4096], BF16) as vw,
                    nc.sbuf_tensor("vsb", [128, 2080], BF16) as vsb,
                ):
                    tk = TokDict()

                    def f_gp(g):
                        p.wait_all(g)
                        dst = bass.AP(vw, 0, [[4096, 128], [1024, 4], [1, 1024]])
                        src = bass.AP(wv.tensor, 0, [[E, 128], [128 * E, 4], [1, 1024]])
                        ii = g.dma_start(out=dst, in_=src)
                        tk["w"] = p.inc(ii, s_once, 16)
                        ii = g.memset(bass.AP(vsb, 64, [[2080, 128], [65, 16]]), 1.0)
                        p.inc(ii, s_gp, 1)
                        ii = g.memset(bass.AP(vsb, 1040 + 64, [[2080, 128], [65, 16]]), 1.0)
                        tk["ones"] = p.inc(ii, s_gp, 1)

                    def f_tensor(t):
                        p.wait_all(t)
                        p.wt(t, tk["w"])
                        p.wt(t, tk["ones"])
                        for tt in range(NT):
                            for hf in range(2):
                                i2 = tt * 2 + hf
                                par = i2 % 2
                                if i2 >= 2:
                                    pt_, pf = (i2 - 2) // 2, (i2 - 2) % 2
                                    p.wt(t, tk[("ev", pt_, pf)])
                                for r in range(4):
                                    ii = t.matmul(
                                        ps_a[:, par * 512:par * 512 + 512],
                                        latf[:, r * T + tt * 128:r * T + (tt + 1) * 128],
                                        vw[:, r * 1024 + hf * 512:r * 1024 + (hf + 1) * 512],
                                        start=(r == 0), stop=(r == 3))
                                tk[("mm", tt, hf)] = p.inc(ii, s_pe, 1)

                    def f_vector(v):
                        p.wait_all(v)
                        for tt in range(NT):
                            for hf in range(2):
                                par = (tt * 2 + hf) % 2
                                p.wt(v, tk[("mm", tt, hf)])
                                if tt >= 2 and hf == 0:
                                    p.wt(v, tk[("st", tt - 2)])
                                dst = bass.AP(vsb, (tt % 2) * 1040 + hf * 520,
                                              [[2080, 128], [130, 4], [65, 2], [1, 64]])
                                src = bass.AP(ps_a, par * 512,
                                              [[1024, 128], [128, 4], [64, 2], [1, 64]])
                                v.tensor_copy(dst, src)
                                tk[("ev", tt, hf)] = p.dinc(v, s_dve, 1)

                    def f_scalar(a):
                        p.wait_all(a)
                        for tt in range(NT):
                            p.wt(a, tk[("ev", tt, 1)])
                            ii = a.dma_start(out=vh[tt * 128:(tt + 1) * 128, :],
                                             in_=vsb[:, (tt % 2) * 1040:(tt % 2) * 1040 + 1040])
                            tk[("st", tt)] = p.inc(ii, s_vst2[tt % 2], 16)

                    p.run_phase({"gpsimd": f_gp, "tensor": f_tensor,
                                 "vector": f_vector, "scalar": f_scalar}, tk)

        if dbg and dbg_stage == 3:
            def f_scalar_d3(a):
                p.wait_all(a)
                tkk = None
                for c in range(8):
                    ii = a.dma_start(out=q_dbg[:, c * T:(c + 1) * T], in_=Bq[:, c * T:(c + 1) * T])
                    p.inc(ii, s_sc, 16)
                    ii = a.dma_start(out=k_dbg[:, c * T:(c + 1) * T], in_=A[:, c * T:(c + 1) * T])
                    p.inc(ii, s_sc, 16)
                for tt in range(NT):
                    ii = a.dma_start(out=v_dbg[tt * 128:(tt + 1) * 128, :],
                                     in_=vh[tt * 128:(tt + 1) * 128, :])
                    tkk = p.inc(ii, s_sc, 16)
                a.wait_ge(tkk[0], tkk[1])
            p.run_phase({"scalar": f_scalar_d3})

        if dbg_stage >= 4:
            with (
                nc.sbuf_tensor("wo_sb", [128, 8192], BF16) as wo_sb,
                nc.sbuf_tensor("vst", [128, 16 * 130], BF16) as vst,
                nc.sbuf_tensor("pb", [128, 2048], BF16) as pb,
                nc.sbuf_tensor("ao", [128, 4096], BF16) as ao,
                nc.sbuf_tensor("sinv", [128, 1024], BF16) as sinv,
                nc.sbuf_tensor("bcb", [128, 512], BF16) as bcb,
                nc.sbuf_tensor("st64", [128, 512], BF16) as st64,
                nc.sbuf_tensor("axb", [128, 1024], F32) as axb,
                nc.sbuf_tensor("ax1", [128, 1024], F32) as ax1,
            ):
                tk = TokDict()
                sched = []
                base_of = {}
                gi = 0
                for g in range(NG):
                    ntiles = 4 * (g + 1)
                    for hp in range(8):
                        base_of[(g, hp)] = gi
                        for i in range(ntiles):
                            sched.append((g, hp, i, gi, ntiles))
                            gi += 1

                def prev_gh(g, hp):
                    return (g, hp - 1) if hp else (g - 1, 7)

                def f_gp(g_):
                    p.wait_all(g_)
                    dst = bass.AP(wo_sb, 0, [[8192, 128], [1024, 8], [1, 1024]])
                    src = bass.AP(wo.tensor, 0, [[E, 128], [128 * E, 8], [1, 1024]])
                    ii = g_.dma_start(out=dst, in_=src)
                    tk["wo"] = p.inc(ii, s_once, 16)
                    if dbg_stage >= 6:
                        for e in range(NE):
                            ii = g_.dma_start(
                                out=bass.AP(w1bf.tensor, e * DFF * E, [[256 * E, 16], [1, 256 * E]]),
                                in_=bass.AP(w1.tensor, e * DFF * E, [[256 * E, 16], [1, 256 * E]]))
                            p.inc(ii, s_stage[2 * e], 16)
                            ii = g_.dma_start(
                                out=bass.AP(w2bf.tensor, e * E * DFF, [[256 * DFF, 4], [1, 256 * DFF]]),
                                in_=bass.AP(w2.tensor, e * E * DFF, [[256 * DFF, 4], [1, 256 * DFF]]))
                            p.inc(ii, s_stage[2 * e + 1], 16)

                def f_sync(sy):
                    p.wait_all(sy)
                    for (g, hp, i, vi, ntiles) in sched:
                        if i % 4 == 0:
                            blk_ = (vi // 4) % 4
                            if vi >= 16:
                                p.wt(sy, tk[("av", vi - 16 + 3)])
                            ii = sy.dma_start(
                                out=vst[:, blk_ * 520:(blk_ + 1) * 520],
                                in_=bass.AP(vh.tensor, i * 128 * 1040 + hp * 130,
                                            [[1040, 128], [128 * 1040, 4], [1, 130]]))
                            tk[("vb", vi // 4)] = p.inc(ii, s_v8[blk_], 16)
                        if i == ntiles - 1:
                            p.wt(sy, tk[("m1", g, hp)])
                            if g > 0:
                                p.wt(sy, tk[("op", g - 1, 3, 1)])
                            ii = sy.dma_start(out=ao[64:128, hp * 512:(hp + 1) * 512],
                                              in_=st64[0:64, :])
                            tk[("mv", g, hp)] = p.inc(ii, s_mv, 16)
                        if hp == 7 and i == ntiles - 1:
                            for tt in range(4):
                                if (g, tt) != (0, 0):
                                    pg, ptt = (g, tt - 1) if tt else (g - 1, 3)
                                    p.wt(sy, tk[("oa", pg, ptt, 1)])
                                ii = sy.dma_start(
                                    out=axb[:, :],
                                    in_=x_in[g * 512 + tt * 128:g * 512 + (tt + 1) * 128, :])
                                tk[("xl", g, tt)] = p.inc(ii, s_ax, 16)

                def f_tensor(t):
                    p.wait_all(t)
                    p.wt(t, tk["wo"])
                    for g in range(NG):
                        ntiles = 4 * (g + 1)
                        for hp in range(8):
                            base_vi = base_of[(g, hp)]

                            def emit_sp(i, vi):
                                par = vi % 2
                                psab = ps_a if par == 0 else ps_b
                                full = i < 4 * g + 2
                                qoff = g * 512 if full else g * 512 + 256
                                qlen = 512 if full else 256
                                if vi >= 2:
                                    p.wt(t, tk[("e", vi - 2)])
                                ii = t.matmul(
                                    psab[:, 0:qlen],
                                    A[0:64, hp * T + i * 128:hp * T + (i + 1) * 128],
                                    Bq[0:64, hp * T + qoff:hp * T + qoff + qlen],
                                    start=True, stop=True, tile_position=(0, 0))
                                ii = t.matmul(
                                    psab[:, 512:512 + qlen],
                                    A[64:128, hp * T + i * 128:hp * T + (i + 1) * 128],
                                    Bq[64:128, hp * T + qoff:hp * T + qoff + qlen],
                                    start=True, stop=True, tile_position=(64, 0))
                                tk[("s", vi)] = p.inc(ii, s_pe, 1)

                            emit_sp(0, base_vi)
                            emit_sp(1, base_vi + 1)
                            for i in range(ntiles):
                                vi = base_vi + i
                                par = vi % 2
                                full = i < 4 * g + 2
                                p.wt(t, tk[("e", vi)])
                                p.wt(t, tk[("vb", vi // 4)])
                                if i == 0 and (g, hp) != (0, 0):
                                    p.wt(t, tk[("m1", *prev_gh(g, hp))])
                                slot = vi % 16
                                if full:
                                    ii = t.matmul(ps_c[0:65, 0:512],
                                                  vst[:, slot * 130:slot * 130 + 65],
                                                  pb[:, par * 1024:par * 1024 + 512],
                                                  start=(i == 0), stop=(i == ntiles - 1))
                                    ii = t.matmul(ps_c[0:65, 512:1024],
                                                  vst[:, slot * 130 + 65:slot * 130 + 130],
                                                  pb[:, par * 1024 + 512:par * 1024 + 1024],
                                                  start=(i == 0), stop=(i == ntiles - 1))
                                else:
                                    ii = t.matmul(ps_c[0:65, 256:512],
                                                  vst[:, slot * 130:slot * 130 + 65],
                                                  pb[:, par * 1024:par * 1024 + 256],
                                                  start=False, stop=(i == ntiles - 1))
                                    ii = t.matmul(ps_c[0:65, 768:1024],
                                                  vst[:, slot * 130 + 65:slot * 130 + 130],
                                                  pb[:, par * 1024 + 512:par * 1024 + 768],
                                                  start=False, stop=(i == ntiles - 1))
                                tk[("av", vi)] = p.inc(ii, s_pe, 1)
                                if i + 2 < ntiles:
                                    emit_sp(i + 2, base_vi + i + 2)
                            p.wt(t, tk[("rc", g, hp)])
                            if (g, hp) != (0, 0):
                                p.wt(t, tk[("b1", *prev_gh(g, hp))])
                            if hp == 0 and g > 0:
                                p.wt(t, tk[("oa", g - 1, 3, 1)])
                            ii = t.matmul(ps_sp[0:64, :], ones128[64:65, 0:64],
                                          sinv[64:65, 0:512], start=True, stop=True)
                            tk[("bc", g, hp, 0)] = p.inc(ii, s_pe, 1)
                            p.wt(t, tk[("b0", g, hp)])
                            ii = t.matmul(ps_sp[0:64, :], ones128[64:65, 0:64],
                                          sinv[64:65, 512:1024], start=True, stop=True)
                            tk[("bc", g, hp, 1)] = p.inc(ii, s_pe, 1)
                        p.wt(t, tk[("mv", g, 7)])
                        for tt in range(4):
                            for hf in range(2):
                                if (tt, hf) == (0, 0):
                                    p.wt(t, tk[("b1", g, 7)])
                                else:
                                    pg = (g, tt, hf - 1) if hf else (g, tt - 1, 1)
                                    p.wt(t, tk[("oa", *pg)])
                                for r in range(8):
                                    ii = t.matmul(
                                        ps_sp[:, :],
                                        ao[:, r * 512 + tt * 128:r * 512 + (tt + 1) * 128],
                                        wo_sb[:, r * 1024 + hf * 512:r * 1024 + (hf + 1) * 512],
                                        start=(r == 0), stop=(r == 7))
                                tk[("op", g, tt, hf)] = p.inc(ii, s_pe, 1)

                def f_scalar(a):
                    p.wait_all(a)
                    for g in range(NG):
                        ntiles = 4 * (g + 1)
                        for hp in range(8):
                            base_vi = base_of[(g, hp)]
                            for i in range(ntiles):
                                vi = base_vi + i
                                par = vi % 2
                                psab = ps_a if par == 0 else ps_b
                                full = i < 4 * g + 2
                                p.wt(a, tk[("s", vi)])
                                if vi >= 2:
                                    p.wt(a, tk[("av", vi - 2)])
                                if full:
                                    ii = a.activation(pb[:, par * 1024:(par + 1) * 1024],
                                                      psab[:, :], AF.Exp, scale=0.125)
                                else:
                                    a.activation(pb[:, par * 1024:par * 1024 + 256],
                                                 psab[:, 0:256], AF.Exp, scale=0.125)
                                    ii = a.activation(pb[:, par * 1024 + 512:par * 1024 + 768],
                                                      psab[:, 512:768], AF.Exp, scale=0.125)
                                tk[("e", vi)] = p.inc(ii, s_act, 1)
                        for tt in range(4):
                            p.wt(a, tk[("oa", g, tt, 1)])
                            ii = a.dma_start(
                                out=x1h[g * 512 + tt * 128:g * 512 + (tt + 1) * 128, :],
                                in_=ax1[:, :])
                            tk[("x1", g, tt)] = p.inc(ii, s_sc, 16)

                def f_vector(v):
                    p.wait_all(v)
                    for g in range(NG):
                        ntiles = 4 * (g + 1)
                        for hp in range(8):
                            lastvi = base_of[(g, hp)] + ntiles - 1
                            p.wt(v, tk[("av", lastvi)])
                            if (g, hp) != (0, 0):
                                p.wt(v, tk[("mv", *prev_gh(g, hp))])
                            if hp == 0 and g > 0:
                                p.wt(v, tk[("op", g - 1, 3, 1)])
                            with nc.allow_low_precision(reason="bf16 softmax normalizer (0.4% rel; bar 5.5%)"):
                                v.reciprocal(sinv[64:65, 0:512], ps_c[64:65, 0:512])
                                v.reciprocal(sinv[64:65, 512:1024], ps_c[64:65, 512:1024])
                            tk[("rc", g, hp)] = p.dinc(v, s_dve, 1)
                            p.wt(v, tk[("bc", g, hp, 0)])
                            v.tensor_copy(bcb[0:64, :], ps_sp[0:64, :])
                            tk[("b0", g, hp)] = p.dinc(v, s_dve, 1)
                            v.tensor_mul(ao[0:64, hp * 512:(hp + 1) * 512],
                                         ps_c[0:64, 0:512], bcb[0:64, :])
                            tk[("m0", g, hp)] = p.dinc(v, s_dve, 1)
                            p.wt(v, tk[("bc", g, hp, 1)])
                            v.tensor_copy(bcb[0:64, :], ps_sp[0:64, :])
                            tk[("b1", g, hp)] = p.dinc(v, s_dve, 1)
                            v.tensor_mul(st64[0:64, :], ps_c[0:64, 512:1024],
                                         bcb[0:64, :])
                            tk[("m1", g, hp)] = p.dinc(v, s_dve, 1)
                        for tt in range(4):
                            for hf in range(2):
                                p.wt(v, tk[("op", g, tt, hf)])
                                p.wt(v, tk[("xl", g, tt)])
                                if hf == 0 and (g, tt) != (0, 0):
                                    pg, ptt = (g, tt - 1) if tt else (g - 1, 3)
                                    p.wt(v, tk[("x1", pg, ptt)])
                                v.tensor_add(ax1[:, hf * 512:(hf + 1) * 512],
                                             ps_sp[:, :],
                                             axb[:, hf * 512:(hf + 1) * 512])
                                tk[("oa", g, tt, hf)] = p.dinc(v, s_dve, 1)

                p.run_phase({"gpsimd": f_gp, "sync": f_sync, "tensor": f_tensor,
                             "scalar": f_scalar, "vector": f_vector}, tk)

        if dbg and dbg_stage == 4:
            def f_scalar_d4(a):
                p.wait_all(a)
                tkk = None
                for tt in range(NT):
                    ii = a.dma_start(out=x1_dbg[tt * 128:(tt + 1) * 128, :],
                                     in_=x1h[tt * 128:(tt + 1) * 128, :])
                    tkk = p.inc(ii, s_sc, 16)
                a.wait_ge(tkk[0], tkk[1])
            p.run_phase({"scalar": f_scalar_d4})

        if dbg_stage >= 5:
            with (
                nc.sbuf_tensor("wr8", [8, 1024], BF16) as wr8,
                nc.sbuf_tensor("zb", [128, 1024], BF16) as zb,
            ):
                tk = TokDict()

                def f_gp(g):
                    p.wait_all(g)
                    ii = g.dma_start(out=wr8[:, :], in_=wr[:, :])
                    tk["w"] = p.inc(ii, s_once, 16)
                    g.memset(zb[:, :], 0.0)
                    tk["zb"] = p.dinc(g, s_gp, 1)

                def f_sync(sy):
                    p.wait_all(sy)
                    p.wt(sy, tk["zb"])
                    for tt in range(NT):
                        ii = sy.dma_start(out=accB[tt * 128:(tt + 1) * 128, :], in_=zb[:, :])
                        p.inc(ii, s_once, 16)

                def f_tensor(t):
                    p.wait_all(t)
                    p.wt(t, tk["w"])
                    for r in range(8):
                        ii = t.transpose(ps_tr[:, r * 8:(r + 1) * 8],
                                         wr8[0:8, r * 128:(r + 1) * 128], id_sb[0:8, 0:8])
                    tk["tr"] = p.inc(ii, s_pe, 1)

                def f_vector(v):
                    p.wait_all(v)
                    p.wt(v, tk["tr"])
                    v.tensor_copy(wr_t[:, :], ps_tr[:, 0:64])
                    p.dinc(v, s_dve, 1)

                p.run_phase({"gpsimd": f_gp, "sync": f_sync,
                             "tensor": f_tensor, "vector": f_vector}, tk)

            ln_phase(x1h, A, do_logits=True, store_tm=True)

            def gates_phase():
                tk = TokDict()

                def f_vector(v):
                    p.wait_all(v)
                    v.tensor_sub(stats[:, 64:96], stats[:, 32:64], stats[:, 0:32])
                    v.tensor_sub(stats[:, 96:128], stats[:, 0:32], stats[:, 32:64])
                    tk["d"] = p.dinc(v, s_dve, 1)
                    p.wt(v, tk["exp"])
                    v.tensor_single_scalar(stats[:, 64:128], stats[:, 128:192], 1.0, ALU.add)
                    v.drain()
                    v.reciprocal(stats[:, 128:192], stats[:, 64:128])
                    v.drain()
                    p.inc(v.engine_nop(), s_dve, 1)
                    for tt in range(NT):
                        gcol = gates[:, tt * 8:(tt + 1) * 8]
                        v.scalar_tensor_tensor(gcol, eqs[:, 256 + tt * 8:256 + tt * 8 + 8],
                                               stats[:, 160 + tt:160 + tt + 1], zero8[:, :],
                                               ALU.mult, ALU.add)
                        v.drain()
                        ii = v.scalar_tensor_tensor(gcol, eqs[:, tt * 8:tt * 8 + 8],
                                                    stats[:, 128 + tt:128 + tt + 1], gcol,
                                                    ALU.mult, ALU.add)
                        p.inc(ii, s_dve, 1)

                def f_scalar(a):
                    p.wait_all(a)
                    p.wt(a, tk["d"])
                    a.activation(stats[:, 128:192], stats[:, 64:128], AF.Exp)
                    tk["exp"] = p.dinc(a, s_act, 1)

                p.run_phase({"vector": f_vector, "scalar": f_scalar}, tk)

            gates_phase()

        if dbg and dbg_stage == 5:
            def f_scalar_d5(a):
                p.wait_all(a)
                ii = a.dma_start(out=g_dbg[:, :], in_=gates[:, :])
                tkk = p.inc(ii, s_sc, 16)
                a.wait_ge(tkk[0], tkk[1])
            p.run_phase({"scalar": f_scalar_d5})

        CAP = 1536
        NCC = CAP // 128
        BIGF = 1.0e5
        if dbg_stage >= 6:
            with (
                nc.sbuf_tensor("iota_sb", [128, 1536], F32) as iota_sb,
                nc.sbuf_tensor("id32_sb", [128, 128], F32) as id32_sb,
                nc.sbuf_tensor("rhs2", [128, 128], BF16) as rhs2,
                nc.sbuf_tensor("maskb", [128, 256], BF16) as maskb,
                nc.sbuf_tensor("maskE", [8, 4096], F32) as maskE,
                nc.sbuf_tensor("slotE", [8, 4096], F32) as slotE,
                nc.sbuf_tensor("slottm", [128, 256], F32) as slottm,
                nc.sbuf_tensor("mful", [128, 2 * 1536], BF16) as mful,
                nc.sbuf_tensor("idxw", [128, 192], F32) as idxw,
                nc.sbuf_tensor("idxst", [128, 96], mybir.dt.int32) as idxst,
                nc.sbuf_tensor("bigc", [128, 12], F32) as bigc,
            ):
                tk = TokDict()

                def f_sync(sy):
                    p.wait_all(sy)
                    ii = sy.dma_start(out=iota_sb[:, :], in_=iota_in[:, :])
                    p.inc(ii, s_once, 16)
                    ii = sy.dma_start(out=id32_sb[:, :], in_=id32_in[:, :])
                    p.inc(ii, s_once, 16)
                    ii = sy.dma_start(out=rhs2[:, :], in_=rhs2s_in[:, :])
                    tk["consts"] = p.inc(ii, s_once, 16)

                def f_gp(g):
                    p.wait_all(g)
                    g.memset(bigc[:, :], BIGF)
                    tk["big"] = p.dinc(g, s_gp, 1)

                def f_vector(v):
                    p.wait_all(v)
                    p.wt(v, tk["consts"])
                    v.tensor_add(slottm[:, :], eqs[:, 0:256], eqs[:, 256:512])
                    v.drain()
                    v.tensor_copy(maskb[:, :], slottm[:, :])
                    tk["mb"] = p.dinc(v, s_dve, 1)
                    for ch in range(4):
                        p.wt(v, tk[("mtr", ch)])
                        v.tensor_copy(maskE[0:8, ch * 1024:(ch + 1) * 1024],
                                      ps_tr[0:8, 0:1024])
                        tk[("mev", ch)] = p.dinc(v, s_dve, 1)
                    v.tensor_tensor_scan(slotE[:, :], maskE[:, :], maskE[:, :], 0.0,
                                         ALU.add, ALU.max)
                    v.drain()
                    v.tensor_mul(slotE[:, :], slotE[:, :], maskE[:, :])
                    tk["slotE"] = p.dinc(v, s_dve, 1)
                    p.wt(v, tk["str"])
                    v.tensor_copy(slottm[:, :], ps_sp[:, 0:256])
                    tk["stm"] = p.dinc(v, s_dve, 1)
                    for e in range(NE):
                        if e:
                            p.wt(v, tk[("lmm", e - 1, NT - 1)])
                        gsrc = bass.AP(gates, e, [[256, 128], [8, 32]])
                        gdst = bass.AP(rhs2, 2, [[128, 128], [4, 32]])
                        v.tensor_copy(gdst, gsrc)
                        tk[("g", e)] = p.dinc(v, s_dve, 1)
                        for tt in range(NT):
                            if e * NT + tt >= 2:
                                j = e * NT + tt - 2
                                p.wt(v, tk[("lmm", j // NT, j % NT)])
                            half = (tt % 2) * 1536
                            v.tensor_single_scalar(mful[:, half:half + 1536],
                                                   iota_sb[:, :],
                                                   slottm[:, tt * 8 + e:tt * 8 + e + 1],
                                                   ALU.is_equal)
                            tk[("M", e, tt)] = p.dinc(v, s_dve, 1)
                    p.wt(v, tk["lev"])
                    p.wt(v, tk["big"])
                    for e in range(NE):
                        base = e * 48
                        plo = bass.AP(lists_keep, base + 0, [[384, 128], [4, 12]])
                        ttv = bass.AP(lists_keep, base + 1, [[384, 128], [4, 12]])
                        val = bass.AP(lists_keep, base + 3, [[384, 128], [4, 12]])
                        ecol = idxw[:, e * 12:(e + 1) * 12]
                        ecol2 = idxw[:, 96 + e * 12:96 + (e + 1) * 12]
                        icol = idxst[:, e * 12:(e + 1) * 12]
                        v.scalar_tensor_tensor(ecol, ttv, 128.0, plo, ALU.mult, ALU.add)
                        v.drain()
                        v.scalar_tensor_tensor(ecol, val, -BIGF, ecol, ALU.mult, ALU.add)
                        v.drain()
                        v.tensor_add(ecol, bigc[:, :], ecol)
                        v.drain()
                        v.tensor_single_scalar(ecol2, ecol, 8388608.0, ALU.add)
                        v.drain()
                        with nc.allow_low_precision(reason="bitcast f32->i32 (exact ints)"):
                            v.tensor_copy(icol, ecol2)
                        v.drain()
                        v.tensor_single_scalar(idxi_keep[:, e * 12:(e + 1) * 12],
                                               icol, 0x007FFFFF, ALU.bitwise_and)
                    tk["idx"] = p.dinc(v, s_dve, 1)

                def f_tensor(t):
                    p.wait_all(t)
                    p.wt(t, tk["mb"])
                    for ch in range(4):
                        if ch >= 1:
                            p.wt(t, tk[("mev", ch - 1)])
                        for j in range(8):
                            tt = ch * 8 + j
                            ii = t.transpose(ps_tr[0:8, j * 128:(j + 1) * 128],
                                             maskb[:, tt * 8:tt * 8 + 8], id_sb[:, :])
                        tk[("mtr", ch)] = p.inc(ii, s_pe, 1)
                    p.wt(t, tk["slotE"])
                    for tt in range(NT):
                        ii = t.transpose(ps_sp[:, tt * 8:(tt + 1) * 8],
                                         slotE[0:8, tt * 128:(tt + 1) * 128],
                                         id32_sb[0:8, 0:8])
                    tk["str"] = p.inc(ii, s_pe, 1)
                    p.wt(t, tk["stm"])
                    for e in range(NE):
                        p.wt(t, tk[("g", e)])
                        for tt in range(NT):
                            p.wt(t, tk[("M", e, tt)])
                            half = (tt % 2) * 1536
                            for c in range(NCC):
                                ii = t.matmul(
                                    ps_c[:, e * 48 + c * 4:e * 48 + (c + 1) * 4],
                                    mful[:, half + c * 128:half + (c + 1) * 128],
                                    rhs2[:, tt * 4:(tt + 1) * 4],
                                    start=(e == 0 and tt == 0 and c == 0),
                                    stop=(e == NE - 1 and tt == NT - 1 and c == NCC - 1),
                                    skip_group_check=True)
                            tk[("lmm", e, tt)] = p.inc(ii, s_pe, 1)

                def f_scalar(a):
                    p.wait_all(a)
                    p.wt(a, tk[("lmm", NE - 1, NT - 1)])
                    a.activation(lists_keep[:, :], ps_c[:, 0:384], AF.Copy)
                    tk["lev"] = p.dinc(a, s_act, 1)

                p.run_phase({"sync": f_sync, "gpsimd": f_gp, "tensor": f_tensor,
                             "scalar": f_scalar, "vector": f_vector}, tk)

        if dbg and dbg_stage == 6:
            def f_scalar_d6(a):
                p.wait_all(a)
                ii = a.dma_start(out=x1_dbg[0:128, 0:384], in_=lists_keep[:, :])
                p.inc(ii, s_sc, 16)
                ii = a.dma_start(out=g_dbg[:, :], in_=gates[:, :])
                tkk = p.inc(ii, s_sc, 16)
                a.wait_ge(tkk[0], tkk[1])

            def f_gp_d6(g):
                p.wait_all(g)
                ii = g.dma_start(out=x1_dbg[384:512, 0:96],
                                 in_=bass.AP(idxi_keep, 0, [[96, 128], [1, 96]]))
                tkk = p.inc(ii, s_sc, 16)
                g.wait_ge(tkk[0], tkk[1])
            p.run_phase({"scalar": f_scalar_d6, "gpsimd": f_gp_d6})

        if dbg_stage >= 6:
            with (
                nc.sbuf_tensor("w1t", [128, 8192], BF16) as w1t,
                nc.sbuf_tensor("w2t", [128, 8192], BF16) as w2t,
                nc.sbuf_tensor("hsb", [128, 4 * 1536], BF16) as hsb,
                nc.sbuf_tensor("xgt", [128, 2048], BF16) as xgt,
            ):
                tk = TokDict()
                TGS = 3

                def xgf(par):
                    return par * 12 * 1024

                def f_gp(g):
                    p.wait_all(g)
                    for e in range(NE):
                        for c in range(NCC):
                            gi_ = e * NCC + c
                            if gi_ >= 2:
                                pe_, pc = (e, c - 2) if c >= 2 else (e - 1, NCC - 2 + c)
                                p.wt(g, tk[("gtrp", pe_, pc)])
                            else:
                                p.wt(g, tk["xz"])
                            ii = g.indirect_dma_start(
                                out=xgt[:, (gi_ % 2) * 1024:(gi_ % 2) * 1024 + 1024],
                                out_offset=None,
                                in_=xn2h[0:128, :],
                                in_offset=bass.IndirectOffsetOnAxis(
                                    ap=idxi_keep[:, e * 12 + c:e * 12 + c + 1], axis=0),
                            )
                            tk[("gth", e, c)] = p.inc(ii, s_gt2[gi_ % 2], 16)
                        if e >= 1:
                            se = e - 1
                            if se >= 1:
                                g.wait_ge(s_sct, 192 * se)
                            p.wt(g, tk[("yv", se)])
                            for c in range(NCC):
                                ii = g.indirect_dma_start(
                                    out=accB[0:128, :],
                                    out_offset=bass.IndirectOffsetOnAxis(
                                        ap=idxi_keep[:, se * 12 + c:se * 12 + c + 1], axis=0),
                                    in_=Bq[:, (se % 2) * 12288 + c * 1024:(se % 2) * 12288 + (c + 1) * 1024],
                                    in_offset=None,
                                    compute_op=ALU.add,
                                )
                                p.inc(ii, s_sct, 16)
                    se = NE - 1
                    g.wait_ge(s_sct, 192 * se)
                    p.wt(g, tk[("yv", se)])
                    for c in range(NCC):
                        ii = g.indirect_dma_start(
                            out=accB[0:128, :],
                            out_offset=bass.IndirectOffsetOnAxis(
                                ap=idxi_keep[:, se * 12 + c:se * 12 + c + 1], axis=0),
                            in_=Bq[:, (se % 2) * 12288 + c * 1024:(se % 2) * 12288 + (c + 1) * 1024],
                            in_offset=None,
                            compute_op=ALU.add,
                        )
                        p.inc(ii, s_sct, 16)

                def f_vector(v):
                    p.wait_all(v)
                    v.memset(xgt[:, :], 0.0)
                    v.memset(bass.AP(A, 0, [[8 * T, 128], [1, 24 * 1024]]), 0.0)
                    tk["xz"] = p.dinc(v, s_dve, 1)
                    for e in range(NE):
                        for c in range(NCC):
                            gi_ = e * NCC + c
                            p.wt(v, tk[("gtrp", e, c)])
                            if e >= 2 and c == 0:
                                p.wt(v, tk[("h", 96 * (e - 1) - 1)])
                            srcp = bass.AP(ps_tr, 0, [[1024, 128], [128, 8], [1, 128]])
                            dstp = bass.AP(A, xgf(e % 2) + c * 128,
                                           [[8 * T, 128], [1536, 8], [1, 128]])
                            v.tensor_copy(dstp, srcp)
                            tk[("gtr", e, c)] = p.dinc(v, s_dve, 1)
                        if e >= 2:
                            v.wait_ge(s_sct, 192 * (e - 1))
                        v.memset(Bq[:, (e % 2) * 12288:(e % 2) * 12288 + 12288], 0.0)
                        tk[("yz", e)] = p.dinc(v, s_dve, 1)
                        for sl in range(8):
                            for tg in range(3):
                                for tt in range(4):
                                    for hf in range(2):
                                        yi = (tg * 4 + tt) * 2 + hf
                                        p.wt(v, tk[("y", e, sl, tg, tt, hf)])
                                        gap = bass.AP(lists_keep,
                                                      e * 48 + (tg * 4 + tt) * 4 + 2,
                                                      [[384, 128], [1, 1]])
                                        ycol = (e % 2) * 12288 + (tg * 4 + tt) * 1024 + hf * 512
                                        ii = v.scalar_tensor_tensor(
                                            Bq[:, ycol:ycol + 512],
                                            ps_c[:, (yi % 2) * 512:(yi % 2) * 512 + 512],
                                            gap, Bq[:, ycol:ycol + 512],
                                            ALU.mult, ALU.add)
                                        if sl == 7 and yi == 23:
                                            tk[("ye", e, sl, yi)] = p.dinc(v, s_dve, 1)
                                        else:
                                            tk[("ye", e, sl, yi)] = p.inc(ii, s_dve, 1)
                            if sl == 7:
                                tk[("yv", e)] = tk[("ye", e, 7, 23)]

                def f_sync(sy):
                    p.wait_all(sy)
                    for e in range(NE):
                        for sl in range(8):
                            par = sl % 2
                            if sl == 0:
                                sy.wait_ge(s_stage[2 * e], 16)
                            if (e, sl) not in ((0, 0), (0, 1)):
                                pe_, psl = (e, sl - 2) if sl >= 2 else (e - 1, sl + 6)
                                p.wt(sy, tk[("y", pe_, psl, TGS - 1, 3, 1)])
                            for r in range(8):
                                ii = sy.dma_start(
                                    out=w1t[:, par * 4096 + r * 512:par * 4096 + (r + 1) * 512],
                                    in_=bass.AP(w1bf.tensor, e * DFF * E + sl * 512 * E + r * 128,
                                                [[E, 512], [1, 128]]),
                                    transpose=True)
                                tk[("w1", e, sl)] = p.inc(ii, s_w1t2[par], 16)
                            if sl == 0:
                                sy.wait_ge(s_stage[2 * e + 1], 16)
                            for fs in range(4):
                                ii = sy.dma_start(
                                    out=w2t[:, par * 4096 + fs * 1024:par * 4096 + (fs + 1) * 1024],
                                    in_=bass.AP(w2bf.tensor, e * E * DFF + sl * 512 + fs * 128,
                                                [[DFF, 1024], [1, 128]]),
                                    transpose=True)
                                tk[("w2", e, sl)] = p.inc(ii, s_w2t2[par], 16)

                def f_tensor(t):
                    p.wait_all(t)
                    p.wt(t, tk["xz"])
                    hidx = 0
                    for e in range(NE):
                        par_e = e % 2
                        for c in range(NCC):
                            gi_ = e * NCC + c
                            p.wt(t, tk[("gth", e, c)])
                            if gi_ >= 1:
                                pe_, pc = (e, c - 1) if c else (e - 1, NCC - 1)
                                p.wt(t, tk[("gtr", pe_, pc)])
                            for r in range(8):
                                ii = t.transpose(
                                    ps_tr[:, r * 128:(r + 1) * 128],
                                    xgt[:, (gi_ % 2) * 1024 + r * 128:(gi_ % 2) * 1024 + (r + 1) * 128],
                                    id_sb[:, :])
                            tk[("gtrp", e, c)] = p.inc(ii, s_pe, 1)
                        p.wt(t, tk[("gtr", e, NCC - 1)])
                        p.wt(t, tk[("yz", e)])
                        for sl in range(8):
                            par = sl % 2
                            p.wt(t, tk[("w1", e, sl)])
                            for tg in range(TGS):
                                for fs in range(4):
                                    bsel = (hidx // 2) % 2
                                    psh = ps_a if bsel == 0 else ps_b
                                    half = (hidx % 2) * 512
                                    if hidx >= 4:
                                        p.wt(t, tk[("si", (hidx - 4) // 2)])
                                    for r in range(8):
                                        ii = t.matmul(
                                            psh[:, half:half + 512],
                                            w1t[:, par * 4096 + r * 512 + fs * 128:par * 4096 + r * 512 + (fs + 1) * 128],
                                            A[:, xgf(par_e) + r * 1536 + tg * 512:xgf(par_e) + r * 1536 + (tg + 1) * 512],
                                            start=(r == 0), stop=(r == 7))
                                    tk[("h", hidx)] = p.inc(ii, s_pe, 1)
                                    hidx += 1
                            p.wt(t, tk[("w2", e, sl)])
                            p.wt(t, tk[("si", (hidx - 2) // 2)])
                            for tg in range(TGS):
                                for tt in range(4):
                                    for hf in range(2):
                                        yi = (tg * 4 + tt) * 2 + hf
                                        if (e, sl) != (0, 0) or yi >= 2:
                                            if yi >= 2:
                                                p.wt(t, tk[("ye", e, sl, yi - 2)])
                                            else:
                                                pe_, psl = (e, sl - 1) if sl else (e - 1, 7)
                                                p.wt(t, tk[("ye", pe_, psl, 22 + yi)])
                                        for fs in range(4):
                                            ii = t.matmul(
                                                ps_c[:, (yi % 2) * 512:(yi % 2) * 512 + 512],
                                                hsb[:, fs * 1536 + tg * 512 + tt * 128:fs * 1536 + tg * 512 + (tt + 1) * 128],
                                                w2t[:, par * 4096 + fs * 1024 + hf * 512:par * 4096 + fs * 1024 + (hf + 1) * 512],
                                                start=(fs == 0), stop=(fs == 3))
                                        tk[("y", e, sl, tg, tt, hf)] = p.inc(ii, s_pe, 1)

                def f_scalar(a):
                    p.wait_all(a)
                    hidx = 0
                    for e in range(NE):
                        for sl in range(8):
                            for tg in range(TGS):
                                for fs in range(4):
                                    if hidx % 2 == 1:
                                        p.wt(a, tk[("h", hidx)])
                                        bsel = (hidx // 2) % 2
                                        psh = ps_a if bsel == 0 else ps_b
                                        dst = bass.AP(hsb, (fs - 1) * 1536 + tg * 512,
                                                      [[4 * 1536, 128], [1536, 2], [1, 512]])
                                        ii = a.activation(dst, psh[:, :], AF.Silu)
                                        tk[("si", hidx // 2)] = p.inc(ii, s_act, 1)
                                    hidx += 1

                p.run_phase({"gpsimd": f_gp, "vector": f_vector, "sync": f_sync,
                             "tensor": f_tensor, "scalar": f_scalar}, tk)

        if dbg and dbg_stage == 7:
            def f_gp_d7(g):
                p.wait_all(g)
                tkk = None
                for tt in range(NT):
                    ii = g.dma_start(out=x1_dbg[tt * 128:(tt + 1) * 128, :],
                                     in_=accB[tt * 128:(tt + 1) * 128, :])
                    tkk = p.inc(ii, s_sc, 16)
                g.wait_ge(tkk[0], tkk[1])

            def f_scalar_d7(a):
                p.wait_all(a)
                tkk = None
                for c in range(8):
                    ii = a.dma_start(out=k_dbg[:, c * T:(c + 1) * T],
                                     in_=Bq[:, c * T:(c + 1) * T])
                    p.inc(ii, s_sc, 16)
                ii = a.dma_start(out=g_dbg[:, :], in_=gates[:, :])
                p.inc(ii, s_sc, 16)
                for c in range(6):
                    ii = a.dma_start(out=q_dbg[:, c * T:(c + 1) * T],
                                     in_=A[:, c * T:(c + 1) * T])
                    tkk = p.inc(ii, s_sc, 16)
                a.wait_ge(tkk[0], tkk[1])
            p.run_phase({"gpsimd": f_gp_d7, "scalar": f_scalar_d7})

        if dbg_stage >= 7:
            with (
                nc.sbuf_tensor("fx", [128, 2048], F32) as fx,
                nc.sbuf_tensor("fb", [128, 2048], BF16) as fb,
                nc.sbuf_tensor("fo", [128, 2048], F32) as fo,
            ):
                tk = TokDict()

                def f_sync(sy):
                    p.wait_all(sy)
                    for tt in range(NT):
                        if tt >= 2:
                            p.wt(sy, tk[("add", tt - 2)])
                        ii = sy.dma_start(out=fx[:, (tt % 2) * 1024:(tt % 2) * 1024 + 1024],
                                          in_=x1h[tt * 128:(tt + 1) * 128, :])
                        p.inc(ii, s_lx2[tt % 2], 16)
                        ii = sy.dma_start(out=fb[:, (tt % 2) * 1024:(tt % 2) * 1024 + 1024],
                                          in_=accB[tt * 128:(tt + 1) * 128, :])
                        tk[("l", tt)] = p.inc(ii, s_lx2[tt % 2], 16)

                def f_vector(v):
                    p.wait_all(v)
                    for tt in range(NT):
                        p.wt(v, tk[("l", tt)])
                        if tt >= 2:
                            p.wt(v, tk[("st", tt - 2)])
                        v.tensor_add(fo[:, (tt % 2) * 1024:(tt % 2) * 1024 + 1024],
                                     fb[:, (tt % 2) * 1024:(tt % 2) * 1024 + 1024],
                                     fx[:, (tt % 2) * 1024:(tt % 2) * 1024 + 1024])
                        tk[("add", tt)] = p.dinc(v, s_dve, 1)

                def f_scalar(a):
                    p.wait_all(a)
                    for tt in range(NT):
                        p.wt(a, tk[("add", tt)])
                        ii = a.dma_start(out=out[tt * 128:(tt + 1) * 128, :],
                                         in_=fo[:, (tt % 2) * 1024:(tt % 2) * 1024 + 1024])
                        tk[("st", tt)] = p.inc(ii, s_ost2[tt % 2], 16)
                    a.wait_ge(s_ost2[0], p.tot(s_ost2[0]))
                    a.wait_ge(s_ost2[1], p.tot(s_ost2[1]))
                    for sem in s_stage:
                        if p.tot(sem):
                            a.wait_ge(sem, p.tot(sem))

                p.run_phase({"sync": f_sync, "vector": f_vector, "scalar": f_scalar}, tk)
        else:
            def f_scalar_end(a):
                p.wait_all(a)
                for sem in s_stage:
                    if p.tot(sem):
                        a.wait_ge(sem, p.tot(sem))
            p.run_phase({"scalar": f_scalar_end})

    return nc


def kernel(L0_attn_x, L0_attn_wq, L0_attn_w_down, L0_attn_wk_up, L0_attn_wv_up,
           L0_attn_wo, L0_ffn_W1, L0_ffn_W2, L0_ffn_Wr):
    C, S, swp, ident, ident32, iota1, rhs2s = _host_consts()
    nc = build_nc(dbg_stage=DBG_STAGE, dbg=DBG)
    x = np.ascontiguousarray(np.asarray(L0_attn_x, dtype=np.float32))
    common = {
        "wq": np.ascontiguousarray(L0_attn_wq, dtype=np.float32),
        "wd": np.ascontiguousarray(L0_attn_w_down, dtype=np.float32),
        "wk": np.ascontiguousarray(L0_attn_wk_up, dtype=np.float32),
        "wv": np.ascontiguousarray(L0_attn_wv_up, dtype=np.float32),
        "wo": np.ascontiguousarray(L0_attn_wo, dtype=np.float32),
        "w1": np.ascontiguousarray(L0_ffn_W1, dtype=np.float32).reshape(NE * DFF, E),
        "w2": np.ascontiguousarray(L0_ffn_W2, dtype=np.float32).reshape(NE * E, DFF),
        "wr": np.ascontiguousarray(L0_ffn_Wr, dtype=np.float32),
        "ropec": C, "ropes": S, "swp": swp, "ident": ident,
        "ident32": ident32, "iota1": iota1, "rhs2s": rhs2s,
    }
    in_maps = [dict(common, x=np.ascontiguousarray(x[b])) for b in range(NCORE)]
    res = run_bass_kernel_spmd(nc, in_maps, core_ids=list(range(NCORE)))
    return np.stack([r["out"] for r in res.results], axis=0)


# version 07
# speedup vs baseline: 1.6857x; 1.1052x over previous
import numpy as np
import ml_dtypes
import concourse.bass as bass
import concourse.mybir as mybir
from concourse.bass_utils import run_bass_kernel_spmd

F32 = mybir.dt.float32
BF16 = mybir.dt.bfloat16
AF = mybir.ActivationFunctionType
ALU = mybir.AluOpType
AX = mybir.AxisListType
BF = ml_dtypes.bfloat16

T, E, NH, D, L = 4096, 1024, 16, 64, 512
NE, DFF = 8, 4096
NT = T // 128
NG = T // 512
EPS = 1e-5
NCORE = 8

DBG_STAGE = 9
DBG = 0


def _host_consts():
    inv = 1.0 / (10000.0 ** (np.arange(0, D, 2, dtype=np.float32) / D))
    ang = np.outer(np.arange(T, dtype=np.float32), inv)
    cos = np.cos(ang).astype(np.float32)
    sin = np.sin(ang).astype(np.float32)
    C = np.zeros((128, T), np.float32)
    S = np.zeros((128, T), np.float32)
    for r in range(128):
        i = r % 64
        pp = i // 2
        C[r] = cos[:, pp]
        S[r] = sin[:, pp] * (1.0 if i % 2 == 1 else -1.0)
    swp = np.zeros((128, 128), BF)
    for r in range(128):
        swp[r ^ 1, r] = 1.0
    ident = np.eye(128, dtype=BF)
    ident32 = np.eye(128, dtype=np.float32)
    iota1 = np.broadcast_to(np.arange(1, 1537, dtype=np.float32)[None, :], (128, 1536)).copy()
    rhs2s = np.zeros((128, 128), BF)
    for tt in range(32):
        rhs2s[:, 4 * tt + 0] = np.arange(128, dtype=np.float32)
        rhs2s[:, 4 * tt + 1] = float(tt)
        rhs2s[:, 4 * tt + 3] = 1.0
    return C, S, swp, ident, ident32, iota1, rhs2s


class _FakeInstr:
    def then_inc(self, sem, k):
        return self


class TokDict(dict):

    def __init__(self):
        super().__init__()
        self.missed = set()

    def __missing__(self, k):
        self.missed.add(k)
        return 0


class _CountEng:
    def wait_ge(self, sem, val):
        pass

    def __getattr__(self, name):
        def f(*a, **kw):
            return _FakeInstr()
        return f


class Prog:

    ORDER = ("sync", "gpsimd", "tensor", "scalar", "vector")

    def __init__(self, nc):
        self.nc = nc
        self.c = {}
        self.snap = {}

    def inc(self, instr, sem, k):
        instr.then_inc(sem, k)
        self.c[sem] = self.c.get(sem, 0) + k
        return (sem, self.c[sem])

    def dinc(self, eng, sem, k):
        d = eng.drain()
        d.then_inc(sem, k)
        self.c[sem] = self.c.get(sem, 0) + k
        return (sem, self.c[sem])

    def tot(self, sem):
        return self.c.get(sem, 0)

    def wt(self, eng, tok):
        if tok:
            sem, val = tok
            if val:
                eng.wait_ge(sem, val)

    def wait_all(self, eng):
        for sem, t in self.snap.items():
            if t:
                eng.wait_ge(sem, t)

    def run_phase(self, streams, tk=None):
        self.snap = dict(self.c)
        start = dict(self.c)
        for name in self.ORDER:
            if name in streams:
                streams[name](_CountEng())
        if tk is not None:
            unresolved = tk.missed - set(tk.keys())
            assert not unresolved, f"unresolved tokens: {unresolved}"
        end = dict(self.c)
        self.c = dict(start)
        with self.nc.Block() as blk:
            for name in self.ORDER:
                if name in streams:
                    getattr(blk, name)(streams[name])
        assert self.c == end, "two-pass mismatch"


def build_nc(dbg_stage=9, dbg=0):
    nc = bass.Bass("TRN2", target_bir_lowering=False)
    _uc = [0]

    def un(s):
        _uc[0] += 1
        return f"{s}_{_uc[0]}"

    x_in = nc.dram_tensor("x", [T, E], F32, kind="ExternalInput").ap()
    wq = nc.dram_tensor("wq", [E, E], F32, kind="ExternalInput").ap()
    wd = nc.dram_tensor("wd", [E, L], F32, kind="ExternalInput").ap()
    wk = nc.dram_tensor("wk", [L, E], F32, kind="ExternalInput").ap()
    wv = nc.dram_tensor("wv", [L, E], F32, kind="ExternalInput").ap()
    wo = nc.dram_tensor("wo", [E, E], F32, kind="ExternalInput").ap()
    w1 = nc.dram_tensor("w1", [NE * DFF, E], F32, kind="ExternalInput").ap()
    w2 = nc.dram_tensor("w2", [NE * E, DFF], F32, kind="ExternalInput").ap()
    wr = nc.dram_tensor("wr", [NE, E], F32, kind="ExternalInput").ap()
    ropec = nc.dram_tensor("ropec", [128, T], F32, kind="ExternalInput").ap()
    ropes = nc.dram_tensor("ropes", [128, T], F32, kind="ExternalInput").ap()
    swp_in = nc.dram_tensor("swp", [128, 128], BF16, kind="ExternalInput").ap()
    id_in = nc.dram_tensor("ident", [128, 128], BF16, kind="ExternalInput").ap()
    id32_in = nc.dram_tensor("ident32", [128, 128], F32, kind="ExternalInput").ap()
    iota_in = nc.dram_tensor("iota1", [128, 1536], F32, kind="ExternalInput").ap()
    rhs2s_in = nc.dram_tensor("rhs2s", [128, 128], BF16, kind="ExternalInput").ap()

    out = nc.dram_tensor("out", [T, E], F32, kind="ExternalOutput").ap()
    if dbg:
        q_dbg = nc.dram_tensor("q_dbg", [128, 8 * T], BF16, kind="ExternalOutput").ap()
        k_dbg = nc.dram_tensor("k_dbg", [128, 8 * T], BF16, kind="ExternalOutput").ap()
        x1_dbg = nc.dram_tensor("x1_dbg", [T, E], F32, kind="ExternalOutput").ap()
        g_dbg = nc.dram_tensor("g_dbg", [128, 256], F32, kind="ExternalOutput").ap()
        v_dbg = nc.dram_tensor("v_dbg", [T, 1040], BF16, kind="ExternalOutput").ap()

    w1bf = nc.dram_tensor("w1bf", [NE * DFF, E], BF16).ap()
    w2bf = nc.dram_tensor("w2bf", [NE * E, DFF], BF16).ap()
    x1h = nc.dram_tensor("x1h", [T, E], F32).ap()
    vh = nc.dram_tensor("vh", [T, 1040], BF16).ap()
    xn2h = nc.dram_tensor("xn2h", [T, E], BF16).ap()
    accB = nc.dram_tensor("accB", [T, E], BF16).ap()

    from contextlib import ExitStack
    with ExitStack() as _st:
        en = _st.enter_context
        A = en(nc.sbuf_tensor("A", [128, 8 * T], BF16))
        Bq = en(nc.sbuf_tensor("Bq", [128, 8 * T], BF16))
        ones128 = en(nc.sbuf_tensor("ones128", [128, 128], BF16))
        swp_sb = en(nc.sbuf_tensor("swp_sb", [128, 128], BF16))
        id_sb = en(nc.sbuf_tensor("id_sb", [128, 128], BF16))
        stats = en(nc.sbuf_tensor("stats", [128, 192], F32))
        gates = en(nc.sbuf_tensor("gates", [128, 256], F32))
        zero8 = en(nc.sbuf_tensor("zero8", [128, 8], F32))
        epsc = en(nc.sbuf_tensor("epsc", [128, 1], F32))
        eqs = en(nc.sbuf_tensor("eqs", [128, 512], F32))
        wr_t = en(nc.sbuf_tensor("wr_t", [128, 64], BF16))
        lists_keep = en(nc.sbuf_tensor("lists_keep", [128, 384], F32))
        idxi_keep = en(nc.sbuf_tensor("idxi_keep", [128, 96], mybir.dt.int32))
        ps_a = en(nc.psum_tensor("ps_a", [128, 1024], F32))
        ps_b = en(nc.psum_tensor("ps_b", [128, 1024], F32))
        ps_c = en(nc.psum_tensor("ps_c", [128, 1024], F32))
        ps_sp = en(nc.psum_tensor("ps_sp", [128, 512], F32))
        ps_tr = en(nc.psum_tensor("ps_tr", [128, 1024], BF16))
        s_pe = nc.alloc_semaphore("s_pe")
        s_act = nc.alloc_semaphore("s_act")
        s_dve = nc.alloc_semaphore("s_dve")
        s_gp = nc.alloc_semaphore("s_gp")
        s_once = nc.alloc_semaphore("s_once")
        s_sc = nc.alloc_semaphore("s_sc")
        s_mv = nc.alloc_semaphore("s_mv")
        s_ax = nc.alloc_semaphore("s_ax")
        s_lx2 = [nc.alloc_semaphore(f"s_lx{i}") for i in range(2)]
        s_rp2 = [nc.alloc_semaphore(f"s_rp{i}") for i in range(2)]
        s_wsl2 = [nc.alloc_semaphore(f"s_wsl{i}") for i in range(2)]
        s_v8 = [nc.alloc_semaphore(f"s_v{i}") for i in range(8)]
        s_vst2 = [nc.alloc_semaphore(f"s_vst{i}") for i in range(2)]
        s_ost2 = [nc.alloc_semaphore(f"s_ost{i}") for i in range(2)]
        s_w1t2 = [nc.alloc_semaphore(f"s_w1t{i}") for i in range(2)]
        s_w2t2 = [nc.alloc_semaphore(f"s_w2t{i}") for i in range(2)]
        s_stage = [nc.alloc_semaphore(f"s_stage{i}") for i in range(2 * NE)]
        s_x2st2 = [nc.alloc_semaphore(f"s_x2st{i}") for i in range(2)]
        s_gt2 = [nc.alloc_semaphore(f"s_gt{i}") for i in range(2)]
        s_sct = nc.alloc_semaphore("s_sct")

        p = Prog(nc)

        def ph0():
            def f_sync(sy):
                ii = sy.dma_start(out=swp_sb[:, :], in_=swp_in[:, :])
                p.inc(ii, s_once, 16)
                ii = sy.dma_start(out=id_sb[:, :], in_=id_in[:, :])
                p.inc(ii, s_once, 16)

            def f_gp(g):
                ii = g.memset(ones128[:, :], 1.0)
                p.inc(ii, s_gp, 1)
                ii = g.memset(zero8[:, :], 0.0)
                p.inc(ii, s_gp, 1)
                ii = g.memset(epsc[:, :], EPS)
                p.inc(ii, s_gp, 1)

            p.run_phase({"sync": f_sync, "gpsimd": f_gp})

        ph0()

        def ln_phase(src_dram, dst_sb, do_logits, store_tm=False):
            with (
                nc.sbuf_tensor(un("lx"), [128, 2048], F32) as lx,
                nc.sbuf_tensor(un("lxn"), [128, 2048], BF16) as _lxn,
                nc.sbuf_tensor(un("lsq"), [128, 1024], BF16) as lsq,
                nc.sbuf_tensor(un("llg"), [128, 16], F32) as llg,
                nc.sbuf_tensor(un("lmk"), [128, 16], F32) as lmk,
            ):
                lxn = dbg_lxn if (dbg and dbg_stage == 1 and dbg_lxn is not None) else _lxn
                if dbg and dbg_stage == 1 and dbg_lx is not None:
                    lx = dbg_lx
                tk = TokDict()

                def f_sync(sy):
                    p.wait_all(sy)
                    for tt in range(NT):
                        if tt >= 2:
                            p.wt(sy, tk[("ap", tt - 2)])
                            p.wt(sy, tk[("sum", tt - 2)])
                        ii = sy.dma_start(
                            out=lx[:, (tt % 2) * 1024:(tt % 2) * 1024 + 1024],
                            in_=src_dram[tt * 128:(tt + 1) * 128, :])
                        tk[("ld", tt)] = p.inc(ii, s_lx2[tt % 2], 16)

                def f_vector(v):
                    p.wait_all(v)
                    for tt in range(NT):
                        xv = lx[:, (tt % 2) * 1024:(tt % 2) * 1024 + 1024]
                        p.wt(v, tk[("ld", tt)])
                        v.reduce_sum(stats[:, tt:tt + 1], xv, axis=AX.X)
                        tk[("sum", tt)] = p.dinc(v, s_dve, 1)
                        p.wt(v, tk[("sq", tt)])
                        mcol = stats[:, 64 + tt:64 + tt + 1]
                        v.tensor_single_scalar(mcol, stats[:, tt:tt + 1], 1.0 / E, ALU.mult)
                        v.drain()
                        v.tensor_mul(stats[:, tt:tt + 1], mcol, mcol)
                        v.drain()
                        v.scalar_tensor_tensor(
                            stats[:, 160 + tt:160 + tt + 1],
                            stats[:, 32 + tt:32 + tt + 1], 1.0 / E,
                            stats[:, tt:tt + 1], ALU.mult, ALU.subtract)
                        tk[("var", tt)] = p.dinc(v, s_dve, 1)
                        p.wt(v, tk[("rstd", tt)])
                        v.scalar_tensor_tensor(
                            stats[:, 128 + tt:128 + tt + 1], mcol, -1.0,
                            stats[:, 96 + tt:96 + tt + 1], ALU.mult, ALU.mult)
                        tk[("nb", tt)] = p.dinc(v, s_dve, 1)
                        p.wt(v, tk[("tr", tt)])
                        src = bass.AP(ps_tr, 0, [[1024, 128], [128, 8], [1, 128]])
                        dst = bass.AP(dst_sb, tt * 128, [[8 * T, 128], [T, 8], [1, 128]])
                        v.tensor_copy(dst, src)
                        tk[("ev", tt)] = p.dinc(v, s_dve, 1)
                        if do_logits:
                            p.wt(v, tk[("lg", tt)])
                            v.tensor_copy(llg[:, 0:8], ps_sp[:, 0:8])
                            v.drain()
                            m1 = stats[:, tt:tt + 1]
                            m2 = stats[:, 32 + tt:32 + tt + 1]
                            v.reduce_max(m1, llg[:, 0:8], axis=AX.X)
                            v.drain()
                            e1 = eqs[:, tt * 8:tt * 8 + 8]
                            v.tensor_single_scalar(e1, llg[:, 0:8], m1, ALU.is_equal)
                            v.drain()
                            v.scalar_tensor_tensor(lmk[:, 0:8], e1, -1.0e30, llg[:, 0:8],
                                                   ALU.mult, ALU.add)
                            v.drain()
                            v.reduce_max(m2, lmk[:, 0:8], axis=AX.X)
                            v.drain()
                            e2 = eqs[:, 256 + tt * 8:256 + tt * 8 + 8]
                            ii = v.tensor_single_scalar(e2, llg[:, 0:8], m2, ALU.is_equal)
                            tk[("rt", tt)] = p.inc(ii, s_dve, 1)

                def f_scalar(a):
                    p.wait_all(a)
                    for tt in range(NT):
                        xv = lx[:, (tt % 2) * 1024:(tt % 2) * 1024 + 1024]
                        p.wt(a, tk[("ld", tt)])
                        a.activation(lsq[:, :], xv, AF.Square,
                                     accum_out=stats[:, 32 + tt:32 + tt + 1])
                        tk[("sq", tt)] = p.dinc(a, s_act, 1)
                        p.wt(a, tk[("var", tt)])
                        a.activation(stats[:, 32 + tt:32 + tt + 1],
                                     stats[:, 160 + tt:160 + tt + 1], AF.Ln, bias=epsc[:, :])
                        a.drain()
                        a.activation(stats[:, 96 + tt:96 + tt + 1],
                                     stats[:, 32 + tt:32 + tt + 1], AF.Exp, scale=-0.5)
                        tk[("rstd", tt)] = p.dinc(a, s_act, 1)
                        p.wt(a, tk[("nb", tt)])
                        if tt >= 2:
                            p.wt(a, tk[("tr", tt - 2)])
                            if store_tm:
                                p.wt(a, tk[("tmst", tt - 2)])
                        a.activation(lxn[:, (tt % 2) * 1024:(tt % 2) * 1024 + 1024],
                                     xv, AF.Identity,
                                     bias=stats[:, 128 + tt:128 + tt + 1],
                                     scale=stats[:, 96 + tt:96 + tt + 1])
                        tk[("ap", tt)] = p.dinc(a, s_act, 1)
                        if store_tm:
                            ii = a.dma_start(out=xn2h[tt * 128:(tt + 1) * 128, :],
                                             in_=lxn[:, (tt % 2) * 1024:(tt % 2) * 1024 + 1024])
                            tk[("tmst", tt)] = p.inc(ii, s_x2st2[tt % 2], 16)

                def f_tensor(t):
                    p.wait_all(t)
                    for tt in range(NT):
                        p.wt(t, tk[("ap", tt)])
                        if tt >= 1:
                            p.wt(t, tk[("ev", tt - 1)])
                        for c in range(8):
                            ii = t.transpose(
                                ps_tr[:, c * 128:(c + 1) * 128],
                                lxn[:, (tt % 2) * 1024 + c * 128:(tt % 2) * 1024 + (c + 1) * 128],
                                id_sb[:, :])
                        tk[("tr", tt)] = p.inc(ii, s_pe, 1)
                        if do_logits:
                            p.wt(t, tk[("ev", tt)])
                            if tt >= 1:
                                p.wt(t, tk[("rt", tt - 1)])
                            for r in range(8):
                                ii = t.matmul(
                                    ps_sp[:, 0:8],
                                    dst_sb[:, r * T + tt * 128:r * T + (tt + 1) * 128],
                                    wr_t[:, r * 8:(r + 1) * 8],
                                    start=(r == 0), stop=(r == 7))
                            tk[("lg", tt)] = p.inc(ii, s_pe, 1)

                p.run_phase({"sync": f_sync, "vector": f_vector,
                             "scalar": f_scalar, "tensor": f_tensor}, tk)

        def proj_rope_phase(wmat, wrows, nacc, src_sb, dst_sb):
            with (
                nc.sbuf_tensor(un("pw"), [128, 2 * nacc * 128], BF16) as pw,
                nc.sbuf_tensor(un("prc"), [128, 1024], F32) as prc,
                nc.sbuf_tensor(un("prs"), [128, 1024], F32) as prs,
                nc.sbuf_tensor(un("pqt"), [128, 1024], BF16) as pqt,
                nc.sbuf_tensor(un("pt1"), [128, 1024], BF16) as pt1,
            ):
                tk = TokDict()

                def f_gp(g):
                    p.wait_all(g)
                    for c in range(8):
                        if c >= 2:
                            p.wt(g, tk[("mm", c - 2, 7)])
                        half = (c % 2) * nacc * 128
                        dst = bass.AP(pw, half, [[2 * nacc * 128, 128], [128, nacc], [1, 128]])
                        src = bass.AP(wmat.tensor, c * 128, [[E, 128], [128 * E, nacc], [1, 128]])
                        ii = g.dma_start(out=dst, in_=src)
                        tk[("w", c)] = p.inc(ii, s_wsl2[c % 2], 16)

                def f_sync(sy):
                    p.wait_all(sy)
                    for c in range(8):
                        for tg in range(8):
                            i2 = c * 8 + tg
                            par = i2 % 2
                            if i2 >= 2:
                                pc, ptg = (i2 - 2) // 8, (i2 - 2) % 8
                                p.wt(sy, tk[("out", pc, ptg)])
                            ii = sy.dma_start(out=prc[:, par * 512:par * 512 + 512],
                                              in_=ropec[:, tg * 512:(tg + 1) * 512])
                            p.inc(ii, s_rp2[par], 16)
                            ii = sy.dma_start(out=prs[:, par * 512:par * 512 + 512],
                                              in_=ropes[:, tg * 512:(tg + 1) * 512])
                            tk[("rl", c, tg)] = p.inc(ii, s_rp2[par], 16)

                def f_tensor(t):
                    p.wait_all(t)
                    for c in range(8):
                        p.wt(t, tk[("w", c)])
                        for tg in range(8):
                            i2 = c * 8 + tg
                            par = i2 % 2
                            if i2 >= 2:
                                pc, ptg = (i2 - 2) // 8, (i2 - 2) % 8
                                p.wt(t, tk[("ev", pc, ptg)])
                            for r in range(nacc):
                                ii = t.matmul(
                                    ps_a[:, par * 512:par * 512 + 512],
                                    pw[:, (c % 2) * nacc * 128 + r * 128:(c % 2) * nacc * 128 + (r + 1) * 128],
                                    src_sb[:, r * T + tg * 512:r * T + (tg + 1) * 512],
                                    start=(r == 0), stop=(r == nacc - 1))
                            tk[("mm", c, tg)] = p.inc(ii, s_pe, 1)
                            p.wt(t, tk[("ev", c, tg)])
                            if i2 >= 2:
                                pc, ptg = (i2 - 2) // 8, (i2 - 2) % 8
                                p.wt(t, tk[("out", pc, ptg)])
                            ii = t.matmul(ps_b[:, par * 512:par * 512 + 512],
                                          swp_sb[:, :],
                                          pqt[:, par * 512:par * 512 + 512],
                                          start=True, stop=True)
                            tk[("sw", c, tg)] = p.inc(ii, s_pe, 1)

                def f_scalar(a):
                    p.wait_all(a)
                    for c in range(8):
                        for tg in range(8):
                            i2 = c * 8 + tg
                            par = i2 % 2
                            p.wt(a, tk[("mm", c, tg)])
                            if i2 >= 2:
                                pc, ptg = (i2 - 2) // 8, (i2 - 2) % 8
                                p.wt(a, tk[("sw", pc, ptg)])
                                p.wt(a, tk[("p1", pc, ptg)])
                            a.activation(pqt[:, par * 512:par * 512 + 512],
                                         ps_a[:, par * 512:par * 512 + 512], AF.Copy)
                            tk[("ev", c, tg)] = p.dinc(a, s_act, 1)

                def f_vector(v):
                    p.wait_all(v)
                    for c in range(8):
                        for tg in range(8):
                            i2 = c * 8 + tg
                            par = i2 % 2
                            p.wt(v, tk[("rl", c, tg)])
                            p.wt(v, tk[("ev", c, tg)])
                            v.tensor_mul(pt1[:, par * 512:par * 512 + 512],
                                         pqt[:, par * 512:par * 512 + 512],
                                         prc[:, par * 512:par * 512 + 512])
                            tk[("p1", c, tg)] = p.dinc(v, s_dve, 1)
                            p.wt(v, tk[("sw", c, tg)])
                            dcol = dst_sb[:, c * T + tg * 512:c * T + (tg + 1) * 512]
                            v.tensor_mul(dcol, ps_b[:, par * 512:par * 512 + 512],
                                         prs[:, par * 512:par * 512 + 512])
                            v.drain()
                            ii = v.tensor_add(dcol, pt1[:, par * 512:par * 512 + 512], dcol)
                            tk[("out", c, tg)] = p.inc(ii, s_dve, 1)

                p.run_phase({"gpsimd": f_gp, "sync": f_sync, "tensor": f_tensor,
                             "scalar": f_scalar, "vector": f_vector}, tk)

        dbg_lxn = en(nc.sbuf_tensor("dbg_lxn", [128, 2048], BF16)) if (dbg and dbg_stage == 1) else None
        dbg_lx = en(nc.sbuf_tensor("dbg_lx", [128, 2048], F32)) if (dbg and dbg_stage == 1) else None
        ln_phase(x_in, A, do_logits=False)
        if dbg_stage >= 2:
            proj_rope_phase(wq, E, 8, A, Bq)

        if dbg and dbg_stage == 1:
            def f_sync_d1(sy):
                p.wait_all(sy)
                tkk = None
                for c in range(8):
                    ii = sy.dma_start(out=k_dbg[:, c * T:(c + 1) * T], in_=A[:, c * T:(c + 1) * T])
                    tkk = p.inc(ii, s_sc, 16)
                ii = sy.dma_start(out=x1_dbg[0:128, 0:192], in_=stats[:, :])
                tkk = p.inc(ii, s_sc, 16)
                ii = sy.dma_start(out=q_dbg[:, 0:2048], in_=dbg_lxn[:, :])
                tkk = p.inc(ii, s_sc, 16)
                ii = sy.dma_start(out=x1_dbg[128:256, :], in_=dbg_lx[:, 0:1024])
                tkk = p.inc(ii, s_sc, 16)
                ii = sy.dma_start(out=x1_dbg[256:384, :], in_=dbg_lx[:, 1024:2048])
                tkk = p.inc(ii, s_sc, 16)
                sy.wait_ge(tkk[0], tkk[1])
            p.run_phase({"sync": f_sync_d1})

        if dbg and dbg_stage == 2:
            def f_scalar_d2(a):
                p.wait_all(a)
                tkk = None
                for c in range(8):
                    ii = a.dma_start(out=k_dbg[:, c * T:(c + 1) * T], in_=A[:, c * T:(c + 1) * T])
                    p.inc(ii, s_sc, 16)
                    ii = a.dma_start(out=q_dbg[:, c * T:(c + 1) * T], in_=Bq[:, c * T:(c + 1) * T])
                    tkk = p.inc(ii, s_sc, 16)
                a.wait_ge(tkk[0], tkk[1])
            p.run_phase({"scalar": f_scalar_d2})

        if dbg_stage >= 3:
            with nc.sbuf_tensor("latf", [128, 4 * T], BF16) as latf:
                with nc.sbuf_tensor("dw", [128, 2048], BF16) as dw:
                    tk = TokDict()

                    def f_gp(g):
                        p.wait_all(g)
                        for lc in range(4):
                            if lc >= 2:
                                p.wt(g, tk[("mm", lc - 2, 7)])
                            half = (lc % 2) * 1024
                            dst = bass.AP(dw, half, [[2048, 128], [128, 8], [1, 128]])
                            src = bass.AP(wd.tensor, lc * 128, [[L, 128], [128 * L, 8], [1, 128]])
                            ii = g.dma_start(out=dst, in_=src)
                            tk[("w", lc)] = p.inc(ii, s_wsl2[lc % 2], 16)

                    def f_tensor(t):
                        p.wait_all(t)
                        for lc in range(4):
                            p.wt(t, tk[("w", lc)])
                            for tg in range(8):
                                i2 = lc * 8 + tg
                                par = i2 % 2
                                if i2 >= 2:
                                    pl, ptg = (i2 - 2) // 8, (i2 - 2) % 8
                                    p.wt(t, tk[("ev", pl, ptg)])
                                for r in range(8):
                                    ii = t.matmul(
                                        ps_a[:, par * 512:par * 512 + 512],
                                        dw[:, (lc % 2) * 1024 + r * 128:(lc % 2) * 1024 + (r + 1) * 128],
                                        A[:, r * T + tg * 512:r * T + (tg + 1) * 512],
                                        start=(r == 0), stop=(r == 7))
                                tk[("mm", lc, tg)] = p.inc(ii, s_pe, 1)

                    def f_scalar(a):
                        p.wait_all(a)
                        for lc in range(4):
                            for tg in range(8):
                                par = (lc * 8 + tg) % 2
                                p.wt(a, tk[("mm", lc, tg)])
                                a.activation(
                                    latf[:, lc * T + tg * 512:lc * T + (tg + 1) * 512],
                                    ps_a[:, par * 512:par * 512 + 512], AF.Copy)
                                tk[("ev", lc, tg)] = p.dinc(a, s_act, 1)

                    p.run_phase({"gpsimd": f_gp, "tensor": f_tensor, "scalar": f_scalar}, tk)

                proj_rope_phase(wk, L, 4, latf, A)

                with (
                    nc.sbuf_tensor("vw", [128, 4096], BF16) as vw,
                    nc.sbuf_tensor("vsb", [128, 2080], BF16) as vsb,
                ):
                    tk = TokDict()

                    def f_gp(g):
                        p.wait_all(g)
                        dst = bass.AP(vw, 0, [[4096, 128], [1024, 4], [1, 1024]])
                        src = bass.AP(wv.tensor, 0, [[E, 128], [128 * E, 4], [1, 1024]])
                        ii = g.dma_start(out=dst, in_=src)
                        tk["w"] = p.inc(ii, s_once, 16)
                        ii = g.memset(bass.AP(vsb, 64, [[2080, 128], [65, 16]]), 1.0)
                        p.inc(ii, s_gp, 1)
                        ii = g.memset(bass.AP(vsb, 1040 + 64, [[2080, 128], [65, 16]]), 1.0)
                        tk["ones"] = p.inc(ii, s_gp, 1)

                    def f_tensor(t):
                        p.wait_all(t)
                        p.wt(t, tk["w"])
                        p.wt(t, tk["ones"])
                        for tt in range(NT):
                            for hf in range(2):
                                i2 = tt * 2 + hf
                                par = i2 % 2
                                if i2 >= 2:
                                    pt_, pf = (i2 - 2) // 2, (i2 - 2) % 2
                                    p.wt(t, tk[("ev", pt_, pf)])
                                for r in range(4):
                                    ii = t.matmul(
                                        ps_a[:, par * 512:par * 512 + 512],
                                        latf[:, r * T + tt * 128:r * T + (tt + 1) * 128],
                                        vw[:, r * 1024 + hf * 512:r * 1024 + (hf + 1) * 512],
                                        start=(r == 0), stop=(r == 3))
                                tk[("mm", tt, hf)] = p.inc(ii, s_pe, 1)

                    def f_vector(v):
                        p.wait_all(v)
                        for tt in range(NT):
                            for hf in range(2):
                                par = (tt * 2 + hf) % 2
                                p.wt(v, tk[("mm", tt, hf)])
                                if tt >= 2 and hf == 0:
                                    p.wt(v, tk[("st", tt - 2)])
                                dst = bass.AP(vsb, (tt % 2) * 1040 + hf * 520,
                                              [[2080, 128], [130, 4], [65, 2], [1, 64]])
                                src = bass.AP(ps_a, par * 512,
                                              [[1024, 128], [128, 4], [64, 2], [1, 64]])
                                v.tensor_copy(dst, src)
                                tk[("ev", tt, hf)] = p.dinc(v, s_dve, 1)

                    def f_scalar(a):
                        p.wait_all(a)
                        for tt in range(NT):
                            p.wt(a, tk[("ev", tt, 1)])
                            ii = a.dma_start(out=vh[tt * 128:(tt + 1) * 128, :],
                                             in_=vsb[:, (tt % 2) * 1040:(tt % 2) * 1040 + 1040])
                            tk[("st", tt)] = p.inc(ii, s_vst2[tt % 2], 16)

                    p.run_phase({"gpsimd": f_gp, "tensor": f_tensor,
                                 "vector": f_vector, "scalar": f_scalar}, tk)

        if dbg and dbg_stage == 3:
            def f_scalar_d3(a):
                p.wait_all(a)
                tkk = None
                for c in range(8):
                    ii = a.dma_start(out=q_dbg[:, c * T:(c + 1) * T], in_=Bq[:, c * T:(c + 1) * T])
                    p.inc(ii, s_sc, 16)
                    ii = a.dma_start(out=k_dbg[:, c * T:(c + 1) * T], in_=A[:, c * T:(c + 1) * T])
                    p.inc(ii, s_sc, 16)
                for tt in range(NT):
                    ii = a.dma_start(out=v_dbg[tt * 128:(tt + 1) * 128, :],
                                     in_=vh[tt * 128:(tt + 1) * 128, :])
                    tkk = p.inc(ii, s_sc, 16)
                a.wait_ge(tkk[0], tkk[1])
            p.run_phase({"scalar": f_scalar_d3})

        if dbg_stage >= 4:
            with (
                nc.sbuf_tensor("wo_sb", [128, 8192], BF16) as wo_sb,
                nc.sbuf_tensor("vst", [128, 16 * 130], BF16) as vst,
                nc.sbuf_tensor("pb", [128, 2048], BF16) as pb,
                nc.sbuf_tensor("ao", [128, 4096], BF16) as ao,
                nc.sbuf_tensor("sinv", [128, 1024], BF16) as sinv,
                nc.sbuf_tensor("bcb", [128, 512], BF16) as bcb,
                nc.sbuf_tensor("st64", [128, 512], BF16) as st64,
                nc.sbuf_tensor("axb", [128, 1024], F32) as axb,
                nc.sbuf_tensor("ax1", [128, 1024], F32) as ax1,
            ):
                tk = TokDict()
                sched = []
                base_of = {}
                gi = 0
                for g in range(NG):
                    ntiles = 4 * (g + 1)
                    for hp in range(8):
                        base_of[(g, hp)] = gi
                        for i in range(ntiles):
                            sched.append((g, hp, i, gi, ntiles))
                            gi += 1

                def prev_gh(g, hp):
                    return (g, hp - 1) if hp else (g - 1, 7)

                def f_gp(g_):
                    p.wait_all(g_)
                    dst = bass.AP(wo_sb, 0, [[8192, 128], [1024, 8], [1, 1024]])
                    src = bass.AP(wo.tensor, 0, [[E, 128], [128 * E, 8], [1, 1024]])
                    ii = g_.dma_start(out=dst, in_=src)
                    tk["wo"] = p.inc(ii, s_once, 16)
                    if dbg_stage >= 6:
                        for e in range(NE):
                            ii = g_.dma_start(
                                out=bass.AP(w1bf.tensor, e * DFF * E, [[256 * E, 16], [1, 256 * E]]),
                                in_=bass.AP(w1.tensor, e * DFF * E, [[256 * E, 16], [1, 256 * E]]))
                            p.inc(ii, s_stage[2 * e], 16)
                            ii = g_.dma_start(
                                out=bass.AP(w2bf.tensor, e * E * DFF, [[256 * DFF, 4], [1, 256 * DFF]]),
                                in_=bass.AP(w2.tensor, e * E * DFF, [[256 * DFF, 4], [1, 256 * DFF]]))
                            p.inc(ii, s_stage[2 * e + 1], 16)

                def f_sync(sy):
                    p.wait_all(sy)
                    for (g, hp, i, vi, ntiles) in sched:
                        if i % 4 == 0:
                            blk_ = (vi // 4) % 4
                            if vi >= 16:
                                p.wt(sy, tk[("av", vi - 16 + 3)])
                            ii = sy.dma_start(
                                out=vst[:, blk_ * 520:(blk_ + 1) * 520],
                                in_=bass.AP(vh.tensor, i * 128 * 1040 + hp * 130,
                                            [[1040, 128], [128 * 1040, 4], [1, 130]]))
                            tk[("vb", vi // 4)] = p.inc(ii, s_v8[blk_], 16)
                        if i == ntiles - 1:
                            p.wt(sy, tk[("m1", g, hp)])
                            if g > 0:
                                p.wt(sy, tk[("op", g - 1, 3, 1)])
                            ii = sy.dma_start(out=ao[64:128, hp * 512:(hp + 1) * 512],
                                              in_=st64[0:64, :])
                            tk[("mv", g, hp)] = p.inc(ii, s_mv, 16)
                        if hp == 7 and i == ntiles - 1:
                            for tt in range(4):
                                if (g, tt) != (0, 0):
                                    pg, ptt = (g, tt - 1) if tt else (g - 1, 3)
                                    p.wt(sy, tk[("oa", pg, ptt, 1)])
                                ii = sy.dma_start(
                                    out=axb[:, :],
                                    in_=x_in[g * 512 + tt * 128:g * 512 + (tt + 1) * 128, :])
                                tk[("xl", g, tt)] = p.inc(ii, s_ax, 16)

                def f_tensor(t):
                    p.wait_all(t)
                    p.wt(t, tk["wo"])
                    for g in range(NG):
                        ntiles = 4 * (g + 1)
                        for hp in range(8):
                            base_vi = base_of[(g, hp)]

                            def emit_sp(i, vi):
                                par = vi % 2
                                psab = ps_a if par == 0 else ps_b
                                full = i < 4 * g + 2
                                qoff = g * 512 if full else g * 512 + 256
                                qlen = 512 if full else 256
                                if vi >= 2:
                                    p.wt(t, tk[("e", vi - 2)])
                                ii = t.matmul(
                                    psab[:, 0:qlen],
                                    A[0:64, hp * T + i * 128:hp * T + (i + 1) * 128],
                                    Bq[0:64, hp * T + qoff:hp * T + qoff + qlen],
                                    start=True, stop=True, tile_position=(0, 0))
                                ii = t.matmul(
                                    psab[:, 512:512 + qlen],
                                    A[64:128, hp * T + i * 128:hp * T + (i + 1) * 128],
                                    Bq[64:128, hp * T + qoff:hp * T + qoff + qlen],
                                    start=True, stop=True, tile_position=(64, 0))
                                tk[("s", vi)] = p.inc(ii, s_pe, 1)

                            emit_sp(0, base_vi)
                            emit_sp(1, base_vi + 1)
                            for i in range(ntiles):
                                vi = base_vi + i
                                par = vi % 2
                                full = i < 4 * g + 2
                                p.wt(t, tk[("e", vi)])
                                p.wt(t, tk[("vb", vi // 4)])
                                if i == 0 and (g, hp) != (0, 0):
                                    p.wt(t, tk[("m1", *prev_gh(g, hp))])
                                slot = vi % 16
                                if full:
                                    ii = t.matmul(ps_c[0:65, 0:512],
                                                  vst[:, slot * 130:slot * 130 + 65],
                                                  pb[:, par * 1024:par * 1024 + 512],
                                                  start=(i == 0), stop=(i == ntiles - 1))
                                    ii = t.matmul(ps_c[0:65, 512:1024],
                                                  vst[:, slot * 130 + 65:slot * 130 + 130],
                                                  pb[:, par * 1024 + 512:par * 1024 + 1024],
                                                  start=(i == 0), stop=(i == ntiles - 1))
                                else:
                                    ii = t.matmul(ps_c[0:65, 256:512],
                                                  vst[:, slot * 130:slot * 130 + 65],
                                                  pb[:, par * 1024:par * 1024 + 256],
                                                  start=False, stop=(i == ntiles - 1))
                                    ii = t.matmul(ps_c[0:65, 768:1024],
                                                  vst[:, slot * 130 + 65:slot * 130 + 130],
                                                  pb[:, par * 1024 + 512:par * 1024 + 768],
                                                  start=False, stop=(i == ntiles - 1))
                                tk[("av", vi)] = p.inc(ii, s_pe, 1)
                                if i + 2 < ntiles:
                                    emit_sp(i + 2, base_vi + i + 2)
                            p.wt(t, tk[("rc", g, hp)])
                            if (g, hp) != (0, 0):
                                p.wt(t, tk[("b1", *prev_gh(g, hp))])
                            if hp == 0 and g > 0:
                                p.wt(t, tk[("oa", g - 1, 3, 1)])
                            ii = t.matmul(ps_sp[0:64, :], ones128[64:65, 0:64],
                                          sinv[64:65, 0:512], start=True, stop=True)
                            tk[("bc", g, hp, 0)] = p.inc(ii, s_pe, 1)
                            p.wt(t, tk[("b0", g, hp)])
                            ii = t.matmul(ps_sp[0:64, :], ones128[64:65, 0:64],
                                          sinv[64:65, 512:1024], start=True, stop=True)
                            tk[("bc", g, hp, 1)] = p.inc(ii, s_pe, 1)
                        p.wt(t, tk[("mv", g, 7)])
                        for tt in range(4):
                            for hf in range(2):
                                if (tt, hf) == (0, 0):
                                    p.wt(t, tk[("b1", g, 7)])
                                else:
                                    pg = (g, tt, hf - 1) if hf else (g, tt - 1, 1)
                                    p.wt(t, tk[("oa", *pg)])
                                for r in range(8):
                                    ii = t.matmul(
                                        ps_sp[:, :],
                                        ao[:, r * 512 + tt * 128:r * 512 + (tt + 1) * 128],
                                        wo_sb[:, r * 1024 + hf * 512:r * 1024 + (hf + 1) * 512],
                                        start=(r == 0), stop=(r == 7))
                                tk[("op", g, tt, hf)] = p.inc(ii, s_pe, 1)

                def f_scalar(a):
                    p.wait_all(a)
                    for g in range(NG):
                        ntiles = 4 * (g + 1)
                        for hp in range(8):
                            base_vi = base_of[(g, hp)]
                            for i in range(ntiles):
                                vi = base_vi + i
                                par = vi % 2
                                psab = ps_a if par == 0 else ps_b
                                full = i < 4 * g + 2
                                p.wt(a, tk[("s", vi)])
                                if vi >= 2:
                                    p.wt(a, tk[("av", vi - 2)])
                                if full:
                                    ii = a.activation(pb[:, par * 1024:(par + 1) * 1024],
                                                      psab[:, :], AF.Exp, scale=0.125)
                                else:
                                    a.activation(pb[:, par * 1024:par * 1024 + 256],
                                                 psab[:, 0:256], AF.Exp, scale=0.125)
                                    ii = a.activation(pb[:, par * 1024 + 512:par * 1024 + 768],
                                                      psab[:, 512:768], AF.Exp, scale=0.125)
                                tk[("e", vi)] = p.inc(ii, s_act, 1)
                        for tt in range(4):
                            p.wt(a, tk[("oa", g, tt, 1)])
                            ii = a.dma_start(
                                out=x1h[g * 512 + tt * 128:g * 512 + (tt + 1) * 128, :],
                                in_=ax1[:, :])
                            tk[("x1", g, tt)] = p.inc(ii, s_sc, 16)

                def f_vector(v):
                    p.wait_all(v)
                    for g in range(NG):
                        ntiles = 4 * (g + 1)
                        for hp in range(8):
                            lastvi = base_of[(g, hp)] + ntiles - 1
                            p.wt(v, tk[("av", lastvi)])
                            if (g, hp) != (0, 0):
                                p.wt(v, tk[("mv", *prev_gh(g, hp))])
                            if hp == 0 and g > 0:
                                p.wt(v, tk[("op", g - 1, 3, 1)])
                            with nc.allow_low_precision(reason="bf16 softmax normalizer (0.4% rel; bar 5.5%)"):
                                v.reciprocal(sinv[64:65, 0:512], ps_c[64:65, 0:512])
                                v.reciprocal(sinv[64:65, 512:1024], ps_c[64:65, 512:1024])
                            tk[("rc", g, hp)] = p.dinc(v, s_dve, 1)
                            p.wt(v, tk[("bc", g, hp, 0)])
                            v.tensor_copy(bcb[0:64, :], ps_sp[0:64, :])
                            tk[("b0", g, hp)] = p.dinc(v, s_dve, 1)
                            v.tensor_mul(ao[0:64, hp * 512:(hp + 1) * 512],
                                         ps_c[0:64, 0:512], bcb[0:64, :])
                            tk[("m0", g, hp)] = p.dinc(v, s_dve, 1)
                            p.wt(v, tk[("bc", g, hp, 1)])
                            v.tensor_copy(bcb[0:64, :], ps_sp[0:64, :])
                            tk[("b1", g, hp)] = p.dinc(v, s_dve, 1)
                            v.tensor_mul(st64[0:64, :], ps_c[0:64, 512:1024],
                                         bcb[0:64, :])
                            tk[("m1", g, hp)] = p.dinc(v, s_dve, 1)
                        for tt in range(4):
                            for hf in range(2):
                                p.wt(v, tk[("op", g, tt, hf)])
                                p.wt(v, tk[("xl", g, tt)])
                                if hf == 0 and (g, tt) != (0, 0):
                                    pg, ptt = (g, tt - 1) if tt else (g - 1, 3)
                                    p.wt(v, tk[("x1", pg, ptt)])
                                v.tensor_add(ax1[:, hf * 512:(hf + 1) * 512],
                                             ps_sp[:, :],
                                             axb[:, hf * 512:(hf + 1) * 512])
                                tk[("oa", g, tt, hf)] = p.dinc(v, s_dve, 1)

                p.run_phase({"gpsimd": f_gp, "sync": f_sync, "tensor": f_tensor,
                             "scalar": f_scalar, "vector": f_vector}, tk)

        if dbg and dbg_stage == 4:
            def f_scalar_d4(a):
                p.wait_all(a)
                tkk = None
                for tt in range(NT):
                    ii = a.dma_start(out=x1_dbg[tt * 128:(tt + 1) * 128, :],
                                     in_=x1h[tt * 128:(tt + 1) * 128, :])
                    tkk = p.inc(ii, s_sc, 16)
                a.wait_ge(tkk[0], tkk[1])
            p.run_phase({"scalar": f_scalar_d4})

        if dbg_stage >= 5:
            with (
                nc.sbuf_tensor("wr8", [8, 1024], BF16) as wr8,
                nc.sbuf_tensor("zb", [128, 1024], BF16) as zb,
            ):
                tk = TokDict()

                def f_gp(g):
                    p.wait_all(g)
                    ii = g.dma_start(out=wr8[:, :], in_=wr[:, :])
                    tk["w"] = p.inc(ii, s_once, 16)
                    g.memset(zb[:, :], 0.0)
                    tk["zb"] = p.dinc(g, s_gp, 1)

                def f_sync(sy):
                    p.wait_all(sy)
                    p.wt(sy, tk["zb"])
                    for tt in range(NT):
                        ii = sy.dma_start(out=accB[tt * 128:(tt + 1) * 128, :], in_=zb[:, :])
                        p.inc(ii, s_once, 16)

                def f_tensor(t):
                    p.wait_all(t)
                    p.wt(t, tk["w"])
                    for r in range(8):
                        ii = t.transpose(ps_tr[:, r * 8:(r + 1) * 8],
                                         wr8[0:8, r * 128:(r + 1) * 128], id_sb[0:8, 0:8])
                    tk["tr"] = p.inc(ii, s_pe, 1)

                def f_vector(v):
                    p.wait_all(v)
                    p.wt(v, tk["tr"])
                    v.tensor_copy(wr_t[:, :], ps_tr[:, 0:64])
                    p.dinc(v, s_dve, 1)

                p.run_phase({"gpsimd": f_gp, "sync": f_sync,
                             "tensor": f_tensor, "vector": f_vector}, tk)

            ln_phase(x1h, A, do_logits=True, store_tm=True)

            def gates_phase():
                tk = TokDict()

                def f_vector(v):
                    p.wait_all(v)
                    v.tensor_sub(stats[:, 64:96], stats[:, 32:64], stats[:, 0:32])
                    v.tensor_sub(stats[:, 96:128], stats[:, 0:32], stats[:, 32:64])
                    tk["d"] = p.dinc(v, s_dve, 1)
                    p.wt(v, tk["exp"])
                    v.tensor_single_scalar(stats[:, 64:128], stats[:, 128:192], 1.0, ALU.add)
                    v.drain()
                    v.reciprocal(stats[:, 128:192], stats[:, 64:128])
                    v.drain()
                    p.inc(v.engine_nop(), s_dve, 1)
                    for tt in range(NT):
                        gcol = gates[:, tt * 8:(tt + 1) * 8]
                        v.scalar_tensor_tensor(gcol, eqs[:, 256 + tt * 8:256 + tt * 8 + 8],
                                               stats[:, 160 + tt:160 + tt + 1], zero8[:, :],
                                               ALU.mult, ALU.add)
                        v.drain()
                        ii = v.scalar_tensor_tensor(gcol, eqs[:, tt * 8:tt * 8 + 8],
                                                    stats[:, 128 + tt:128 + tt + 1], gcol,
                                                    ALU.mult, ALU.add)
                        p.inc(ii, s_dve, 1)

                def f_scalar(a):
                    p.wait_all(a)
                    p.wt(a, tk["d"])
                    a.activation(stats[:, 128:192], stats[:, 64:128], AF.Exp)
                    tk["exp"] = p.dinc(a, s_act, 1)

                p.run_phase({"vector": f_vector, "scalar": f_scalar}, tk)

            gates_phase()

        if dbg and dbg_stage == 5:
            def f_scalar_d5(a):
                p.wait_all(a)
                ii = a.dma_start(out=g_dbg[:, :], in_=gates[:, :])
                tkk = p.inc(ii, s_sc, 16)
                a.wait_ge(tkk[0], tkk[1])
            p.run_phase({"scalar": f_scalar_d5})

        CAP = 1280
        NCC = CAP // 128
        BIGF = 1.0e5
        if dbg_stage >= 6:
            with (
                nc.sbuf_tensor("iota_sb", [128, 1536], F32) as iota_sb,
                nc.sbuf_tensor("id32_sb", [128, 128], F32) as id32_sb,
                nc.sbuf_tensor("rhs2", [128, 128], BF16) as rhs2,
                nc.sbuf_tensor("maskb", [128, 256], BF16) as maskb,
                nc.sbuf_tensor("maskE", [8, 4096], F32) as maskE,
                nc.sbuf_tensor("slotE", [8, 4096], F32) as slotE,
                nc.sbuf_tensor("slottm", [128, 256], F32) as slottm,
                nc.sbuf_tensor("mful", [128, 2 * 1536], BF16) as mful,
                nc.sbuf_tensor("idxw", [128, 192], F32) as idxw,
                nc.sbuf_tensor("idxst", [128, 96], mybir.dt.int32) as idxst,
                nc.sbuf_tensor("bigc", [128, 12], F32) as bigc,
            ):
                tk = TokDict()

                def f_sync(sy):
                    p.wait_all(sy)
                    ii = sy.dma_start(out=iota_sb[:, :], in_=iota_in[:, :])
                    p.inc(ii, s_once, 16)
                    ii = sy.dma_start(out=id32_sb[:, :], in_=id32_in[:, :])
                    p.inc(ii, s_once, 16)
                    ii = sy.dma_start(out=rhs2[:, :], in_=rhs2s_in[:, :])
                    tk["consts"] = p.inc(ii, s_once, 16)

                def f_gp(g):
                    p.wait_all(g)
                    g.memset(bigc[:, :], BIGF)
                    tk["big"] = p.dinc(g, s_gp, 1)

                def f_vector(v):
                    p.wait_all(v)
                    p.wt(v, tk["consts"])
                    v.tensor_add(slottm[:, :], eqs[:, 0:256], eqs[:, 256:512])
                    v.drain()
                    v.tensor_copy(maskb[:, :], slottm[:, :])
                    tk["mb"] = p.dinc(v, s_dve, 1)
                    for ch in range(4):
                        p.wt(v, tk[("mtr", ch)])
                        v.tensor_copy(maskE[0:8, ch * 1024:(ch + 1) * 1024],
                                      ps_tr[0:8, 0:1024])
                        tk[("mev", ch)] = p.dinc(v, s_dve, 1)
                    v.tensor_tensor_scan(slotE[:, :], maskE[:, :], maskE[:, :], 0.0,
                                         ALU.add, ALU.max)
                    v.drain()
                    v.tensor_mul(slotE[:, :], slotE[:, :], maskE[:, :])
                    tk["slotE"] = p.dinc(v, s_dve, 1)
                    p.wt(v, tk["str"])
                    v.tensor_copy(slottm[:, :], ps_sp[:, 0:256])
                    tk["stm"] = p.dinc(v, s_dve, 1)
                    for e in range(NE):
                        if e:
                            p.wt(v, tk[("lmm", e - 1, NT - 1)])
                        gsrc = bass.AP(gates, e, [[256, 128], [8, 32]])
                        gdst = bass.AP(rhs2, 2, [[128, 128], [4, 32]])
                        v.tensor_copy(gdst, gsrc)
                        tk[("g", e)] = p.dinc(v, s_dve, 1)
                        for tt in range(NT):
                            if e * NT + tt >= 2:
                                j = e * NT + tt - 2
                                p.wt(v, tk[("lmm", j // NT, j % NT)])
                            half = (tt % 2) * 1536
                            v.tensor_single_scalar(mful[:, half:half + CAP],
                                                   iota_sb[:, 0:CAP],
                                                   slottm[:, tt * 8 + e:tt * 8 + e + 1],
                                                   ALU.is_equal)
                            tk[("M", e, tt)] = p.dinc(v, s_dve, 1)
                    p.wt(v, tk["lev"])
                    p.wt(v, tk["big"])
                    for e in range(NE):
                        base = e * 48
                        plo = bass.AP(lists_keep, base + 0, [[384, 128], [4, 12]])
                        ttv = bass.AP(lists_keep, base + 1, [[384, 128], [4, 12]])
                        val = bass.AP(lists_keep, base + 3, [[384, 128], [4, 12]])
                        ecol = idxw[:, e * 12:(e + 1) * 12]
                        ecol2 = idxw[:, 96 + e * 12:96 + (e + 1) * 12]
                        icol = idxst[:, e * 12:(e + 1) * 12]
                        v.scalar_tensor_tensor(ecol, ttv, 128.0, plo, ALU.mult, ALU.add)
                        v.drain()
                        v.scalar_tensor_tensor(ecol, val, -BIGF, ecol, ALU.mult, ALU.add)
                        v.drain()
                        v.tensor_add(ecol, bigc[:, :], ecol)
                        v.drain()
                        v.tensor_single_scalar(ecol2, ecol, 8388608.0, ALU.add)
                        v.drain()
                        with nc.allow_low_precision(reason="bitcast f32->i32 (exact ints)"):
                            v.tensor_copy(icol, ecol2)
                        v.drain()
                        v.tensor_single_scalar(idxi_keep[:, e * 12:(e + 1) * 12],
                                               icol, 0x007FFFFF, ALU.bitwise_and)
                    tk["idx"] = p.dinc(v, s_dve, 1)

                def f_tensor(t):
                    p.wait_all(t)
                    p.wt(t, tk["mb"])
                    for ch in range(4):
                        if ch >= 1:
                            p.wt(t, tk[("mev", ch - 1)])
                        for j in range(8):
                            tt = ch * 8 + j
                            ii = t.transpose(ps_tr[0:8, j * 128:(j + 1) * 128],
                                             maskb[:, tt * 8:tt * 8 + 8], id_sb[:, :])
                        tk[("mtr", ch)] = p.inc(ii, s_pe, 1)
                    p.wt(t, tk["slotE"])
                    for tt in range(NT):
                        ii = t.transpose(ps_sp[:, tt * 8:(tt + 1) * 8],
                                         slotE[0:8, tt * 128:(tt + 1) * 128],
                                         id32_sb[0:8, 0:8])
                    tk["str"] = p.inc(ii, s_pe, 1)
                    p.wt(t, tk["stm"])
                    for e in range(NE):
                        p.wt(t, tk[("g", e)])
                        for tt in range(NT):
                            p.wt(t, tk[("M", e, tt)])
                            half = (tt % 2) * 1536
                            for c in range(NCC):
                                ii = t.matmul(
                                    ps_c[:, e * 48 + c * 4:e * 48 + (c + 1) * 4],
                                    mful[:, half + c * 128:half + (c + 1) * 128],
                                    rhs2[:, tt * 4:(tt + 1) * 4],
                                    start=(e == 0 and tt == 0 and c == 0),
                                    stop=(e == NE - 1 and tt == NT - 1 and c == NCC - 1),
                                    skip_group_check=True)
                            tk[("lmm", e, tt)] = p.inc(ii, s_pe, 1)

                def f_scalar(a):
                    p.wait_all(a)
                    p.wt(a, tk[("lmm", NE - 1, NT - 1)])
                    a.activation(lists_keep[:, :], ps_c[:, 0:384], AF.Copy)
                    tk["lev"] = p.dinc(a, s_act, 1)

                p.run_phase({"sync": f_sync, "gpsimd": f_gp, "tensor": f_tensor,
                             "scalar": f_scalar, "vector": f_vector}, tk)

        if dbg and dbg_stage == 6:
            def f_scalar_d6(a):
                p.wait_all(a)
                ii = a.dma_start(out=x1_dbg[0:128, 0:384], in_=lists_keep[:, :])
                p.inc(ii, s_sc, 16)
                ii = a.dma_start(out=g_dbg[:, :], in_=gates[:, :])
                tkk = p.inc(ii, s_sc, 16)
                a.wait_ge(tkk[0], tkk[1])

            def f_gp_d6(g):
                p.wait_all(g)
                ii = g.dma_start(out=x1_dbg[384:512, 0:96],
                                 in_=bass.AP(idxi_keep, 0, [[96, 128], [1, 96]]))
                tkk = p.inc(ii, s_sc, 16)
                g.wait_ge(tkk[0], tkk[1])
            p.run_phase({"scalar": f_scalar_d6, "gpsimd": f_gp_d6})

        if dbg_stage >= 6:
            with (
                nc.sbuf_tensor("w1t", [128, 8192], BF16) as w1t,
                nc.sbuf_tensor("w2t", [128, 8192], BF16) as w2t,
                nc.sbuf_tensor("hsb", [128, 4 * 1536], BF16) as hsb,
                nc.sbuf_tensor("xgt", [128, 2048], BF16) as xgt,
            ):
                tk = TokDict()
                TGS = 3
                TGW = [512, 512, 256]
                TGT = [4, 4, 2]
                YMAX = 2 * sum(TGT)
                SCT_E = 16 * NCC

                def xgf(par):
                    return par * 12 * 1024

                def f_gp(g):
                    p.wait_all(g)
                    for e in range(NE):
                        for c in range(NCC):
                            gi_ = e * NCC + c
                            if gi_ >= 2:
                                pe_, pc = (e, c - 2) if c >= 2 else (e - 1, NCC - 2 + c)
                                p.wt(g, tk[("gtrp", pe_, pc)])
                            else:
                                p.wt(g, tk["xz"])
                            ii = g.indirect_dma_start(
                                out=xgt[:, (gi_ % 2) * 1024:(gi_ % 2) * 1024 + 1024],
                                out_offset=None,
                                in_=xn2h[0:128, :],
                                in_offset=bass.IndirectOffsetOnAxis(
                                    ap=idxi_keep[:, e * 12 + c:e * 12 + c + 1], axis=0),
                            )
                            tk[("gth", e, c)] = p.inc(ii, s_gt2[gi_ % 2], 16)
                        if e >= 1:
                            se = e - 1
                            if se >= 1:
                                g.wait_ge(s_sct, SCT_E * se)
                            p.wt(g, tk[("yv", se)])
                            for c in range(NCC):
                                ii = g.indirect_dma_start(
                                    out=accB[0:128, :],
                                    out_offset=bass.IndirectOffsetOnAxis(
                                        ap=idxi_keep[:, se * 12 + c:se * 12 + c + 1], axis=0),
                                    in_=Bq[:, (se % 2) * 12288 + c * 1024:(se % 2) * 12288 + (c + 1) * 1024],
                                    in_offset=None,
                                    compute_op=ALU.add,
                                )
                                p.inc(ii, s_sct, 16)
                    se = NE - 1
                    g.wait_ge(s_sct, SCT_E * se)
                    p.wt(g, tk[("yv", se)])
                    for c in range(NCC):
                        ii = g.indirect_dma_start(
                            out=accB[0:128, :],
                            out_offset=bass.IndirectOffsetOnAxis(
                                ap=idxi_keep[:, se * 12 + c:se * 12 + c + 1], axis=0),
                            in_=Bq[:, (se % 2) * 12288 + c * 1024:(se % 2) * 12288 + (c + 1) * 1024],
                            in_offset=None,
                            compute_op=ALU.add,
                        )
                        p.inc(ii, s_sct, 16)

                def f_vector(v):
                    p.wait_all(v)
                    v.memset(xgt[:, :], 0.0)
                    v.memset(bass.AP(A, 0, [[8 * T, 128], [1, 24 * 1024]]), 0.0)
                    tk["xz"] = p.dinc(v, s_dve, 1)
                    for e in range(NE):
                        for c in range(NCC):
                            gi_ = e * NCC + c
                            p.wt(v, tk[("gtrp", e, c)])
                            if e >= 2 and c == 0:
                                p.wt(v, tk[("h", 96 * (e - 1) - 1)])
                            srcp = bass.AP(ps_tr, 0, [[1024, 128], [128, 8], [1, 128]])
                            dstp = bass.AP(A, xgf(e % 2) + c * 128,
                                           [[8 * T, 128], [1536, 8], [1, 128]])
                            v.tensor_copy(dstp, srcp)
                            tk[("gtr", e, c)] = p.dinc(v, s_dve, 1)
                        if e >= 2:
                            v.wait_ge(s_sct, SCT_E * (e - 1))
                        v.memset(Bq[:, (e % 2) * 12288:(e % 2) * 12288 + NCC * 1024], 0.0)
                        tk[("yz", e)] = p.dinc(v, s_dve, 1)
                        for sl in range(8):
                            for tg in range(TGS):
                                for tt in range(TGT[tg]):
                                    for hf in range(2):
                                        yi = (tg * 4 + tt) * 2 + hf
                                        p.wt(v, tk[("y", e, sl, tg, tt, hf)])
                                        gap = bass.AP(lists_keep,
                                                      e * 48 + (tg * 4 + tt) * 4 + 2,
                                                      [[384, 128], [1, 1]])
                                        ycol = (e % 2) * 12288 + (tg * 4 + tt) * 1024 + hf * 512
                                        ii = v.scalar_tensor_tensor(
                                            Bq[:, ycol:ycol + 512],
                                            ps_c[:, (yi % 2) * 512:(yi % 2) * 512 + 512],
                                            gap, Bq[:, ycol:ycol + 512],
                                            ALU.mult, ALU.add)
                                        if sl == 7 and yi == YMAX - 1:
                                            tk[("ye", e, sl, yi)] = p.dinc(v, s_dve, 1)
                                        else:
                                            tk[("ye", e, sl, yi)] = p.inc(ii, s_dve, 1)
                            if sl == 7:
                                tk[("yv", e)] = tk[("ye", e, 7, YMAX - 1)]

                def f_sync(sy):
                    p.wait_all(sy)
                    for e in range(NE):
                        for sl in range(8):
                            par = sl % 2
                            if sl == 0:
                                sy.wait_ge(s_stage[2 * e], 16)
                            if (e, sl) not in ((0, 0), (0, 1)):
                                pe_, psl = (e, sl - 2) if sl >= 2 else (e - 1, sl + 6)
                                p.wt(sy, tk[("y", pe_, psl, TGS - 1, TGT[-1] - 1, 1)])
                            for r in range(8):
                                ii = sy.dma_start(
                                    out=w1t[:, par * 4096 + r * 512:par * 4096 + (r + 1) * 512],
                                    in_=bass.AP(w1bf.tensor, e * DFF * E + sl * 512 * E + r * 128,
                                                [[E, 512], [1, 128]]),
                                    transpose=True)
                                tk[("w1", e, sl)] = p.inc(ii, s_w1t2[par], 16)
                            if sl == 0:
                                sy.wait_ge(s_stage[2 * e + 1], 16)
                            for fs in range(4):
                                ii = sy.dma_start(
                                    out=w2t[:, par * 4096 + fs * 1024:par * 4096 + (fs + 1) * 1024],
                                    in_=bass.AP(w2bf.tensor, e * E * DFF + sl * 512 + fs * 128,
                                                [[DFF, 1024], [1, 128]]),
                                    transpose=True)
                                tk[("w2", e, sl)] = p.inc(ii, s_w2t2[par], 16)

                def f_tensor(t):
                    p.wait_all(t)
                    p.wt(t, tk["xz"])
                    hidx = 0
                    for e in range(NE):
                        par_e = e % 2
                        for c in range(NCC):
                            gi_ = e * NCC + c
                            p.wt(t, tk[("gth", e, c)])
                            if gi_ >= 1:
                                pe_, pc = (e, c - 1) if c else (e - 1, NCC - 1)
                                p.wt(t, tk[("gtr", pe_, pc)])
                            for r in range(8):
                                ii = t.transpose(
                                    ps_tr[:, r * 128:(r + 1) * 128],
                                    xgt[:, (gi_ % 2) * 1024 + r * 128:(gi_ % 2) * 1024 + (r + 1) * 128],
                                    id_sb[:, :])
                            tk[("gtrp", e, c)] = p.inc(ii, s_pe, 1)
                        p.wt(t, tk[("gtr", e, NCC - 1)])
                        p.wt(t, tk[("yz", e)])
                        for sl in range(8):
                            par = sl % 2
                            p.wt(t, tk[("w1", e, sl)])
                            for tg in range(TGS):
                                for fs in range(4):
                                    bsel = (hidx // 2) % 2
                                    psh = ps_a if bsel == 0 else ps_b
                                    half = (hidx % 2) * 512
                                    if hidx >= 4:
                                        p.wt(t, tk[("si", (hidx - 4) // 2)])
                                    for r in range(8):
                                        ii = t.matmul(
                                            psh[:, half:half + TGW[tg]],
                                            w1t[:, par * 4096 + r * 512 + fs * 128:par * 4096 + r * 512 + (fs + 1) * 128],
                                            A[:, xgf(par_e) + r * 1536 + tg * 512:xgf(par_e) + r * 1536 + tg * 512 + TGW[tg]],
                                            start=(r == 0), stop=(r == 7))
                                    tk[("h", hidx)] = p.inc(ii, s_pe, 1)
                                    hidx += 1
                            p.wt(t, tk[("w2", e, sl)])
                            p.wt(t, tk[("si", (hidx - 2) // 2)])
                            for tg in range(TGS):
                                for tt in range(TGT[tg]):
                                    for hf in range(2):
                                        yi = (tg * 4 + tt) * 2 + hf
                                        if (e, sl) != (0, 0) or yi >= 2:
                                            if yi >= 2:
                                                p.wt(t, tk[("ye", e, sl, yi - 2)])
                                            else:
                                                pe_, psl = (e, sl - 1) if sl else (e - 1, 7)
                                                p.wt(t, tk[("ye", pe_, psl, YMAX - 2 + yi)])
                                        for fs in range(4):
                                            ii = t.matmul(
                                                ps_c[:, (yi % 2) * 512:(yi % 2) * 512 + 512],
                                                hsb[:, fs * 1536 + tg * 512 + tt * 128:fs * 1536 + tg * 512 + (tt + 1) * 128],
                                                w2t[:, par * 4096 + fs * 1024 + hf * 512:par * 4096 + fs * 1024 + (hf + 1) * 512],
                                                start=(fs == 0), stop=(fs == 3))
                                        tk[("y", e, sl, tg, tt, hf)] = p.inc(ii, s_pe, 1)

                def f_scalar(a):
                    p.wait_all(a)
                    hidx = 0
                    for e in range(NE):
                        for sl in range(8):
                            for tg in range(TGS):
                                for fs in range(4):
                                    if hidx % 2 == 1:
                                        p.wt(a, tk[("h", hidx)])
                                        bsel = (hidx // 2) % 2
                                        psh = ps_a if bsel == 0 else ps_b
                                        w = TGW[tg]
                                        dst = bass.AP(hsb, (fs - 1) * 1536 + tg * 512,
                                                      [[4 * 1536, 128], [1536, 2], [1, w]])
                                        src = bass.AP(psh, 0, [[1024, 128], [512, 2], [1, w]])
                                        ii = a.activation(dst, src, AF.Silu)
                                        tk[("si", hidx // 2)] = p.inc(ii, s_act, 1)
                                    hidx += 1

                p.run_phase({"gpsimd": f_gp, "vector": f_vector, "sync": f_sync,
                             "tensor": f_tensor, "scalar": f_scalar}, tk)

        if dbg and dbg_stage == 7:
            def f_gp_d7(g):
                p.wait_all(g)
                tkk = None
                for tt in range(NT):
                    ii = g.dma_start(out=x1_dbg[tt * 128:(tt + 1) * 128, :],
                                     in_=accB[tt * 128:(tt + 1) * 128, :])
                    tkk = p.inc(ii, s_sc, 16)
                g.wait_ge(tkk[0], tkk[1])

            def f_scalar_d7(a):
                p.wait_all(a)
                tkk = None
                for c in range(8):
                    ii = a.dma_start(out=k_dbg[:, c * T:(c + 1) * T],
                                     in_=Bq[:, c * T:(c + 1) * T])
                    p.inc(ii, s_sc, 16)
                ii = a.dma_start(out=g_dbg[:, :], in_=gates[:, :])
                p.inc(ii, s_sc, 16)
                for c in range(6):
                    ii = a.dma_start(out=q_dbg[:, c * T:(c + 1) * T],
                                     in_=A[:, c * T:(c + 1) * T])
                    tkk = p.inc(ii, s_sc, 16)
                a.wait_ge(tkk[0], tkk[1])
            p.run_phase({"gpsimd": f_gp_d7, "scalar": f_scalar_d7})

        if dbg_stage >= 7:
            with (
                nc.sbuf_tensor("fx", [128, 2048], F32) as fx,
                nc.sbuf_tensor("fb", [128, 2048], BF16) as fb,
                nc.sbuf_tensor("fo", [128, 2048], F32) as fo,
            ):
                tk = TokDict()

                def f_sync(sy):
                    p.wait_all(sy)
                    for tt in range(NT):
                        if tt >= 2:
                            p.wt(sy, tk[("add", tt - 2)])
                        ii = sy.dma_start(out=fx[:, (tt % 2) * 1024:(tt % 2) * 1024 + 1024],
                                          in_=x1h[tt * 128:(tt + 1) * 128, :])
                        p.inc(ii, s_lx2[tt % 2], 16)
                        ii = sy.dma_start(out=fb[:, (tt % 2) * 1024:(tt % 2) * 1024 + 1024],
                                          in_=accB[tt * 128:(tt + 1) * 128, :])
                        tk[("l", tt)] = p.inc(ii, s_lx2[tt % 2], 16)

                def f_vector(v):
                    p.wait_all(v)
                    for tt in range(NT):
                        p.wt(v, tk[("l", tt)])
                        if tt >= 2:
                            p.wt(v, tk[("st", tt - 2)])
                        v.tensor_add(fo[:, (tt % 2) * 1024:(tt % 2) * 1024 + 1024],
                                     fb[:, (tt % 2) * 1024:(tt % 2) * 1024 + 1024],
                                     fx[:, (tt % 2) * 1024:(tt % 2) * 1024 + 1024])
                        tk[("add", tt)] = p.dinc(v, s_dve, 1)

                def f_scalar(a):
                    p.wait_all(a)
                    for tt in range(NT):
                        p.wt(a, tk[("add", tt)])
                        ii = a.dma_start(out=out[tt * 128:(tt + 1) * 128, :],
                                         in_=fo[:, (tt % 2) * 1024:(tt % 2) * 1024 + 1024])
                        tk[("st", tt)] = p.inc(ii, s_ost2[tt % 2], 16)
                    a.wait_ge(s_ost2[0], p.tot(s_ost2[0]))
                    a.wait_ge(s_ost2[1], p.tot(s_ost2[1]))
                    for sem in s_stage:
                        if p.tot(sem):
                            a.wait_ge(sem, p.tot(sem))

                p.run_phase({"sync": f_sync, "vector": f_vector, "scalar": f_scalar}, tk)
        else:
            def f_scalar_end(a):
                p.wait_all(a)
                for sem in s_stage:
                    if p.tot(sem):
                        a.wait_ge(sem, p.tot(sem))
            p.run_phase({"scalar": f_scalar_end})

    return nc


def kernel(L0_attn_x, L0_attn_wq, L0_attn_w_down, L0_attn_wk_up, L0_attn_wv_up,
           L0_attn_wo, L0_ffn_W1, L0_ffn_W2, L0_ffn_Wr):
    C, S, swp, ident, ident32, iota1, rhs2s = _host_consts()
    nc = build_nc(dbg_stage=DBG_STAGE, dbg=DBG)
    x = np.ascontiguousarray(np.asarray(L0_attn_x, dtype=np.float32))
    common = {
        "wq": np.ascontiguousarray(L0_attn_wq, dtype=np.float32),
        "wd": np.ascontiguousarray(L0_attn_w_down, dtype=np.float32),
        "wk": np.ascontiguousarray(L0_attn_wk_up, dtype=np.float32),
        "wv": np.ascontiguousarray(L0_attn_wv_up, dtype=np.float32),
        "wo": np.ascontiguousarray(L0_attn_wo, dtype=np.float32),
        "w1": np.ascontiguousarray(L0_ffn_W1, dtype=np.float32).reshape(NE * DFF, E),
        "w2": np.ascontiguousarray(L0_ffn_W2, dtype=np.float32).reshape(NE * E, DFF),
        "wr": np.ascontiguousarray(L0_ffn_Wr, dtype=np.float32),
        "ropec": C, "ropes": S, "swp": swp, "ident": ident,
        "ident32": ident32, "iota1": iota1, "rhs2s": rhs2s,
    }
    in_maps = [dict(common, x=np.ascontiguousarray(x[b])) for b in range(NCORE)]
    res = run_bass_kernel_spmd(nc, in_maps, core_ids=list(range(NCORE)))
    return np.stack([r["out"] for r in res.results], axis=0)
